# version 03
# baseline (speedup 1.0000x reference)
import math
import os as _os
import contextlib
from collections import defaultdict

import numpy as np
import ml_dtypes
import concourse.bass as bass
import concourse.mybir as mybir
from concourse.bass_utils import run_bass_kernel_spmd

F32 = mybir.dt.float32
BF16 = mybir.dt.bfloat16
I32 = mybir.dt.int32
I16 = mybir.dt.int16
AF = mybir.ActivationFunctionType
OP = mybir.AluOpType
PI = math.pi
TWO_PI = 2.0 * math.pi

T = 2048
D = 1024
FF = 4096
NSL = 16
C = 64
NCH = T // C
EPS = 1e-5
AX = mybir.AxisListType.X


class _Op:
    __slots__ = ("eng", "emit", "reads", "writes", "dma", "idx", "need_inc",
                 "inc_val", "waits", "tag")

    def __init__(self, eng, emit, reads, writes, dma, tag):
        self.eng, self.emit, self.reads, self.writes = eng, emit, reads, writes
        self.dma = dma
        self.idx = None
        self.need_inc = False
        self.inc_val = None
        self.waits = []
        self.tag = tag


_Q_OF = {"qsp": "sp", "qpool": "pool", "qact": "act", "qdve": "dve"}


class Prog:
    def __init__(self, nc):
        self.nc = nc
        self.ops = []
        self.last_w = {}
        self.readers = defaultdict(dict)
        self.pending_fence = None
        self._fenced = set()

    def op(self, eng, emit, reads=(), writes=(), tag=""):
        return self._plan(eng, emit, reads, writes, None, tag)

    def dma(self, queue, emit, reads=(), writes=(), tag=""):
        return self._plan(_Q_OF[queue], emit, reads, writes, queue, tag)

    def barrier_read(self, eng, keys, tag="fin"):
        return self._plan(eng, None, tuple(keys), (), None, tag)

    def fence(self):
        snap = {}
        for o in self.ops:
            snap[o.dma if o.dma else o.eng] = o
        for o in snap.values():
            o.need_inc = True
        self.pending_fence = dict(snap)
        self._fenced = set()

    def _plan(self, eng, emit, reads, writes, dma, tag):
        o = _Op(eng, emit, tuple(reads), tuple(writes), dma, tag)
        sem_name = dma if dma else eng
        deps = {}

        def add_dep(p_):
            if p_ is None:
                return
            psem = p_.dma if p_.dma else p_.eng
            cur = deps.get(psem)
            if cur is None or p_.idx > cur.idx:
                deps[psem] = p_

        if self.pending_fence is not None and sem_name not in self._fenced:
            self._fenced.add(sem_name)
            for p_ in self.pending_fence.values():
                add_dep(p_)
        for r in o.reads:
            add_dep(self.last_w.get(r))
        for w in o.writes:
            add_dep(self.last_w.get(w))
            for p_ in self.readers.get(w, {}).values():
                add_dep(p_)

        o.idx = len(self.ops)
        if dma:
            o.need_inc = True
        self.ops.append(o)
        for p_ in deps.values():
            p_.need_inc = True
            o.waits.append(p_)
        if emit is not None:
            for w in o.writes:
                self.last_w[w] = o
                self.readers[w] = {}
            for r in o.reads:
                self.readers[r][sem_name] = o
        return o

    def emit(self):
        nc = self.nc
        counts = defaultdict(int)
        for o in self.ops:
            if o.need_inc:
                sem = o.dma if o.dma else o.eng
                counts[sem] += 16 if o.dma else 1
                o.inc_val = counts[sem]
        per_eng = defaultdict(list)
        for o in self.ops:
            per_eng[o.eng].append(o)
        stack = contextlib.ExitStack()
        sems = {s: stack.enter_context(nc.semaphore("sem_" + s))
                for s in sorted(counts)}

        def run_engine(eng_obj, eng_name):
            waited = {}
            for o in per_eng[eng_name]:
                for p_ in o.waits:
                    psem = p_.dma if p_.dma else p_.eng
                    v = p_.inc_val
                    if waited.get(psem, -1) < v:
                        eng_obj.wait_ge(sems[psem], v)
                        waited[psem] = v
                if o.emit is None:
                    continue
                inst = o.emit(eng_obj)
                if o.need_inc:
                    sem = o.dma if o.dma else o.eng
                    inst.then_inc(sems[sem], 16 if o.dma else 1)

        with stack:
            with nc.Block() as block:
                for name, attr in [("pe", "tensor"), ("act", "scalar"),
                                   ("dve", "vector"), ("pool", "gpsimd"),
                                   ("sp", "sync")]:
                    if per_eng[name]:
                        getattr(block, attr)(
                            lambda e, n=name: run_engine(e, n))


def build():
    nc = bass.Bass("TRN2", target_bir_lowering=False)
    p = Prog(nc)

    dt = nc.dram_tensor
    x_d = dt("x", [T, D], F32, kind="ExternalInput").ap()
    wq_d = dt("wq", [D, D], F32, kind="ExternalInput").ap()
    wk_d = dt("wk", [D, D], F32, kind="ExternalInput").ap()
    wv_d = dt("wv", [D, D], F32, kind="ExternalInput").ap()
    wg_d = dt("wg", [D, D], F32, kind="ExternalInput").ap()
    wa_d = dt("wa", [D, 2 * D], F32, kind="ExternalInput").ap()
    wo_d = dt("wo", [D, D], F32, kind="ExternalInput").ap()
    w1t_d = dt("w1t", [32, D, 128], F32, kind="ExternalInput").ap()
    w2t_d = dt("w2t", [FF, D], F32, kind="ExternalInput").ap()
    b1c_d = dt("b1c", [128, 32], F32, kind="ExternalInput").ap()
    b2r_d = dt("b2r", [1, D], F32, kind="ExternalInput").ap()
    gr_d = dt("gr", [128, D], F32, kind="ExternalInput").ap()
    tri_m1_d = dt("tri_m1", [128, 128], BF16, kind="ExternalInput").ap()
    tri_p4_d = dt("tri_p4", [128, 128], BF16, kind="ExternalInput").ap()
    tri_p1_d = dt("tri_p1", [128, 128], BF16, kind="ExternalInput").ap()
    ident_d = dt("ident", [128, 128], BF16, kind="ExternalInput").ap()
    cind_d = dt("cind", [128, 2], BF16, kind="ExternalInput").ap()
    ones_d = dt("onesr", [128, 128], BF16, kind="ExternalInput").ap()
    chsel_d = dt("chsel", [32, NSL * 128], BF16, kind="ExternalInput").ap()
    selrep_d = dt("selrep", [128, 128], F32, kind="ExternalInput").ap()
    out_d = dt("out", [T, D], F32, kind="ExternalOutput").ap()
    w1t16_d = dt("w1t16", [32, 128, 1024], BF16, kind="Internal").ap()
    w2t16_d = dt("w2t16", [FF, D], BF16, kind="Internal").ap()
    wo16_d = dt("wo16", [D, D], BF16, kind="Internal").ap()
    ylocD = dt("ylocD", [NSL, 128, D], BF16, kind="Internal").ap()
    a1D = dt("a1D", [NSL, 128, D], BF16, kind="Internal").ap()
    a2D = dt("a2D", [NSL, 128, D], BF16, kind="Internal").ap()
    _KP1 = _os.environ.get("KDBG", "") == "p1"
    if _KP1:
        dbg_d = dt("dbgout", [12, 128, D], F32, kind="ExternalOutput").ap()
    ccin_d = dt("ccin", [1, 2048], F32, kind="Internal").ap()
    ccout_d = dt("ccout", [8, 2048], F32, kind="Internal").ap()

    gstack = contextlib.ExitStack()

    def sb(stk, n, shape, dty):
        return stk.enter_context(nc.sbuf_tensor(n, shape, dty))

    def psum(stk, n, shape, dty):
        return stk.enter_context(nc.psum_tensor(n, shape, dty))

    tri_m1 = sb(gstack, "s_tri_m1", [128, 128], BF16)
    tri_p4 = sb(gstack, "s_tri_p4", [128, 128], BF16)
    tri_p1 = sb(gstack, "s_tri_p1", [128, 128], BF16)
    ident = sb(gstack, "s_ident", [128, 128], BF16)
    cind = sb(gstack, "s_cind", [128, 2], BF16)
    onesr = sb(gstack, "s_onesr", [128, 128], BF16)
    chselA = sb(gstack, "s_chsel", [32, NSL * 128], BF16)
    selrep = sb(gstack, "s_selrep", [128, 128], F32)
    gr = sb(gstack, "s_gr", [128, D], F32)
    b1c = sb(gstack, "s_b1c", [128, 32], F32)
    b2r16 = sb(gstack, "b2r16", [1, D], BF16)
    bn6 = sb(gstack, "bn6", [128, 12], F32)
    bn2 = sb(gstack, "bn2", [128, 2], F32)
    istd_a = sb(gstack, "istd_a", [128, NSL], F32)
    nmu_a = sb(gstack, "nmu_a", [128, NSL], F32)
    scl_a = sb(gstack, "scl_a", [128, NSL], F32)
    ct_a = sb(gstack, "ct_a", [128, NSL], F32)
    invs_a = sb(gstack, "invs_a", [128, NSL], F32)
    cEPS = sb(gstack, "cEPS", [128, 1], F32)
    cPIH = sb(gstack, "cPIH", [128, 1], F32)
    tA = sb(gstack, "tA", [128, 1], F32)
    tB = sb(gstack, "tB", [128, 1], F32)
    tC = sb(gstack, "tC", [128, 1], F32)
    tD_ = sb(gstack, "tD", [128, 1], F32)
    aggS = sb(gstack, "aggS", [128, 80], F32)
    hP_r = sb(gstack, "hP_r", [128, 8 * (NCH + 1)], F32)
    hP_i = sb(gstack, "hP_i", [128, 8 * (NCH + 1)], F32)
    PP_r = sb(gstack, "PP_r", [128, 8 * NCH], F32)
    PP_i = sb(gstack, "PP_i", [128, 8 * NCH], F32)
    magt = sb(gstack, "magt", [128, 16], F32)
    tt0z = sb(gstack, "tt0z", [128, 16], F32)
    tt1 = sb(gstack, "tt1", [128, 16], F32)
    tt2 = sb(gstack, "tt2", [128, 16], F32)
    tt3 = sb(gstack, "tt3", [128, 16], F32)
    tt4 = sb(gstack, "tt4", [128, 16], F32)
    tki = sb(gstack, "tki", [128, 16], I32)
    acr_t = sb(gstack, "acr_t", [128, 16], F32)
    aci_t = sb(gstack, "aci_t", [128, 16], F32)
    slr_t = sb(gstack, "slr_t", [128, 16], F32)
    sli_t = sb(gstack, "sli_t", [128, 16], F32)
    ch1 = sb(gstack, "ch1", [128, 8], F32)
    ch2 = sb(gstack, "ch2", [128, 8], F32)
    Hpack = sb(gstack, "Hpack", [128, 16], F32)
    gatc = sb(gstack, "gatc", [128, 128], F32)
    Hprev = sb(gstack, "Hprev", [128, 16], F32)
    HrR = sb(gstack, "HrR", [128, 8 * NCH], F32)
    HiR = sb(gstack, "HiR", [128, 8 * NCH], F32)
    httR = sb(gstack, "httR", [128, 8 * NCH], BF16)
    httI = sb(gstack, "httI", [128, 8 * NCH], BF16)
    httRsnap = sb(gstack, "httRsnap", [128, 8 * NCH], BF16)
    hrt2R = sb(gstack, "hrt2R", [32, D], BF16)
    hrt2I = sb(gstack, "hrt2I", [32, D], BF16)
    xs = sb(gstack, "xs", [128, D], F32)
    xng_tm = sb(gstack, "xng_tm", [128, D], BF16)
    fA = sb(gstack, "fA", [128, D], F32)
    fB = sb(gstack, "fB", [128, D], F32)
    wsl_f = sb(gstack, "wsl_f", [128, 1024], F32)
    wsl_b = sb(gstack, "wsl_b", [128, 1024], BF16)
    wslq = sb(gstack, "wslq", [128, 1024], F32)
    wslq2 = sb(gstack, "wslq2", [128, 1024], F32)
    onesf = sb(gstack, "onesf", [128, 128], F32)
    acc_w1 = sb(gstack, "acc_w1", [128, 1], F32)
    acc_w2 = sb(gstack, "acc_w2", [128, 1], F32)
    asum = sb(gstack, "asum", [128, 1], F32)
    sw_w1 = sb(gstack, "sw_w1", [128, 1], F32)
    sw_w2 = sb(gstack, "sw_w2", [128, 1], F32)
    mc_w1 = sb(gstack, "mc_w1", [128, 1], F32)
    mc_w2 = sb(gstack, "mc_w2", [128, 1], F32)

    psMain = psum(gstack, "psMain", [128, 1536], F32)
    psAux = psum(gstack, "psAux", [128, 1024], F32)
    psCum = psum(gstack, "psCum", [128, 1024], F32)
    psTrB = psum(gstack, "psTrB", [128, 1024], BF16)

    for nm, dst, src in [("tri_m1", tri_m1, tri_m1_d), ("tri_p4", tri_p4, tri_p4_d),
                         ("tri_p1", tri_p1, tri_p1_d), ("ident", ident, ident_d),
                         ("cind", cind, cind_d), ("onesr", onesr, ones_d),
                         ("chselA", chselA, chsel_d), ("selrep", selrep, selrep_d),
                         ("gr", gr, gr_d), ("b1c", b1c, b1c_d)]:
        p.dma("qsp", lambda e, dst=dst, src=src: e.dma_start(out=dst[:, :], in_=src),
              writes=[nm], tag="cst")
    p.dma("qsp", lambda e: e.dma_start(out=wsl_f[0:1, :], in_=b2r_d),
          writes=["wslf"], tag="cst")
    p.op("dve", lambda e: e.tensor_copy(out=b2r16[:, :], in_=wsl_f[0:1, :]),
         reads=["wslf"], writes=["b2r16", "wslb"])
    p.op("dve", lambda e: e.memset(cEPS[:, :], EPS), writes=["cEPS"])
    p.op("dve", lambda e: e.memset(cPIH[:, :], PI / 2), writes=["cPIH"])
    p.op("dve", lambda e: e.memset(hP_r[:, 0:8], 0.0), writes=["hP:0"])
    p.op("dve", lambda e: e.memset(hP_i[:, 0:8], 0.0), writes=["hPi:0"])
    p.op("dve", lambda e: e.memset(PP_r[:, 0:8], 1.0), writes=["PP:0"])
    p.op("dve", lambda e: e.memset(PP_i[:, 0:8], 0.0), writes=["PPi:0"])
    p.op("dve", lambda e: e.memset(tt0z[:, :], 0.0), writes=["tt0z"])
    p.op("dve", lambda e: e.memset(acc_w1[:, :], 0.0), writes=["acc_w1"])
    p.op("dve", lambda e: e.memset(onesf[:, :], 1.0), writes=["onesf"])
    p.op("dve", lambda e: e.memset(acc_w2[:, :], 0.0), writes=["acc_w2"])

    s0 = contextlib.ExitStack()
    wstage = s0.enter_context(
        nc.sbuf_tensor("wstage", [128, 6144], F32, side="right"))
    s1 = contextlib.ExitStack()
    wqkv16 = sb(s1, "wqkv16", [128, 8 * 6144], BF16)
    for dc in range(8):
        srcs = [(wq_d, 0, D), (wk_d, D, D), (wv_d, 2 * D, D), (wg_d, 3 * D, D),
                (wa_d, 4 * D, 2 * D)]
        rk = [f"wst:{dc}:{off}" for (_, off, _) in srcs]
        for (w, off, width) in srcs:
            p.dma("qsp",
                  lambda e, w=w, off=off, width=width, dc=dc:
                  e.dma_start(out=wstage[:, off:off + width],
                              in_=w[dc * 128:(dc + 1) * 128, :]),
                  reads=[f"wstfree:{j}" for j in range(3)] if dc else [],
                  writes=[f"wst:{dc}:{off}"], tag="wld")
        for j, eng in enumerate(["dve", "act", "pool"]):
            lo, hi = j * 2048, (j + 1) * 2048
            base = dc * 6144
            if eng == "act":
                emit = (lambda e, lo=lo, hi=hi, base=base:
                        e.activation(out=wqkv16[:, base + lo:base + hi],
                                     in_=wstage[:, lo:hi], func=AF.Copy))
            else:
                emit = (lambda e, lo=lo, hi=hi, base=base:
                        e.tensor_copy(out=wqkv16[:, base + lo:base + hi],
                                      in_=wstage[:, lo:hi]))
            p.op(eng, emit, reads=rk, writes=[f"wq16:{dc}", f"wstfree:{j}"],
                 tag="wcast")

    def _slab_src(kind, fc):
        if kind == "w1":
            return (w1t_d[fc].rearrange("(c p) j -> p c j", p=128),
                    wsl_f.ap().rearrange("p (c j) -> p c j", c=8))
        if kind == "w2":
            return (w2t_d[fc * 128:(fc + 1) * 128, :], wsl_f[:, :])
        return (wo_d[fc * 128:(fc + 1) * 128, :], wsl_f[:, :])

    sum_jobs = [("w1", fc) for fc in range(32)] + [("w2", fc) for fc in range(32)]
    sum_i = [0]

    def plan_sums(n):
        for _ in range(n):
            if sum_i[0] >= len(sum_jobs):
                return
            kind, fc = sum_jobs[sum_i[0]]
            sum_i[0] += 1
            srcd, ovw = _slab_src(kind, fc)
            acc = acc_w1 if kind == "w1" else acc_w2
            ak = "acc_" + kind
            p.dma("qsp", lambda e, srcd=srcd, ovw=ovw: e.dma_start(out=ovw, in_=srcd),
                  reads=["wabs", "wslq"], writes=["wslf"], tag="sma")
            p.op("act", lambda e: e.activation(out=wslq[:, :], in_=wsl_f[:, :],
                                               func=AF.Abs, accum_out=asum[:, :]),
                 reads=["wslf", "wslq"], writes=["wabs", "asum", "wslq"], tag="smb")
            p.op("dve", lambda e, acc=acc: e.tensor_add(out=acc[:, :],
                                                        in0=acc[:, :],
                                                        in1=asum[:, :]),
                 reads=[ak, "asum"], writes=[ak, "wslf"], tag="smc")

    def plan_wscale():
        for kind, acc, sw, mc, ps_off in [("w1", acc_w1, sw_w1, mc_w1, 0),
                                          ("w2", acc_w2, sw_w2, mc_w2, 1)]:
            ak, swk, mck = "acc_" + kind, "sw_" + kind, "mc_" + kind
            p.op("pe", lambda e, acc=acc, ps_off=ps_off: e.matmul(
                psAux[:, ps_off:ps_off + 1], onesf[:, :], acc[:, :],
                start=True, stop=True),
                reads=[ak, "onesf"], writes=["psA:0"], tag="wsc")
            p.op("dve", lambda e, mc=mc, ps_off=ps_off: e.tensor_scalar(
                out=mc[:, :], in0=psAux[:, ps_off:ps_off + 1],
                scalar1=1.0 / (4096.0 * 1024.0), scalar2=1e-5,
                op0=OP.mult, op1=OP.max),
                reads=["psA:0"], writes=[mck], tag="wsc")
            p.op("dve", lambda e, sw=sw, mc=mc: e.reciprocal(out=sw[:, :],
                                                             in_=mc[:, :]),
                 reads=[mck], writes=[swk], tag="wsc")

    slab_jobs = ([("w1", fc) for fc in range(32)] +
                 [("w2", fc) for fc in range(32)] +
                 [("wo", fc) for fc in range(8)])
    slab_i = [0]

    def plan_slabs(n):
        for _ in range(n):
            if slab_i[0] >= len(slab_jobs):
                return
            kind, fc = slab_jobs[slab_i[0]]
            slab_i[0] += 1
            srcd, ovw = _slab_src(kind, fc)
            if kind == "w1":
                dst = w1t16_d[fc]
            elif kind == "w2":
                dst = w2t16_d[fc * 128:(fc + 1) * 128, :]
            else:
                dst = wo16_d[fc * 128:(fc + 1) * 128, :]
            p.dma("qsp", lambda e, srcd=srcd, ovw=ovw: e.dma_start(out=ovw, in_=srcd),
                  reads=["wslb", "wslq"], writes=["wslf"], tag="slf")
            if kind == "wo":
                p.op("pool", lambda e: e.tensor_copy(out=wsl_b[:, :],
                                                     in_=wsl_f[:, :]),
                     reads=["wslf"], writes=["wslc", "wslq"], tag="slc")
            else:
                sw = sw_w1 if kind == "w1" else sw_w2
                mc = mc_w1 if kind == "w1" else mc_w2
                swk, mck = "sw_" + kind, "mc_" + kind
                p.op("act", lambda e, sw=sw: e.activation(
                    out=wslq[:, :], in_=wsl_f[:, :], func=AF.Identity,
                    scale=sw[:, :]),
                    reads=["wslf", swk, "wslq2"], writes=["wslq"], tag="slq")
                p.op("dve", lambda e: e.tensor_copy(
                    out=wslq2.ap().bitcast(I32), in_=wslq[:, :]),
                    reads=["wslq"], writes=["wslq2"], tag="slq")
                p.op("dve", lambda e: e.tensor_scalar(
                    out=wslq[:, :], in0=wslq2.ap().bitcast(I32),
                    scalar1=1.0, scalar2=-1.0, op0=OP.min, op1=OP.max),
                    reads=["wslq2"], writes=["wslq", "wslf"], tag="slq")
                p.op("act", lambda e, mc=mc: e.activation(
                    out=wsl_b[:, :], in_=wslq[:, :], func=AF.Identity,
                    scale=mc[:, :]),
                    reads=["wslq", mck], writes=["wslc"], tag="slq")
            p.dma("qpool", lambda e, dst=dst: e.dma_start(out=dst, in_=wsl_b[:, :]),
                  reads=["wslc"], writes=["wslb", f"slab:{kind}:{fc}"], tag="slw")

    s0.close()
    p.fence()
    xngfm = [sb(s1, f"xngfm{i}", [128, 1024], BF16) for i in range(2)]
    arf = sb(s1, "arf", [128, D], F32)
    aif = sb(s1, "aif", [128, D], F32)
    fC = sb(s1, "fC", [128, D], F32)
    fD = sb(s1, "fD", [128, D], F32)
    fE = sb(s1, "fE", [128, D], F32)
    fG = sb(s1, "fG", [128, D], F32)
    resf = sb(s1, "resf", [128, D], F32)
    bb = [sb(s1, f"bb{i}", [128, D], BF16) for i in range(12)]

    def p1_slice(s):
        col = lambda a: a[:, s:s + 1]
        xsl = xngfm[s % 2]
        xk = f"xngfm:{s % 2}"
        p.dma("qsp", lambda e: e.dma_start(out=xs[:, :],
                                           in_=x_d[s * 128:(s + 1) * 128, :]),
              reads=["xs"], writes=["xs"], tag=f"x{s}")
        p.op("dve", lambda e: e.bn_stats(out=bn6[:, 0:6], in_=xs[:, 0:512]),
             reads=["xs"], writes=["bn6"])
        p.op("dve", lambda e: e.bn_stats(out=bn6[:, 6:12], in_=xs[:, 512:1024]),
             reads=["xs"], writes=["bn6"])
        p.op("dve", lambda e: e.bn_aggr(out=bn2[:, :], in_=bn6[:, :]),
             reads=["bn6"], writes=["bn2"])
        p.op("act", lambda e: e.activation(out=tA[:, :], in_=bn2[:, 1:2],
                                           func=AF.Sqrt, bias=cEPS[:, :]),
             reads=["bn2", "cEPS"], writes=["tA"])
        p.op("dve", lambda e: e.reciprocal(out=col(istd_a), in_=tA[:, :]),
             reads=["tA"], writes=[f"istd:{s}"])
        p.op("dve", lambda e: e.tensor_mul(out=tB[:, :], in0=bn2[:, 0:1],
                                           in1=col(istd_a)),
             reads=["bn2", f"istd:{s}"], writes=["tB"])
        p.op("dve", lambda e: e.tensor_scalar(out=col(nmu_a), in0=tB[:, :],
                                              scalar1=-1.0, scalar2=None,
                                              op0=OP.mult),
             reads=["tB"], writes=[f"nmu:{s}"])
        p.op("dve", lambda e: e.tensor_mul(out=tB[:, :], in0=bn2[:, 1:2],
                                           in1=col(istd_a)),
             reads=["bn2", f"istd:{s}"], writes=["tB"])
        p.op("dve", lambda e: e.tensor_mul(out=tB[:, :], in0=tB[:, :],
                                           in1=col(istd_a)),
             reads=["tB", f"istd:{s}"], writes=["tB"])
        p.op("act", lambda e: e.activation(out=tC[:, :], in_=tB[:, :],
                                           func=AF.Sqrt, scale=float(D), bias=cEPS[:, :]),
             reads=["tB", "cEPS"], writes=["tC"])
        p.op("dve", lambda e: e.reciprocal(out=tC[:, :], in_=tC[:, :]),
             reads=["tC"], writes=["tC"])
        p.op("dve", lambda e: e.tensor_scalar(out=col(scl_a), in0=tC[:, :],
                                              scalar1=32.0, scalar2=None,
                                              op0=OP.mult),
             reads=["tC"], writes=[f"scl:{s}"])
        p.op("act", lambda e: e.activation(out=tC[:, :], in_=tB[:, :],
                                           func=AF.Sqrt, bias=cEPS[:, :]),
             reads=["tB", "cEPS"], writes=["tC"])
        p.op("dve", lambda e: e.reciprocal(out=tC[:, :], in_=tC[:, :]),
             reads=["tC"], writes=["tC"])
        p.op("act", lambda e: e.activation(out=xs[:, :], in_=xs[:, :],
                                           func=AF.Identity, scale=col(istd_a),
                                           bias=col(nmu_a)),
             reads=["xs", f"istd:{s}", f"nmu:{s}"], writes=["xs"])
        p.op("dve", lambda e: e.tensor_reduce(out=tD_[:, :], in_=xs[:, :],
                                              axis=AX, op=OP.max,
                                              apply_absolute_value=True),
             reads=["xs"], writes=["tD"])
        p.op("dve", lambda e: e.tensor_mul(out=tB[:, :], in0=tD_[:, :], in1=tC[:, :]),
             reads=["tD", "tC"], writes=["tB"])
        p.op("dve", lambda e: e.tensor_scalar(out=tB[:, :], in0=tB[:, :],
                                              scalar1=1e-5, scalar2=None, op0=OP.max),
             reads=["tB"], writes=["tB"])
        p.op("dve", lambda e: e.tensor_scalar(out=col(invs_a), in0=tB[:, :],
                                              scalar1=1.0 / 127.0, scalar2=None,
                                              op0=OP.mult),
             reads=["tB"], writes=[f"invs:{s}"])
        p.op("dve", lambda e: e.reciprocal(out=tB[:, :], in_=tB[:, :]),
             reads=["tB"], writes=["tB"])
        p.op("dve", lambda e: e.tensor_mul(out=tB[:, :], in0=tB[:, :], in1=tC[:, :]),
             reads=["tB", "tC"], writes=["tB"])
        p.op("dve", lambda e: e.tensor_scalar(out=col(ct_a), in0=tB[:, :],
                                              scalar1=127.0, scalar2=None,
                                              op0=OP.mult),
             reads=["tB"], writes=[f"ct:{s}"])
        p.op("dve", lambda e: e.tensor_mul(out=xng_tm[:, :], in0=xs[:, :],
                                           in1=gr[:, :]),
             reads=["xs", "gr"], writes=["xng_tm"])
        for dc in range(8):
            p.op("pe", lambda e, dc=dc: e.transpose(
                psTrB[:, dc * 128:(dc + 1) * 128],
                xng_tm[:, dc * 128:(dc + 1) * 128], ident[:, :]),
                reads=["xng_tm", "ident"], writes=["psTrB"], tag=f"tr{s}")
        p.op("act", lambda e: e.activation(out=xsl[:, :], in_=psTrB[:, :],
                                           func=AF.Copy),
             reads=["psTrB"], writes=[xk], tag=f"tre{s}")

        def mm_pass(groups, tagp):
            for dc in range(8):
                for (kk_, pt, po, wo_) in groups:
                    p.op("pe", lambda e, dc=dc, pt=pt, po=po, wo_=wo_:
                         e.matmul(pt[:, po:po + 512],
                                  xsl[:, dc * 128:(dc + 1) * 128],
                                  wqkv16[:, dc * 6144 + wo_: dc * 6144 + wo_ + 512],
                                  start=(dc == 0), stop=(dc == 7)),
                         reads=[xk, f"wq16:{dc}"], writes=[kk_], tag=tagp)

        mm_pass([("psM:0", psMain, 0, 0), ("psM:1", psMain, 512, 512),
                 ("psM:2", psMain, 1024, 1024), ("psA:0", psAux, 0, 1536),
                 ("psA:1", psAux, 512, 2048)], f"p1m{s}")
        p.op("act", lambda e: e.activation(out=bb[0][:, :], in_=psMain[:, 0:1024],
                                           func=AF.Identity, scale=col(scl_a)),
             reads=["psM:0", "psM:1", f"scl:{s}"], writes=["b0"])
        p.op("act", lambda e: e.activation(out=bb[1][:, 0:512],
                                           in_=psMain[:, 1024:1536],
                                           func=AF.Identity, scale=col(scl_a)),
             reads=["psM:2", f"scl:{s}"], writes=["b1"])
        p.op("act", lambda e: e.activation(out=bb[1][:, 512:1024],
                                           in_=psAux[:, 0:512],
                                           func=AF.Identity, scale=col(scl_a)),
             reads=["psA:0", f"scl:{s}"], writes=["b1"])
        p.op("act", lambda e: e.activation(out=bb[2][:, 0:512],
                                           in_=psAux[:, 512:1024],
                                           func=AF.Identity, scale=col(scl_a)),
             reads=["psA:1", f"scl:{s}"], writes=["b2"])
        mm_pass([("psM:0", psMain, 0, 2560), ("psM:1", psMain, 512, 3072),
                 ("psM:2", psMain, 1024, 3584), ("psA:0", psAux, 0, 4096),
                 ("psA:1", psAux, 512, 4608)], f"p2m{s}")
        p.op("act", lambda e: e.activation(out=bb[2][:, 512:1024],
                                           in_=psMain[:, 0:512],
                                           func=AF.Identity, scale=col(scl_a)),
             reads=["psM:0", f"scl:{s}"], writes=["b2"])
        p.op("act", lambda e: e.activation(out=bb[3][:, :],
                                           in_=psMain[:, 512:1536],
                                           func=AF.Silu, scale=col(scl_a)),
             reads=["psM:1", "psM:2", f"scl:{s}"], writes=["b3"])
        p.op("act", lambda e: e.activation(out=arf[:, :], in_=psAux[:, 0:1024],
                                           func=AF.Identity, scale=col(scl_a)),
             reads=["psA:0", "psA:1", f"scl:{s}"], writes=["arf"])
        mm_pass([("psM:0", psMain, 0, 5120), ("psM:1", psMain, 512, 5632)],
                f"p3m{s}")
        p.op("act", lambda e: e.activation(out=aif[:, :], in_=psMain[:, 0:1024],
                                           func=AF.Identity, scale=col(scl_a)),
             reads=["psM:0", "psM:1", f"scl:{s}"], writes=["aif"])
        p.op("pool", lambda e: e.tensor_tensor(out=bb[11][:, :], in0=bb[1][:, :],
                                               in1=bb[2][:, :], op=OP.mult),
             reads=["b1", "b2"], writes=["b11"])
        p.op("act", lambda e: e.activation(out=fA[:, :], in_=arf[:, :],
                                           func=AF.Square),
             reads=["arf"], writes=["fA"])
        p.op("act", lambda e: e.activation(out=fB[:, :], in_=aif[:, :],
                                           func=AF.Square),
             reads=["aif"], writes=["fB"])
        p.op("dve", lambda e: e.tensor_add(out=fC[:, :], in0=fA[:, :], in1=fB[:, :]),
             reads=["fA", "fB"], writes=["fC"])
        p.op("act", lambda e: e.activation(out=fC[:, :], in_=fC[:, :], func=AF.Sqrt),
             reads=["fC"], writes=["fC"])
        p.op("dve", lambda e: e.tensor_add(out=fD[:, :], in0=fC[:, :],
                                           in1=arf[:, :]),
             reads=["fC", "arf"], writes=["fD"])
        p.op("act", lambda e: e.activation(out=fA[:, :], in_=fD[:, :],
                                           func=AF.Square),
             reads=["fD"], writes=["fA"])
        p.op("pool", lambda e: e.tensor_add(out=fA[:, :], in0=fA[:, :],
                                            in1=fB[:, :]),
             reads=["fA", "fB"], writes=["fA"])
        p.op("act", lambda e: e.activation(out=fA[:, :], in_=fA[:, :], func=AF.Sqrt),
             reads=["fA"], writes=["fA"])
        p.op("dve", lambda e: e.scalar_tensor_tensor(out=fD[:, :], in0=fA[:, :],
                                                     scalar=1e-12, in1=fD[:, :],
                                                     op0=OP.add, op1=OP.add),
             reads=["fA", "fD"], writes=["fD"])
        p.op("act", lambda e: e.activation(out=fE[:, :], in_=fC[:, :], func=AF.Exp,
                                           scale=-1.0),
             reads=["fC"], writes=["fE"])
        p.op("act", lambda e: e.activation(out=bb[4][:, :], in_=fE[:, :],
                                           func=AF.Ln, bias=1.0),
             reads=["fE"], writes=["b4"])
        p.op("act", lambda e: e.activation(out=fE[:, :], in_=fD[:, :], func=AF.Ln),
             reads=["fD"], writes=["fE"])
        p.op("act", lambda e: e.activation(out=fE[:, :], in_=fE[:, :], func=AF.Exp,
                                           scale=-1.0),
             reads=["fE"], writes=["fE"])
        p.op("dve", lambda e: e.tensor_mul(out=fG[:, :], in0=aif[:, :],
                                           in1=fE[:, :]),
             reads=["aif", "fE"], writes=["fG"])
        p.op("act", lambda e: e.activation(out=fG[:, :], in_=fG[:, :],
                                           func=AF.Arctan),
             reads=["fG"], writes=["fG"])
        p.op("dve", lambda e: e.tensor_copy(out=bb[5][:, :], in_=fG[:, :]),
             reads=["fG"], writes=["b5"])
        p.op("dve", lambda e: e.scalar_tensor_tensor(out=bb[6][:, :],
                                                     in0=bb[5][:, :], scalar=-1.0,
                                                     in1=fG[:, :], op0=OP.mult,
                                                     op1=OP.add),
             reads=["b5", "fG"], writes=["b6"])
        for h in range(2):
            p.op("pe", lambda e, h=h: e.matmul(psCum[:, h * 512:(h + 1) * 512],
                                               tri_m1[:, :],
                                               bb[4][:, h * 512:(h + 1) * 512],
                                               start=True, stop=True),
                 reads=["tri_m1", "b4"], writes=[f"psC:{h}"], tag=f"cs1{s}")
        p.op("act", lambda e: e.activation(out=bb[7][:, :], in_=psCum[:, :],
                                           func=AF.Exp),
             reads=["psC:0", "psC:1"], writes=["b7"])
        p.op("act", lambda e: e.activation(out=bb[8][:, :], in_=psCum[:, :],
                                           func=AF.Exp, scale=-1.0),
             reads=["psC:0", "psC:1"], writes=["b8"])
        for cc in range(8):
            p.op("pe", lambda e, cc=cc: e.matmul(
                psMain[:, cc * 2:cc * 2 + 2], bb[4][:, cc * 128:(cc + 1) * 128],
                cind[:, :], start=True, stop=True),
                reads=["b4", "cind"], writes=["psM:0"], tag=f"ag{s}")
            p.op("pe", lambda e, cc=cc: e.matmul(
                psMain[:, 16 + cc * 2:16 + cc * 2 + 2],
                bb[5][:, cc * 128:(cc + 1) * 128],
                cind[:, :], start=True, stop=False),
                reads=["b5", "cind"], writes=["psM:0"], tag=f"ag{s}")
            p.op("pe", lambda e, cc=cc: e.matmul(
                psMain[:, 16 + cc * 2:16 + cc * 2 + 2],
                bb[6][:, cc * 128:(cc + 1) * 128],
                cind[:, :], start=False, stop=True),
                reads=["b6", "cind"], writes=["psM:0"], tag=f"ag{s}")
        for h in range(2):
            p.op("pe", lambda e, h=h: e.matmul(psCum[:, h * 512:(h + 1) * 512],
                                               tri_p4[:, :],
                                               bb[5][:, h * 512:(h + 1) * 512],
                                               start=True, stop=False),
                 reads=["tri_p4", "b5", "b7", "b8"], writes=[f"psC:{h}"],
                 tag=f"cs2{s}")
            p.op("pe", lambda e, h=h: e.matmul(psCum[:, h * 512:(h + 1) * 512],
                                               tri_p4[:, :],
                                               bb[6][:, h * 512:(h + 1) * 512],
                                               start=False, stop=True),
                 reads=["tri_p4", "b6"], writes=[f"psC:{h}"], tag=f"cs2{s}")
        p.op("dve", lambda e: e.tensor_scalar(out=arf.ap().bitcast(I32),
                                              in0=psCum[:, :],
                                              scalar1=1.0 / TWO_PI, scalar2=None,
                                              op0=OP.mult),
             reads=["psC:0", "psC:1", "arf"], writes=["arf"])
        p.op("dve", lambda e: e.scalar_tensor_tensor(out=resf[:, :],
                                                     in0=arf.ap().bitcast(I32),
                                                     scalar=-TWO_PI,
                                                     in1=psCum[:, :],
                                                     op0=OP.mult, op1=OP.add),
             reads=["arf", "psC:0", "psC:1"], writes=["resf"])
        p.op("dve", lambda e: e.tensor_scalar(out=resf[:, :], in0=resf[:, :],
                                              scalar1=6.2821, scalar2=-6.2821,
                                              op0=OP.min, op1=OP.max),
             reads=["resf"], writes=["resf"])
        p.op("act", lambda e: e.activation(out=fA[:, :], in_=resf[:, :],
                                           func=AF.Sin, scale=0.25),
             reads=["resf"], writes=["fA"])
        p.op("act", lambda e: e.activation(out=fB[:, :], in_=resf[:, :],
                                           func=AF.Sin, scale=0.25, bias=cPIH[:, :]),
             reads=["resf", "cPIH"], writes=["fB"])
        p.op("act", lambda e: e.activation(out=fC[:, :], in_=fA[:, :],
                                           func=AF.Square),
             reads=["fA"], writes=["fC"])
        p.op("dve", lambda e: e.tensor_mul(out=fD[:, :], in0=fA[:, :],
                                           in1=fB[:, :]),
             reads=["fA", "fB"], writes=["fD"])
        p.op("dve", lambda e: e.tensor_scalar(out=fE[:, :], in0=fC[:, :],
                                              scalar1=-2.0, scalar2=1.0,
                                              op0=OP.mult, op1=OP.add),
             reads=["fC"], writes=["fE"])
        p.op("dve", lambda e: e.tensor_mul(out=fG[:, :], in0=fD[:, :],
                                           in1=fE[:, :]),
             reads=["fD", "fE"], writes=["fG"])
        p.op("dve", lambda e: e.tensor_scalar(out=bb[5][:, :], in0=fG[:, :],
                                              scalar1=4.0, scalar2=None,
                                              op0=OP.mult),
             reads=["fG"], writes=["b5"])
        p.op("act", lambda e: e.activation(out=fC[:, :], in_=fD[:, :],
                                           func=AF.Square),
             reads=["fD"], writes=["fC"])
        p.op("dve", lambda e: e.tensor_scalar(out=bb[6][:, :], in0=fC[:, :],
                                              scalar1=-8.0, scalar2=1.0,
                                              op0=OP.mult, op1=OP.add),
             reads=["fC"], writes=["b6"])
        p.op("dve", lambda e: e.tensor_mul(out=fD[:, :], in0=bb[11][:, :],
                                           in1=bb[8][:, :]),
             reads=["b11", "b8"], writes=["fD"])
        p.op("dve", lambda e: e.tensor_mul(out=bb[4][:, :], in0=fD[:, :],
                                           in1=bb[6][:, :]),
             reads=["fD", "b6"], writes=["b4"])
        p.op("dve", lambda e: e.tensor_mul(out=bb[9][:, :], in0=fD[:, :],
                                           in1=bb[5][:, :]),
             reads=["fD", "b5"], writes=["b9"])
        p.op("dve", lambda e: e.tensor_mul(out=bb[1][:, :], in0=bb[7][:, :],
                                           in1=bb[6][:, :]),
             reads=["b7", "b6"], writes=["b1"])
        p.op("dve", lambda e: e.tensor_mul(out=bb[2][:, :], in0=bb[7][:, :],
                                           in1=bb[5][:, :]),
             reads=["b7", "b5"], writes=["b2"])
        for cc in range(8):
            p.op("pe", lambda e, cc=cc: e.matmul(
                psMain[:, 32 + cc * 2:32 + cc * 2 + 2],
                bb[4][:, cc * 128:(cc + 1) * 128], cind[:, :],
                start=True, stop=True),
                reads=["b4", "cind"], writes=["psM:0"], tag=f"ag2{s}")
            p.op("pe", lambda e, cc=cc: e.matmul(
                psMain[:, 48 + cc * 2:48 + cc * 2 + 2],
                bb[9][:, cc * 128:(cc + 1) * 128], cind[:, :],
                start=True, stop=True),
                reads=["b9", "cind"], writes=["psM:0"], tag=f"ag2{s}")
        p.op("act", lambda e: e.activation(
            out=aggS[:, 0:64].rearrange("p (g k cc) -> p g k cc", g=4, k=2),
            in_=psMain[:, 0:64].rearrange("p (g cc k) -> p g k cc", g=4, cc=8),
            func=AF.Copy),
            reads=["psM:0"], writes=["aggS"])
        for h in range(2):
            p.op("pe", lambda e, h=h: e.matmul(psCum[:, h * 512:(h + 1) * 512],
                                               tri_p1[:, :],
                                               bb[4][:, h * 512:(h + 1) * 512],
                                               start=True, stop=True),
                 reads=["tri_p1", "b4", "resf"], writes=[f"psC:{h}"],
                 tag=f"cs3{s}")
            p.op("pe", lambda e, h=h: e.matmul(psAux[:, h * 512:(h + 1) * 512],
                                               tri_m1[:, :],
                                               bb[9][:, h * 512:(h + 1) * 512],
                                               start=True, stop=True),
                 reads=["tri_m1", "b9"], writes=[f"psA:{h}"], tag=f"cs3{s}")
        p.op("dve", lambda e: e.tensor_mul(out=fA[:, :], in0=bb[1][:, :],
                                           in1=psCum[:, :]),
             reads=["b1", "psC:0", "psC:1"], writes=["fA"])
        p.op("dve", lambda e: e.tensor_mul(out=fB[:, :], in0=bb[2][:, :],
                                           in1=psAux[:, :]),
             reads=["b2", "psA:0", "psA:1"], writes=["fB"])
        p.op("dve", lambda e: e.tensor_sub(out=fA[:, :], in0=fA[:, :],
                                           in1=fB[:, :]),
             reads=["fA", "fB"], writes=["fA"])
        p.op("dve", lambda e: e.tensor_mul(out=bb[10][:, :], in0=bb[0][:, :],
                                           in1=bb[3][:, :]),
             reads=["b0", "b3"], writes=["b10"])
        p.op("dve", lambda e: e.tensor_mul(out=bb[0][:, :], in0=bb[10][:, :],
                                           in1=fA[:, :]),
             reads=["b10", "fA"], writes=["b0"])
        p.op("dve", lambda e: e.tensor_mul(out=bb[3][:, :], in0=bb[10][:, :],
                                           in1=bb[1][:, :]),
             reads=["b10", "b1"], writes=["b3"])
        p.op("dve", lambda e: e.tensor_mul(out=bb[7][:, :], in0=bb[10][:, :],
                                           in1=bb[2][:, :]),
             reads=["b10", "b2"], writes=["b7"])
        if _KP1 and s == 2:
            dbg_items = [("b4srin", bb[4], "b4"), ("b9siin", bb[9], "b9"),
                         ("b7eL", bb[7], "b7"), ("b8eLi", bb[8], "b8"),
                         ("b5sin", bb[5], "b5"), ("b6cos", bb[6], "b6"),
                         ("b1cr", bb[1], "b1"), ("b2ci", bb[2], "b2"),
                         ("fAyre", fA, "fA"), ("resf", resf, "resf"),
                         ("b11u", bb[11], "b11"), ("b0yl", bb[0], "b0")]
            for di, (nm, tn, kk_) in enumerate(dbg_items):
                p.dma("qpool", lambda e, tn=tn, di=di: e.dma_start(
                    out=dbg_d[di], in_=tn[:, :]),
                    reads=[kk_], writes=[f"dbg:{di}"], tag="p1d")
        p.dma("qpool", lambda e: e.dma_start(out=ylocD[s], in_=bb[0][:, :]),
              reads=["b0"], writes=[f"ylD:{s}"], tag=f"sp{s}")
        p.dma("qpool", lambda e: e.dma_start(out=a1D[s], in_=bb[3][:, :]),
              reads=["b3"], writes=[f"a1D:{s}"], tag=f"sp{s}")
        p.dma("qpool", lambda e: e.dma_start(out=a2D[s], in_=bb[7][:, :]),
              reads=["b7"], writes=[f"a2D:{s}"], tag=f"sp{s}")

        p.op("act", lambda e: e.activation(out=magt[:, :], in_=aggS[:, 0:16],
                                           func=AF.Exp, scale=-1.0),
             reads=["aggS"], writes=["magt"])
        p.op("dve", lambda e: e.tensor_scalar(out=tki[:, :], in0=aggS[:, 16:32],
                                              scalar1=4.0 / TWO_PI, scalar2=None,
                                              op0=OP.mult),
             reads=["aggS"], writes=["tki"])
        p.op("dve", lambda e: e.tensor_scalar(out=tt1[:, :], in0=aggS[:, 16:32],
                                              scalar1=4.0, scalar2=None,
                                              op0=OP.mult),
             reads=["aggS"], writes=["tt1"])
        p.op("dve", lambda e: e.scalar_tensor_tensor(out=tt1[:, :], in0=tki[:, :],
                                                     scalar=-TWO_PI, in1=tt1[:, :],
                                                     op0=OP.mult, op1=OP.add),
             reads=["tki", "tt1"], writes=["tt1"])
        p.op("dve", lambda e: e.tensor_scalar(out=tt1[:, :], in0=tt1[:, :],
                                              scalar1=6.2821, scalar2=-6.2821,
                                              op0=OP.min, op1=OP.max),
             reads=["tt1"], writes=["tt1"])
        p.op("act", lambda e: e.activation(out=tt2[:, :], in_=tt1[:, :],
                                           func=AF.Sin, scale=0.25),
             reads=["tt1"], writes=["tt2"])
        p.op("act", lambda e: e.activation(out=tt3[:, :], in_=tt1[:, :],
                                           func=AF.Sin, scale=0.25, bias=cPIH[:, :]),
             reads=["tt1", "cPIH"], writes=["tt3"])
        p.op("act", lambda e: e.activation(out=tt4[:, :], in_=tt2[:, :],
                                           func=AF.Square),
             reads=["tt2"], writes=["tt4"])
        p.op("dve", lambda e: e.tensor_mul(out=tt2[:, :], in0=tt2[:, :],
                                           in1=tt3[:, :]),
             reads=["tt2", "tt3"], writes=["tt2"])
        p.op("dve", lambda e: e.tensor_scalar(out=tt3[:, :], in0=tt4[:, :],
                                              scalar1=-2.0, scalar2=1.0,
                                              op0=OP.mult, op1=OP.add),
             reads=["tt4"], writes=["tt3"])
        p.op("dve", lambda e: e.tensor_mul(out=tt1[:, :], in0=tt2[:, :],
                                           in1=tt3[:, :]),
             reads=["tt2", "tt3"], writes=["tt1"])
        p.op("dve", lambda e: e.tensor_scalar(out=tt1[:, :], in0=tt1[:, :],
                                              scalar1=4.0, scalar2=None,
                                              op0=OP.mult),
             reads=["tt1"], writes=["tt1"])
        p.op("act", lambda e: e.activation(out=tt4[:, :], in_=tt2[:, :],
                                           func=AF.Square),
             reads=["tt2"], writes=["tt4"])
        p.op("dve", lambda e: e.tensor_scalar(out=tt4[:, :], in0=tt4[:, :],
                                              scalar1=-8.0, scalar2=1.0,
                                              op0=OP.mult, op1=OP.add),
             reads=["tt4"], writes=["tt4"])
        p.op("dve", lambda e: e.tensor_mul(out=acr_t[:, :], in0=magt[:, :],
                                           in1=tt4[:, :]),
             reads=["magt", "tt4"], writes=["acr_t"])
        p.op("dve", lambda e: e.tensor_mul(out=aci_t[:, :], in0=magt[:, :],
                                           in1=tt1[:, :]),
             reads=["magt", "tt1"], writes=["aci_t"])
        p.op("dve", lambda e: e.tensor_mul(out=slr_t[:, :], in0=acr_t[:, :],
                                           in1=aggS[:, 32:48]),
             reads=["acr_t", "aggS"], writes=["slr_t"])
        p.op("dve", lambda e: e.tensor_mul(out=tt1[:, :], in0=aci_t[:, :],
                                           in1=aggS[:, 48:64]),
             reads=["aci_t", "aggS"], writes=["tt1"])
        p.op("dve", lambda e: e.tensor_add(out=slr_t[:, :], in0=slr_t[:, :],
                                           in1=tt1[:, :]),
             reads=["slr_t", "tt1"], writes=["slr_t"])
        p.op("dve", lambda e: e.tensor_mul(out=sli_t[:, :], in0=aci_t[:, :],
                                           in1=aggS[:, 32:48]),
             reads=["aci_t", "aggS"], writes=["sli_t"])
        p.op("dve", lambda e: e.tensor_mul(out=tt1[:, :], in0=acr_t[:, :],
                                           in1=aggS[:, 48:64]),
             reads=["acr_t", "aggS"], writes=["tt1"])
        p.op("dve", lambda e: e.tensor_sub(out=sli_t[:, :], in0=sli_t[:, :],
                                           in1=tt1[:, :]),
             reads=["sli_t", "tt1"], writes=["sli_t"])
        for k in range(2):
            c = 2 * s + k
            lo, hi = k * 8, (k + 1) * 8
            co, co1 = c * 8, (c + 1) * 8

            def cop(fn, out_ap, a_ap, b_ap, rd, wr):
                p.op("dve",
                     lambda e, fn=fn, out_ap=out_ap, a_ap=a_ap, b_ap=b_ap:
                     getattr(e, fn)(out=out_ap, in0=a_ap, in1=b_ap),
                     reads=rd, writes=wr)

            cop("tensor_mul", ch1[:, :], acr_t[:, lo:hi], hP_r[:, co:co + 8],
                ["acr_t", f"hP:{c}"], ["ch1"])
            cop("tensor_mul", ch2[:, :], aci_t[:, lo:hi], hP_i[:, co:co + 8],
                ["aci_t", f"hPi:{c}"], ["ch2"])
            cop("tensor_sub", ch1[:, :], ch1[:, :], ch2[:, :],
                ["ch1", "ch2"], ["ch1"])
            cop("tensor_add", hP_r[:, co1:co1 + 8], ch1[:, :], slr_t[:, lo:hi],
                ["ch1", "slr_t"], [f"hP:{c + 1}"])
            cop("tensor_mul", ch1[:, :], acr_t[:, lo:hi], hP_i[:, co:co + 8],
                ["acr_t", f"hPi:{c}"], ["ch1"])
            cop("tensor_mul", ch2[:, :], aci_t[:, lo:hi], hP_r[:, co:co + 8],
                ["aci_t", f"hP:{c}"], ["ch2"])
            cop("tensor_add", ch1[:, :], ch1[:, :], ch2[:, :],
                ["ch1", "ch2"], ["ch1"])
            cop("tensor_add", hP_i[:, co1:co1 + 8], ch1[:, :], sli_t[:, lo:hi],
                ["ch1", "sli_t"], [f"hPi:{c + 1}"])
            if c + 1 < NCH:
                cop("tensor_mul", ch1[:, :], acr_t[:, lo:hi], PP_r[:, co:co + 8],
                    ["acr_t", f"PP:{c}"], ["ch1"])
                cop("tensor_mul", ch2[:, :], aci_t[:, lo:hi], PP_i[:, co:co + 8],
                    ["aci_t", f"PPi:{c}"], ["ch2"])
                cop("tensor_sub", PP_r[:, co1:co1 + 8], ch1[:, :], ch2[:, :],
                    ["ch1", "ch2"], [f"PP:{c + 1}"])
                cop("tensor_mul", ch1[:, :], acr_t[:, lo:hi], PP_i[:, co:co + 8],
                    ["acr_t", f"PPi:{c}"], ["ch1"])
                cop("tensor_mul", ch2[:, :], aci_t[:, lo:hi], PP_r[:, co:co + 8],
                    ["aci_t", f"PP:{c}"], ["ch2"])
                cop("tensor_add", PP_i[:, co1:co1 + 8], ch1[:, :], ch2[:, :],
                    ["ch1", "ch2"], [f"PPi:{c + 1}"])
        if s < 8:
            plan_sums(8)
            if s == 7:
                plan_sums(100)
                plan_wscale()
        else:
            plan_slabs(9)

    for s in range(NSL):
        p1_slice(s)
    plan_slabs(100)

    p.op("dve", lambda e: e.tensor_copy(out=Hpack[:, 0:8],
                                        in_=hP_r[:, NCH * 8:(NCH + 1) * 8]),
         reads=[f"hP:{NCH}"], writes=["Hpack"])
    p.op("dve", lambda e: e.tensor_copy(out=Hpack[:, 8:16],
                                        in_=hP_i[:, NCH * 8:(NCH + 1) * 8]),
         reads=[f"hPi:{NCH}"], writes=["Hpack"])
    p.dma("qpool", lambda e: e.dma_start(out=ccin_d, in_=Hpack[:, :]),
          reads=["Hpack"], writes=["ccin"], tag="cc")
    p.op("pool", lambda e: e.collective_compute(
        "AllGather", OP.bypass, ins=[ccin_d], outs=[ccout_d],
        replica_groups=[[0, 1, 2, 3, 4, 5, 6, 7]]),
        reads=["ccin"], writes=["ccout"], tag="cc")
    p.dma("qpool", lambda e: e.dma_start(
        out=gatc.ap().rearrange("p (r j) -> p r j", r=8),
        in_=ccout_d.rearrange("r (p j) -> p r j", p=128)),
        reads=["ccout"], writes=["gatc"], tag="cc")
    p.op("dve", lambda e: e.tensor_mul(out=gatc[:, :], in0=gatc[:, :],
                                       in1=selrep[:, :]),
         reads=["gatc", "selrep"], writes=["gatc"])
    p.op("dve", lambda e: e.tensor_copy(out=Hprev[:, :], in_=gatc[:, 0:16]),
         reads=["gatc"], writes=["Hprev"])
    for r_ in range(1, 8):
        p.op("dve", lambda e, r_=r_: e.tensor_add(
            out=Hprev[:, :], in0=Hprev[:, :], in1=gatc[:, r_ * 16:(r_ + 1) * 16]),
            reads=["gatc", "Hprev"], writes=["Hprev"])
    for c in range(NCH):
        p.op("dve", lambda e, c=c: e.tensor_copy(out=HrR[:, c * 8:(c + 1) * 8],
                                                 in_=Hprev[:, 0:8]),
             reads=["Hprev"], writes=["HrR"])
        p.op("dve", lambda e, c=c: e.tensor_copy(out=HiR[:, c * 8:(c + 1) * 8],
                                                 in_=Hprev[:, 8:16]),
             reads=["Hprev"], writes=["HiR"])
    hp_reads = ([f"hP:{c}" for c in range(NCH)] +
                [f"hPi:{c}" for c in range(NCH)] +
                [f"PP:{c}" for c in range(NCH)] +
                [f"PPi:{c}" for c in range(NCH)])
    httR_o = httR.ap().rearrange("p (cc c) -> p cc c", cc=8).transpose([0, 2, 1])
    httI_o = httI.ap().rearrange("p (cc c) -> p cc c", cc=8).transpose([0, 2, 1])
    fA3 = fA.ap().rearrange("p (c cc) -> p c cc", cc=8)
    hPr3 = hP_r.ap().rearrange("p (c cc) -> p c cc", cc=8)
    hPi3 = hP_i.ap().rearrange("p (c cc) -> p c cc", cc=8)
    p.op("dve", lambda e: e.tensor_mul(out=fA[:, 0:256], in0=PP_r[:, 0:256],
                                       in1=HrR[:, :]),
         reads=hp_reads + ["HrR"], writes=["fA"])
    p.op("dve", lambda e: e.tensor_mul(out=fB[:, 0:256], in0=PP_i[:, 0:256],
                                       in1=HiR[:, :]),
         reads=["HiR"] + hp_reads, writes=["fB"])
    p.op("dve", lambda e: e.tensor_sub(out=fA[:, 0:256], in0=fA[:, 0:256],
                                       in1=fB[:, 0:256]),
         reads=["fA", "fB"], writes=["fA"])
    p.op("dve", lambda e: e.tensor_tensor(out=httR_o, in0=fA3[:, 0:32, :],
                                          in1=hPr3[:, 0:32, :], op=OP.add),
         reads=["fA"] + hp_reads, writes=["httR"])
    p.op("dve", lambda e: e.tensor_mul(out=fA[:, 0:256], in0=PP_r[:, 0:256],
                                       in1=HiR[:, :]),
         reads=hp_reads + ["HiR", "httR"], writes=["fA"])
    p.op("dve", lambda e: e.tensor_mul(out=fB[:, 0:256], in0=PP_i[:, 0:256],
                                       in1=HrR[:, :]),
         reads=["HrR"] + hp_reads, writes=["fB"])
    p.op("dve", lambda e: e.tensor_add(out=fA[:, 0:256], in0=fA[:, 0:256],
                                       in1=fB[:, 0:256]),
         reads=["fA", "fB"], writes=["fA"])
    p.op("dve", lambda e: e.tensor_tensor(out=httI_o, in0=fA3[:, 0:32, :],
                                          in1=hPi3[:, 0:32, :], op=OP.add),
         reads=["fA"] + hp_reads, writes=["httI"])
    p.op("dve", lambda e: e.tensor_copy(out=httRsnap[:, :], in_=httR[:, :]),
         reads=["httR"], writes=["httRsnap"], tag="dmpF")
    for cc in range(8):
        p.op("pe", lambda e, cc=cc: e.transpose(
            psTrB[0:32, 0:128], httR[:, cc * 32:(cc + 1) * 32], ident[:, :]),
            reads=["httR", "ident", "psTrB"], writes=["psTrB"], tag="htr")
        p.op("act", lambda e, cc=cc: e.activation(
            out=hrt2R[0:32, cc * 128:(cc + 1) * 128], in_=psTrB[0:32, 0:128],
            func=AF.Copy),
            reads=["psTrB"], writes=["hrt2R"], tag="htr")
        p.op("pe", lambda e, cc=cc: e.transpose(
            psTrB[0:32, 0:128], httI[:, cc * 32:(cc + 1) * 32], ident[:, :]),
            reads=["httI", "ident", "psTrB"], writes=["psTrB"], tag="htr")
        p.op("act", lambda e, cc=cc: e.activation(
            out=hrt2I[0:32, cc * 128:(cc + 1) * 128], in_=psTrB[0:32, 0:128],
            func=AF.Copy),
            reads=["psTrB"], writes=["hrt2I"], tag="htr")

    s1.close()
    p.fence()
    s2 = contextlib.ExitStack()
    hrt2snap = sb(s2, "hrt2snap", [32, D], BF16)
    can0 = sb(s2, "can0", [128, 64], F32)
    hsnapA = sb(s2, "hsnapA", [128, 256], BF16)
    hsnapB = sb(s2, "hsnapB", [128, 256], BF16)
    hsnapC = sb(s2, "hsnapC", [128, 256], BF16)
    _KD = _os.environ.get("KDBG", "") == "chain"
    def snap_httR(buf, key):
        if _KD:
            p.op("dve", lambda e, buf=buf: e.tensor_copy(out=buf[:, :], in_=httR[:, :]),
                 reads=["httR"], writes=[key], tag="snp")
    if _os.environ.get("KDBG", "") == "chain":
        p.op("dve", lambda e: e.tensor_copy(out=hrt2snap[:, :], in_=hrt2R[:, :]),
             reads=["hrt2R"], writes=["hrt2snap"], tag="dmpE")
    can1 = sb(s2, "can1", [128, 64], F32)
    wo16s = sb(s2, "wo16s", [128, 8 * D], BF16)
    can2 = sb(s2, "can2", [128, 64], F32)
    yattn = sb(s2, "yattn", [128, NSL * D], BF16)
    ylb = sb(s2, "ylb", [128, D], BF16)
    a1b = sb(s2, "a1b", [128, D], BF16)
    a2b = sb(s2, "a2b", [128, D], BF16)
    ypb = sb(s2, "ypb", [128, D], BF16)
    ypfm = sb(s2, "ypfm", [128, D], BF16)
    fCC = sb(s2, "fCC", [128, D], F32)

    for fc in range(8):
        p.dma("qsp", lambda e, fc=fc: e.dma_start(
            out=wo16s[:, fc * D:(fc + 1) * D],
            in_=wo16_d[fc * 128:(fc + 1) * 128, :]),
            reads=[f"slab:wo:{fc}"], writes=["wo16s"], tag="wold")

    _KSKIP = set(_os.environ.get("KSKIP", "").split(","))
    _KP4 = _os.environ.get("KDBG", "") == "p4"
    p4dbg = {}
    if _KP4:
        for nm in ["d_hrr", "d_hri", "d_ypb", "d_fa"]:
            p4dbg[nm] = sb(s2, nm, [128, D], F32 if nm == "d_fa" else BF16)

    def p4_slice(s):
        p.dma("qsp", lambda e: e.dma_start(out=ylb[:, :], in_=ylocD[s]),
              reads=[f"ylD:{s}", "ylb"], writes=["ylb"], tag=f"rl{s}")
        p.dma("qsp", lambda e: e.dma_start(out=a1b[:, :], in_=a1D[s]),
              reads=[f"a1D:{s}", "a1b"], writes=["a1b"], tag=f"rl{s}")
        p.dma("qsp", lambda e: e.dma_start(out=a2b[:, :], in_=a2D[s]),
              reads=[f"a2D:{s}", "a2b"], writes=["a2b"], tag=f"rl{s}")
        for h in range(2):
            p.op("pe", lambda e, h=h: e.matmul(
                psCum[:, h * 512:(h + 1) * 512], chselA[:, s * 128:(s + 1) * 128],
                hrt2R[:, h * 512:(h + 1) * 512], start=True, stop=True),
                reads=["chselA", "hrt2R"], writes=[f"psC:{h}"], tag=f"hr{s}")
            p.op("pe", lambda e, h=h: e.matmul(
                psMain[:, h * 512:(h + 1) * 512], chselA[:, s * 128:(s + 1) * 128],
                hrt2I[:, h * 512:(h + 1) * 512], start=True, stop=True),
                reads=["chselA", "hrt2I"], writes=[f"psM:{h}"], tag=f"hr{s}")
        p.op("dve", lambda e: e.tensor_mul(out=fA[:, :], in0=a1b[:, :],
                                           in1=psCum[:, :]),
             reads=["a1b", "psC:0", "psC:1"], writes=["fA"])
        p.op("dve", lambda e: e.tensor_mul(out=fB[:, :], in0=a2b[:, :],
                                           in1=psMain[:, 0:1024]),
             reads=["a2b", "psM:0", "psM:1"], writes=["fB"])
        p.op("dve", lambda e: e.tensor_sub(out=fA[:, :], in0=fA[:, :],
                                           in1=fB[:, :]),
             reads=["fA", "fB"], writes=["fA"])
        if "corr" in _KSKIP:
            p.op("dve", lambda e: e.tensor_copy(out=ypb[:, :], in_=ylb[:, :]),
                 reads=["ylb"], writes=["ypb"])
        else:
            p.op("dve", lambda e: e.tensor_add(out=ypb[:, :], in0=ylb[:, :],
                                               in1=fA[:, :]),
                 reads=["ylb", "fA"], writes=["ypb"])
        if _KP4 and s == 2:
            p.op("act", lambda e: e.activation(out=p4dbg["d_hrr"][:, :],
                                               in_=psCum[:, :], func=AF.Copy),
                 reads=["psC:0", "psC:1"], writes=["d_hrr"], tag="p4d")
            p.op("act", lambda e: e.activation(out=p4dbg["d_hri"][:, :],
                                               in_=psMain[:, 0:1024], func=AF.Copy),
                 reads=["psM:0", "psM:1"], writes=["d_hri"], tag="p4d")
            p.op("dve", lambda e: e.tensor_copy(
                out=p4dbg["d_ypb"][:, :], in_=ypb[:, :]),
                reads=["ypb"], writes=["d_ypb"], tag="p4d")
            p.op("dve", lambda e: e.tensor_copy(out=p4dbg["d_fa"][:, :],
                                                in_=fA[:, :]),
                 reads=["fA"], writes=["d_fa"], tag="p4d")
        for dc in range(8):
            p.op("pe", lambda e, dc=dc: e.transpose(
                psTrB[:, dc * 128:(dc + 1) * 128],
                ypb[:, dc * 128:(dc + 1) * 128], ident[:, :]),
                reads=["ypb", "ident", "psTrB"], writes=["psTrB"], tag=f"ytr{s}")
        p.op("act", lambda e: e.activation(out=ypfm[:, :], in_=psTrB[:, :],
                                           func=AF.Copy),
             reads=["psTrB"], writes=["ypfm"], tag=f"ytre{s}")
        for fc in range(8):
            for h in range(2):
                p.op("pe", lambda e, fc=fc, h=h: e.matmul(
                    psAux[:, h * 512:(h + 1) * 512],
                    ypfm[:, fc * 128:(fc + 1) * 128],
                    wo16s[:, fc * D + h * 512: fc * D + (h + 1) * 512],
                    start=(fc == 0), stop=(fc == 7)),
                    reads=["ypfm", "wo16s"], writes=[f"psA:{h}"], tag=f"wo{s}")
        p.op("act", lambda e: e.activation(out=yattn[:, s * D:(s + 1) * D],
                                           in_=psAux[:, :], func=AF.Copy),
             reads=["psA:0", "psA:1"], writes=[f"yattn:{s}"], tag=f"woe{s}")

    snap_httR(hsnapA, "hsnapA")
    for s in range(NSL):
        p4_slice(s)
    snap_httR(hsnapB, "hsnapB")

    can3 = sb(s2, "can3", [128, 64], F32)
    xqfm = sb(s2, "xqfm", [128, 8 * 512], BF16)
    can4 = sb(s2, "can4", [128, 64], F32)
    hg_g = sb(s2, "hg_g", [128, 32 * 512], BF16)
    hsqb = sb(s2, "hsqb", [128, 512], BF16)
    mxb = sb(s2, "mxb", [128, 512], BF16)
    mnb = sb(s2, "mnb", [128, 512], BF16)
    can5 = sb(s2, "can5", [128, 64], F32)
    w1sl = sb(s2, "w1sl", [128, 2 * 1024], BF16)
    can6 = sb(s2, "can6", [128, 64], F32)
    w2sl = sb(s2, "w2sl", [128, 2 * 1024], BF16)
    can7 = sb(s2, "can7", [128, 64], F32)
    xq_tm = sb(s2, "xq_tm", [128, D], BF16)
    xs2 = sb(s2, "xs2", [128, D], F32)
    mrow = sb(s2, "mrow", [1, 512], F32)
    mrowb = sb(s2, "mrowb", [1, 512], BF16)
    mrowb2 = sb(s2, "mrowb2", [1, 512], BF16)
    r1 = sb(s2, "r1", [1, 512], F32)
    r2 = sb(s2, "r2", [1, 512], F32)
    c2rep = sb(s2, "c2rep", [128, 512], BF16)
    i2rep = sb(s2, "i2rep", [128, 512], BF16)
    q16i = sb(s2, "q16i", [128, 512], I32)
    of1 = sb(s2, "of1", [128, D], F32)
    mxcol = sb(s2, "mxcol", [128, 4], F32)

    if _KD:
        for i_, cx in enumerate([can0, can1, can2, can3, can4, can5, can6, can7]):
            p.op("dve", lambda e, cx=cx: e.memset(cx[:, :], 777.0),
                 writes=[f"can:{i_}"], tag="can")

    _KSKIP = set(_os.environ.get("KSKIP", "").split(","))

    def p5_group(g):
        for j in range(4):
            s = g * 4 + j
            p.dma("qsp", lambda e, s=s: e.dma_start(
                out=xs[:, :], in_=x_d[s * 128:(s + 1) * 128, :]),
                reads=["xs"], writes=["xs"], tag=f"x2{s}")
            p.op("act", lambda e, s=s: e.activation(
                out=xs[:, :], in_=xs[:, :], func=AF.Identity,
                scale=istd_a[:, s:s + 1], bias=nmu_a[:, s:s + 1]),
                reads=["xs", f"istd:{s}", f"nmu:{s}"], writes=["xs"])
            p.op("act", lambda e, s=s: e.activation(
                out=fA[:, :], in_=xs[:, :], func=AF.Identity,
                scale=ct_a[:, s:s + 1]),
                reads=["xs", f"ct:{s}"], writes=["fA"])
            p.op("dve", lambda e: e.tensor_copy(out=fB.ap().bitcast(I32),
                                                in_=fA[:, :]),
                 reads=["fA", "fB"], writes=["fB"])
            p.op("dve", lambda e: e.tensor_scalar(out=fA[:, :],
                                                  in0=fB.ap().bitcast(I32),
                                                  scalar1=127.0, scalar2=None,
                                                  op0=OP.min),
                 reads=["fB"], writes=["fA"])
            p.op("act", lambda e, s=s: e.activation(
                out=xq_tm[:, :], in_=fA[:, :], func=AF.Identity,
                scale=invs_a[:, s:s + 1]),
                reads=["fA", f"invs:{s}"], writes=["xq_tm"])
            for dc in range(8):
                p.op("pe", lambda e, dc=dc: e.transpose(
                    psTrB[:, dc * 128:(dc + 1) * 128],
                    xq_tm[:, dc * 128:(dc + 1) * 128], ident[:, :]),
                    reads=["xq_tm", "ident", "psTrB"], writes=["psTrB"],
                    tag=f"xqt{s}")
            p.op("act", lambda e, j=j: e.activation(
                out=xqfm.ap().rearrange("p (c t) -> p c t", c=8)[:, :,
                                                                j * 128:(j + 1) * 128],
                in_=psTrB.ap().rearrange("p (c t) -> p c t", c=8),
                func=AF.Copy),
                reads=["psTrB"], writes=[f"xqfm:{j}"], tag=f"xqe{s}")
        xq_reads = [f"xqfm:{j}" for j in range(4)]
        for fc in range(32 if "ffn1" not in _KSKIP else 0):
            sl = fc % 2
            p.dma("qsp", lambda e, fc=fc, sl=sl: e.dma_start(
                out=w1sl[:, sl * 1024:(sl + 1) * 1024], in_=w1t16_d[fc]),
                reads=[f"slab:w1:{fc}", f"w1sl:{sl}"], writes=[f"w1sl:{sl}"],
                tag=f"w1l{g}")
            for dc in range(8):
                p.op("pe", lambda e, fc=fc, sl=sl, dc=dc: e.matmul(
                    psMain[:, 0:512],
                    w1sl[:, sl * 1024 + dc * 128: sl * 1024 + (dc + 1) * 128],
                    xqfm[:, dc * 512:(dc + 1) * 512],
                    start=(dc == 0), stop=(dc == 7)),
                    reads=[f"w1sl:{sl}"] + xq_reads, writes=["psM:0"],
                    tag=f"f1m{g}")
            p.op("act", lambda e, fc=fc: e.activation(
                out=hg_g[:, fc * 512:(fc + 1) * 512], in_=psMain[:, 0:512],
                func=AF.Gelu, bias=b1c[:, fc:fc + 1]),
                reads=["psM:0", "b1c"], writes=[f"hg:{fc}"], tag=f"f1e{g}")
            p.op("act", lambda e, fc=fc: e.activation(
                out=hsqb[:, :], in_=hg_g[:, fc * 512:(fc + 1) * 512],
                func=AF.Square),
                reads=[f"hg:{fc}", "hsqb"], writes=["hsqb"])
            p.op("pe", lambda e, fc=fc: e.matmul(
                psMain[:, 512:1024], onesr[:, :], hsqb[:, :],
                start=(fc == 0), stop=(fc == 31)),
                reads=["hsqb", "onesr"], writes=["psM:1"], tag=f"sq{g}")
            if fc == 0:
                p.op("dve", lambda e: e.tensor_copy(out=mxb[:, :],
                                                    in_=hg_g[:, 0:512]),
                     reads=["hg:0", "mxb"], writes=["mxb"])
                p.op("dve", lambda e: e.tensor_copy(out=mnb[:, :],
                                                    in_=hg_g[:, 0:512]),
                     reads=["hg:0", "mnb"], writes=["mnb"])
            else:
                p.op("dve", lambda e, fc=fc: e.tensor_tensor(
                    out=mxb[:, :], in0=mxb[:, :],
                    in1=hg_g[:, fc * 512:(fc + 1) * 512], op=OP.max),
                    reads=["mxb", f"hg:{fc}"], writes=["mxb"])
                p.op("dve", lambda e, fc=fc: e.tensor_tensor(
                    out=mnb[:, :], in0=mnb[:, :],
                    in1=hg_g[:, fc * 512:(fc + 1) * 512], op=OP.min),
                    reads=["mnb", f"hg:{fc}"], writes=["mnb"])
        if "stats" in _KSKIP:
            return
        p.op("dve", lambda e: e.tensor_scalar(out=mnb[:, :], in0=mnb[:, :],
                                              scalar1=-1.0, scalar2=None,
                                              op0=OP.mult),
             reads=["mnb"], writes=["mnb"])
        p.op("dve", lambda e: e.tensor_max(out=mxb[:, :], in0=mxb[:, :],
                                           in1=mnb[:, :]),
             reads=["mxb", "mnb"], writes=["mxb"])
        for t_ in range(4):
            p.op("pe", lambda e, t_=t_: e.transpose(
                psTrB[:, 0:128], mxb[:, t_ * 128:(t_ + 1) * 128], ident[:, :]),
                reads=["mxb", "ident", "psTrB"], writes=["psTrB"], tag=f"mxt{g}")
            p.op("dve", lambda e, t_=t_: e.tensor_reduce(
                out=mxcol[:, t_:t_ + 1], in_=psTrB[:, 0:128], axis=AX,
                op=OP.max),
                reads=["psTrB"], writes=["mxcol"])
        for t_ in range(4):
            p.dma("qpool", lambda e, t_=t_: e.dma_start(
                out=mrow[0:1, t_ * 128:(t_ + 1) * 128],
                in_=mxcol[:, t_:t_ + 1]),
                reads=["mxcol"], writes=["mrow"], tag=f"mx{g}")
        p.op("act", lambda e: e.activation(out=r1[:, :], in_=psMain[0:1, 512:1024],
                                           func=AF.Sqrt, scale=1.0 / FF, bias=cEPS[0:1, :]),
             reads=["psM:1", "cEPS"], writes=["r1"])
        p.op("dve", lambda e: e.reciprocal(out=r1[:, :], in_=r1[:, :]),
             reads=["r1"], writes=["r1"])
        p.op("dve", lambda e: e.tensor_mul(out=r2[:, :], in0=mrow[:, :],
                                           in1=r1[:, :]),
             reads=["mrow", "r1"], writes=["r2"])
        p.op("dve", lambda e: e.tensor_scalar(out=r2[:, :], in0=r2[:, :],
                                              scalar1=1e-5, scalar2=None,
                                              op0=OP.max),
             reads=["r2"], writes=["r2"])
        p.op("dve", lambda e: e.tensor_scalar(out=mrowb2[:, :], in0=r2[:, :],
                                              scalar1=1.0 / 127.0, scalar2=None,
                                              op0=OP.mult),
             reads=["r2"], writes=["mrowb2"])
        p.op("dve", lambda e: e.reciprocal(out=r2[:, :], in_=r2[:, :]),
             reads=["r2"], writes=["r2"])
        p.op("dve", lambda e: e.tensor_mul(out=r2[:, :], in0=r2[:, :],
                                           in1=r1[:, :]),
             reads=["r2", "r1"], writes=["r2"])
        p.op("dve", lambda e: e.tensor_scalar(out=mrowb[:, :], in0=r2[:, :],
                                              scalar1=127.0, scalar2=None,
                                              op0=OP.mult),
             reads=["r2"], writes=["mrowb"])
        p.op("pe", lambda e: e.matmul(psAux[:, 0:512], onesr[0:1, :],
                                      mrowb[:, :], start=True, stop=True),
             reads=["onesr", "mrowb"], writes=["psA:0"], tag=f"rep{g}")
        p.op("pe", lambda e: e.matmul(psAux[:, 512:1024], onesr[0:1, :],
                                      mrowb2[:, :], start=True, stop=True),
             reads=["onesr", "mrowb2"], writes=["psA:1"], tag=f"rep{g}")
        p.op("act", lambda e: e.activation(out=c2rep[:, :], in_=psAux[:, 0:512],
                                           func=AF.Copy),
             reads=["psA:0"], writes=["c2rep"])
        p.op("act", lambda e: e.activation(out=i2rep[:, :], in_=psAux[:, 512:1024],
                                           func=AF.Copy),
             reads=["psA:1"], writes=["i2rep"])
        for fc in range(32 if "quant" not in _KSKIP else 0):
            p.op("dve", lambda e, fc=fc: e.tensor_tensor(
                out=q16i[:, :], in0=hg_g[:, fc * 512:(fc + 1) * 512],
                in1=c2rep[:, :], op=OP.mult),
                reads=[f"hg:{fc}", "c2rep"], writes=["q16i"])
            p.op("dve", lambda e, fc=fc: e.scalar_tensor_tensor(
                out=hg_g[:, fc * 512:(fc + 1) * 512], in0=q16i[:, :],
                scalar=127.0, in1=i2rep[:, :], op0=OP.min, op1=OP.mult),
                reads=["q16i", "i2rep"], writes=[f"hg:{fc}"])
        for half in range(2 if "ffn2" not in _KSKIP else 0):
            psO = [psCum, psAux]
            pks = ["psC", "psA"]
            for fc in range(32):
                sl = fc % 2
                p.dma("qsp", lambda e, fc=fc, sl=sl: e.dma_start(
                    out=w2sl[:, sl * 1024:(sl + 1) * 1024],
                    in_=w2t16_d[fc * 128:(fc + 1) * 128, :]),
                    reads=[f"slab:w2:{fc}", f"w2sl:{sl}"], writes=[f"w2sl:{sl}"],
                    tag=f"w2l{g}{half}")
                for j2 in range(2):
                    j = half * 2 + j2
                    for h in range(2):
                        p.op("pe", lambda e, fc=fc, sl=sl, j=j, j2=j2, h=h:
                             e.matmul(
                                 psO[j2][:, h * 512:(h + 1) * 512],
                                 hg_g[:, fc * 512 + j * 128:
                                      fc * 512 + (j + 1) * 128],
                                 w2sl[:, sl * 1024 + h * 512:
                                      sl * 1024 + (h + 1) * 512],
                                 start=(fc == 0), stop=False),
                             reads=[f"w2sl:{sl}", f"hg:{fc}"],
                             writes=[pks[j2] + f":{h}"], tag=f"f2m{g}")
            for j2 in range(2):
                j = half * 2 + j2
                s = g * 4 + j
                pk = pks[j2]
                for h in range(2):
                    p.op("pe", lambda e, j2=j2, h=h: e.matmul(
                        psO[j2][:, h * 512:(h + 1) * 512], onesr[0:1, :],
                        b2r16[:, h * 512:(h + 1) * 512], start=False, stop=True),
                        reads=["onesr", "b2r16"], writes=[pk + f":{h}"],
                        tag=f"b2{g}")
                p.dma("qsp", lambda e, s=s: e.dma_start(
                    out=xs2[:, :], in_=x_d[s * 128:(s + 1) * 128, :]),
                    reads=["xs2"], writes=["xs2"], tag=f"x3{s}")
                DBG = _os.environ.get("KDBG", "")
                if DBG == "attn":
                    p.op("dve", lambda e, s=s: e.tensor_copy(
                        out=of1[:, :], in_=yattn[:, s * D:(s + 1) * D]),
                        reads=[f"yattn:{s}", "of1", pk + ":0", pk + ":1", "xs2"],
                        writes=["of1"])
                elif DBG == "ffn":
                    p.op("dve", lambda e, j2=j2: e.tensor_add(
                        out=of1[:, :], in0=psO[j2][:, :], in1=xs2[:, :]),
                        reads=[pk + ":0", pk + ":1", "xs2", "of1"], writes=["of1"])
                else:
                    p.op("dve", lambda e, j2=j2: e.tensor_add(
                        out=of1[:, :], in0=psO[j2][:, :], in1=xs2[:, :]),
                        reads=[pk + ":0", pk + ":1", "xs2", "of1"], writes=["of1"])
                    p.op("dve", lambda e, s=s: e.tensor_add(
                        out=of1[:, :], in0=of1[:, :], in1=yattn[:, s * D:(s + 1) * D]),
                        reads=["of1", f"yattn:{s}"], writes=["of1"])
                p.dma("qpool", lambda e, s=s: e.dma_start(
                    out=out_d[s * 128:(s + 1) * 128, :], in_=of1[:, :]),
                    reads=["of1"], writes=[f"out:{s}"], tag=f"o{s}")

    for g in range(4):
        p5_group(g)

    if _os.environ.get("KDBG", "") == "chain":
        dump = [(can0, 64, "c0"), (can1, 64, "c1"), (can2, 64, "c2"),
                (can3, 64, "c3"), (can4, 64, "c4"), (can5, 64, "c5"),
                (can6, 64, "c6"), (can7, 64, "c7"),
                (hP_r, 264, "hPr"), (httR, 256, "httR"),
                (hrt2R, 1024, "hrt2R"), (hrt2snap, 1024, "hrt2snap"),
                (hsnapA, 256, "hsnapA"), (hsnapB, 256, "hsnapB"),
                (hsnapC, 256, "hsnapC"), (httRsnap, 256, "httRsnap")]
        row = 0
        for (tn, w, nm) in dump:
            nrow = tn.shape[0]
            p.dma("qpool", lambda e, tn=tn, w=w, row=row, nrow=nrow: e.dma_start(
                out=out_d[row * 128:row * 128 + nrow, 0:w], in_=tn[:, 0:w]),
                reads=[f"out:{row}"], writes=[f"dmp:{row}"], tag="dmp")
            row += 1
        p.barrier_read("sp", [f"dmp:{r}" for r in range(row)])
    if _KP4:
        rowd = 0
        for nm in ["d_hrr", "d_hri", "d_ypb", "d_fa"]:
            p.dma("qpool", lambda e, nm=nm, rowd=rowd: e.dma_start(
                out=out_d[rowd * 128:(rowd + 1) * 128, :], in_=p4dbg[nm][:, :]),
                reads=[nm, f"out:{rowd}"], writes=[f"p4dmp:{rowd}"], tag="p4dmp")
            rowd += 1
        p.barrier_read("sp", [f"p4dmp:{r}" for r in range(rowd)])
    if _KP1:
        p.barrier_read("sp", [f"dbg:{di}" for di in range(12)])
    p.barrier_read("sp", [f"out:{s}" for s in range(NSL)])
    rem = nc.sbuf_bytes_remaining
    import sys
    print(f"[kernel-build] sbuf remaining {rem}B/partition, "
          f"{len(p.ops)} ops", file=sys.stderr)
    p.emit()
    s2.close()
    gstack.close()
    return nc, rem


_NC_CACHE = [None]
kernel_last_res = [None]


def kernel(L0_attn_x, L0_attn_gamma, L0_attn_wq, L0_attn_wk, L0_attn_wv,
           L0_attn_wa, L0_attn_wg, L0_attn_wo, L0_ffn_w1, L0_ffn_b1,
           L0_ffn_w2, L0_ffn_b2):
    x = np.ascontiguousarray(np.asarray(L0_attn_x, np.float32))
    gamma = np.asarray(L0_attn_gamma, np.float32)
    wq = np.ascontiguousarray(np.asarray(L0_attn_wq, np.float32))
    wk = np.ascontiguousarray(np.asarray(L0_attn_wk, np.float32))
    wv = np.ascontiguousarray(np.asarray(L0_attn_wv, np.float32))
    wa = np.ascontiguousarray(np.asarray(L0_attn_wa, np.float32))
    wg = np.ascontiguousarray(np.asarray(L0_attn_wg, np.float32))
    wo = np.ascontiguousarray(np.asarray(L0_attn_wo, np.float32))
    w1 = np.asarray(L0_ffn_w1, np.float32)
    b1 = np.asarray(L0_ffn_b1, np.float32)
    w2 = np.asarray(L0_ffn_w2, np.float32)
    b2 = np.asarray(L0_ffn_b2, np.float32)

    w1t = np.ascontiguousarray(w1.T.reshape(D, 32, 128).transpose(1, 0, 2))
    w2t = np.ascontiguousarray(w2.T)
    b1c = np.ascontiguousarray(b1.reshape(32, 128).T)
    b2r = np.ascontiguousarray(b2.reshape(1, D))
    gr = np.ascontiguousarray(np.broadcast_to(gamma, (128, D)))

    BF = ml_dtypes.bfloat16
    tri = np.zeros((128, 128), np.float32)
    for b_ in range(2):
        for s_ in range(64):
            tri[b_ * 64 + s_, b_ * 64 + s_:(b_ + 1) * 64] = 1.0
    consts = {
        "tri_m1": (-tri).astype(BF),
        "tri_p4": (4.0 * tri).astype(BF),
        "tri_p1": tri.astype(BF),
        "ident": np.eye(128, dtype=np.float32).astype(BF),
        "onesr": np.ones((128, 128), np.float32).astype(BF),
    }
    cind = np.zeros((128, 2), np.float32)
    cind[0:64, 0] = 1.0
    cind[64:128, 1] = 1.0
    consts["cind"] = cind.astype(BF)
    chsel = np.zeros((32, NSL * 128), np.float32)
    for s_ in range(NSL):
        chsel[2 * s_, s_ * 128: s_ * 128 + 64] = 1.0
        chsel[2 * s_ + 1, s_ * 128 + 64: (s_ + 1) * 128] = 1.0
    consts["chsel"] = chsel.astype(BF)

    if _NC_CACHE[0] is None:
        _NC_CACHE[0] = build()[0]
    nc = _NC_CACHE[0]

    in_maps = []
    for c in range(8):
        b_, h_ = c // 2, c % 2
        sel = np.zeros(8, np.float32)
        if h_ == 1:
            sel[c - 1] = 1.0
        selrep = np.ascontiguousarray(
            np.broadcast_to(np.repeat(sel, 16)[None, :], (128, 128)))
        m = {"x": np.ascontiguousarray(x[b_, h_ * T:(h_ + 1) * T]),
             "wq": wq, "wk": wk, "wv": wv, "wg": wg, "wa": wa, "wo": wo,
             "w1t": w1t, "w2t": w2t, "b1c": b1c, "b2r": b2r, "gr": gr,
             "selrep": selrep}
        m.update(consts)
        in_maps.append(m)
    res = run_bass_kernel_spmd(nc, in_maps, core_ids=list(range(8)))
    kernel_last_res[0] = res
    out = np.empty((4, 4096, D), np.float32)
    for c in range(8):
        b_, h_ = c // 2, c % 2
        out[b_, h_ * T:(h_ + 1) * T] = res.results[c]["out"]
    return out


# version 04
# speedup vs baseline: 1.0060x; 1.0060x over previous
import math
import os as _os
import contextlib
from collections import defaultdict

import numpy as np
import ml_dtypes
import concourse.bass as bass
import concourse.mybir as mybir
from concourse.bass_utils import run_bass_kernel_spmd

F32 = mybir.dt.float32
BF16 = mybir.dt.bfloat16
I32 = mybir.dt.int32
I16 = mybir.dt.int16
AF = mybir.ActivationFunctionType
OP = mybir.AluOpType
PI = math.pi
TWO_PI = 2.0 * math.pi

T = 2048
D = 1024
FF = 4096
NSL = 16
C = 64
NCH = T // C
EPS = 1e-5
AX = mybir.AxisListType.X


class _Op:
    __slots__ = ("eng", "emit", "reads", "writes", "dma", "idx", "need_inc",
                 "inc_val", "waits", "tag")

    def __init__(self, eng, emit, reads, writes, dma, tag):
        self.eng, self.emit, self.reads, self.writes = eng, emit, reads, writes
        self.dma = dma
        self.idx = None
        self.need_inc = False
        self.inc_val = None
        self.waits = []
        self.tag = tag


_Q_OF = {"qsp": "sp", "qpool": "pool", "qact": "act", "qdve": "dve"}


class Prog:
    def __init__(self, nc):
        self.nc = nc
        self.ops = []
        self.last_w = {}
        self.readers = defaultdict(dict)
        self.pending_fence = None
        self._fenced = set()

    def op(self, eng, emit, reads=(), writes=(), tag=""):
        return self._plan(eng, emit, reads, writes, None, tag)

    def dma(self, queue, emit, reads=(), writes=(), tag=""):
        return self._plan(_Q_OF[queue], emit, reads, writes, queue, tag)

    def barrier_read(self, eng, keys, tag="fin"):
        return self._plan(eng, None, tuple(keys), (), None, tag)

    def fence(self):
        snap = {}
        for o in self.ops:
            snap[o.dma if o.dma else o.eng] = o
        for o in snap.values():
            o.need_inc = True
        self.pending_fence = dict(snap)
        self._fenced = set()

    def _plan(self, eng, emit, reads, writes, dma, tag):
        o = _Op(eng, emit, tuple(reads), tuple(writes), dma, tag)
        sem_name = dma if dma else eng
        deps = {}

        def add_dep(p_):
            if p_ is None:
                return
            psem = p_.dma if p_.dma else p_.eng
            cur = deps.get(psem)
            if cur is None or p_.idx > cur.idx:
                deps[psem] = p_

        if self.pending_fence is not None and sem_name not in self._fenced:
            self._fenced.add(sem_name)
            for p_ in self.pending_fence.values():
                add_dep(p_)
        for r in o.reads:
            add_dep(self.last_w.get(r))
        for w in o.writes:
            add_dep(self.last_w.get(w))
            for p_ in self.readers.get(w, {}).values():
                add_dep(p_)

        o.idx = len(self.ops)
        if dma:
            o.need_inc = True
        self.ops.append(o)
        for p_ in deps.values():
            p_.need_inc = True
            o.waits.append(p_)
        if emit is not None:
            for w in o.writes:
                self.last_w[w] = o
                self.readers[w] = {}
            for r in o.reads:
                self.readers[r][sem_name] = o
        return o

    def emit(self):
        nc = self.nc
        counts = defaultdict(int)
        for o in self.ops:
            if o.need_inc:
                sem = o.dma if o.dma else o.eng
                counts[sem] += 16 if o.dma else 1
                o.inc_val = counts[sem]
        per_eng = defaultdict(list)
        for o in self.ops:
            per_eng[o.eng].append(o)
        stack = contextlib.ExitStack()
        sems = {s: stack.enter_context(nc.semaphore("sem_" + s))
                for s in sorted(counts)}

        def run_engine(eng_obj, eng_name):
            waited = {}
            for o in per_eng[eng_name]:
                for p_ in o.waits:
                    psem = p_.dma if p_.dma else p_.eng
                    v = p_.inc_val
                    if waited.get(psem, -1) < v:
                        eng_obj.wait_ge(sems[psem], v)
                        waited[psem] = v
                if o.emit is None:
                    continue
                inst = o.emit(eng_obj)
                if o.need_inc:
                    sem = o.dma if o.dma else o.eng
                    inst.then_inc(sems[sem], 16 if o.dma else 1)

        with stack:
            with nc.Block() as block:
                for name, attr in [("pe", "tensor"), ("act", "scalar"),
                                   ("dve", "vector"), ("pool", "gpsimd"),
                                   ("sp", "sync")]:
                    if per_eng[name]:
                        getattr(block, attr)(
                            lambda e, n=name: run_engine(e, n))


def build():
    nc = bass.Bass("TRN2", target_bir_lowering=False)
    p = Prog(nc)

    dt = nc.dram_tensor
    x_d = dt("x", [T, D], F32, kind="ExternalInput").ap()
    wq_d = dt("wq", [D, D], F32, kind="ExternalInput").ap()
    wk_d = dt("wk", [D, D], F32, kind="ExternalInput").ap()
    wv_d = dt("wv", [D, D], F32, kind="ExternalInput").ap()
    wg_d = dt("wg", [D, D], F32, kind="ExternalInput").ap()
    wa_d = dt("wa", [D, 2 * D], F32, kind="ExternalInput").ap()
    wo_d = dt("wo", [D, D], F32, kind="ExternalInput").ap()
    w1t_d = dt("w1t", [32, D, 128], F32, kind="ExternalInput").ap()
    w2t_d = dt("w2t", [FF, D], F32, kind="ExternalInput").ap()
    b1c_d = dt("b1c", [128, 32], F32, kind="ExternalInput").ap()
    b2r_d = dt("b2r", [1, D], F32, kind="ExternalInput").ap()
    gr_d = dt("gr", [128, D], F32, kind="ExternalInput").ap()
    tri_m1_d = dt("tri_m1", [128, 128], BF16, kind="ExternalInput").ap()
    tri_p4_d = dt("tri_p4", [128, 128], BF16, kind="ExternalInput").ap()
    tri_p1_d = dt("tri_p1", [128, 128], BF16, kind="ExternalInput").ap()
    ident_d = dt("ident", [128, 128], BF16, kind="ExternalInput").ap()
    cind_d = dt("cind", [128, 2], BF16, kind="ExternalInput").ap()
    ones_d = dt("onesr", [128, 128], BF16, kind="ExternalInput").ap()
    chsel_d = dt("chsel", [32, NSL * 128], BF16, kind="ExternalInput").ap()
    selrep_d = dt("selrep", [128, 128], F32, kind="ExternalInput").ap()
    out_d = dt("out", [T, D], F32, kind="ExternalOutput").ap()
    w1t16_d = dt("w1t16", [32, 128, 1024], BF16, kind="Internal").ap()
    w2t16_d = dt("w2t16", [FF, D], BF16, kind="Internal").ap()
    wo16_d = dt("wo16", [D, D], BF16, kind="Internal").ap()
    ylocD = dt("ylocD", [NSL, 128, D], BF16, kind="Internal").ap()
    a1D = dt("a1D", [NSL, 128, D], BF16, kind="Internal").ap()
    a2D = dt("a2D", [NSL, 128, D], BF16, kind="Internal").ap()
    ccin_d = dt("ccin", [1, 2048], F32, kind="Internal").ap()
    ccout_d = dt("ccout", [8, 2048], F32, kind="Internal").ap()

    gstack = contextlib.ExitStack()

    def sb(stk, n, shape, dty):
        return stk.enter_context(nc.sbuf_tensor(n, shape, dty))

    def psum(stk, n, shape, dty):
        return stk.enter_context(nc.psum_tensor(n, shape, dty))

    tri_m1 = sb(gstack, "s_tri_m1", [128, 128], BF16)
    tri_p4 = sb(gstack, "s_tri_p4", [128, 128], BF16)
    tri_p1 = sb(gstack, "s_tri_p1", [128, 128], BF16)
    ident = sb(gstack, "s_ident", [128, 128], BF16)
    cind = sb(gstack, "s_cind", [128, 2], BF16)
    onesr = sb(gstack, "s_onesr", [128, 128], BF16)
    chselA = sb(gstack, "s_chsel", [32, NSL * 128], BF16)
    selrep = sb(gstack, "s_selrep", [128, 128], F32)
    gr = sb(gstack, "s_gr", [128, D], F32)
    b1c = sb(gstack, "s_b1c", [128, 32], F32)
    bn6 = sb(gstack, "bn6", [128, 12], F32)
    bn2 = sb(gstack, "bn2", [128, 2], F32)
    istd_a = sb(gstack, "istd_a", [128, NSL], F32)
    nmu_a = sb(gstack, "nmu_a", [128, NSL], F32)
    scl_a = sb(gstack, "scl_a", [128, NSL], F32)
    ct_a = sb(gstack, "ct_a", [128, NSL], F32)
    invs_a = sb(gstack, "invs_a", [128, NSL], F32)
    cEPS = sb(gstack, "cEPS", [128, 1], F32)
    cPIH = sb(gstack, "cPIH", [128, 1], F32)
    c38 = sb(gstack, "c38", [128, 1], F32)
    tA = sb(gstack, "tA", [128, 1], F32)
    tB = sb(gstack, "tB", [128, 1], F32)
    tC = sb(gstack, "tC", [128, 1], F32)
    tD_ = sb(gstack, "tD", [128, 1], F32)
    aggS = sb(gstack, "aggS", [128, 80], F32)
    hP_r = sb(gstack, "hP_r", [128, 8 * (NCH + 1)], F32)
    hP_i = sb(gstack, "hP_i", [128, 8 * (NCH + 1)], F32)
    PP_r = sb(gstack, "PP_r", [128, 8 * NCH], F32)
    PP_i = sb(gstack, "PP_i", [128, 8 * NCH], F32)
    magt = sb(gstack, "magt", [128, 16], F32)
    tt1 = sb(gstack, "tt1", [128, 16], F32)
    tt2 = sb(gstack, "tt2", [128, 16], F32)
    tt3 = sb(gstack, "tt3", [128, 16], F32)
    tt4 = sb(gstack, "tt4", [128, 16], F32)
    tki = sb(gstack, "tki", [128, 16], I32)
    acr_t = sb(gstack, "acr_t", [128, 16], F32)
    aci_t = sb(gstack, "aci_t", [128, 16], F32)
    slr_t = sb(gstack, "slr_t", [128, 16], F32)
    sli_t = sb(gstack, "sli_t", [128, 16], F32)
    ch1 = sb(gstack, "ch1", [128, 8], F32)
    ch2 = sb(gstack, "ch2", [128, 8], F32)
    Hpack = sb(gstack, "Hpack", [128, 16], F32)
    gatc = sb(gstack, "gatc", [128, 128], F32)
    Hprev = sb(gstack, "Hprev", [128, 16], F32)
    HrR = sb(gstack, "HrR", [128, 8 * NCH], F32)
    HiR = sb(gstack, "HiR", [128, 8 * NCH], F32)
    httR = sb(gstack, "httR", [128, 8 * NCH], BF16)
    httI = sb(gstack, "httI", [128, 8 * NCH], BF16)
    hrt2R = sb(gstack, "hrt2R", [32, D], BF16)
    hrt2I = sb(gstack, "hrt2I", [32, D], BF16)
    xs = sb(gstack, "xs", [128, D], F32)
    xng_tm = sb(gstack, "xng_tm", [128, D], BF16)
    fA = sb(gstack, "fA", [128, D], F32)
    fB = sb(gstack, "fB", [128, D], F32)
    wsl_f = sb(gstack, "wsl_f", [128, 1024], F32)
    wsl_b = sb(gstack, "wsl_b", [128, 1024], BF16)
    wslq = sb(gstack, "wslq", [128, 1024], F32)
    wslq2 = sb(gstack, "wslq2", [128, 1024], F32)
    onesf = sb(gstack, "onesf", [128, 128], F32)
    acc_w1 = sb(gstack, "acc_w1", [128, 1], F32)
    acc_w2 = sb(gstack, "acc_w2", [128, 1], F32)
    asum = sb(gstack, "asum", [128, 1], F32)
    sw_w1 = sb(gstack, "sw_w1", [128, 1], F32)
    sw_w2 = sb(gstack, "sw_w2", [128, 1], F32)
    mc_w1 = sb(gstack, "mc_w1", [128, 1], F32)
    mc_w2 = sb(gstack, "mc_w2", [128, 1], F32)

    psMain = psum(gstack, "psMain", [128, 1536], F32)
    psAux = psum(gstack, "psAux", [128, 1024], F32)
    psCum = psum(gstack, "psCum", [128, 1024], F32)
    psTrB = psum(gstack, "psTrB", [128, 1024], BF16)

    for nm, dst, src in [("tri_m1", tri_m1, tri_m1_d), ("tri_p4", tri_p4, tri_p4_d),
                         ("tri_p1", tri_p1, tri_p1_d), ("ident", ident, ident_d),
                         ("cind", cind, cind_d), ("onesr", onesr, ones_d),
                         ("chselA", chselA, chsel_d), ("selrep", selrep, selrep_d),
                         ("gr", gr, gr_d), ("b1c", b1c, b1c_d)]:
        p.dma("qsp", lambda e, dst=dst, src=src: e.dma_start(out=dst[:, :], in_=src),
              writes=[nm], tag="cst")
    p.op("dve", lambda e: e.memset(cEPS[:, :], EPS), writes=["cEPS"])
    p.op("dve", lambda e: e.memset(cPIH[:, :], PI / 2), writes=["cPIH"])
    p.op("dve", lambda e: e.memset(c38[:, :], 1e-38), writes=["c38"])
    p.op("dve", lambda e: e.memset(hP_r[:, 0:8], 0.0), writes=["hP:0"])
    p.op("dve", lambda e: e.memset(hP_i[:, 0:8], 0.0), writes=["hPi:0"])
    p.op("dve", lambda e: e.memset(PP_r[:, 0:8], 1.0), writes=["PP:0"])
    p.op("dve", lambda e: e.memset(PP_i[:, 0:8], 0.0), writes=["PPi:0"])
    p.op("dve", lambda e: e.memset(acc_w1[:, :], 0.0), writes=["acc_w1"])
    p.op("dve", lambda e: e.memset(onesf[:, :], 1.0), writes=["onesf"])
    p.op("dve", lambda e: e.memset(acc_w2[:, :], 0.0), writes=["acc_w2"])

    s0 = contextlib.ExitStack()
    wstage = s0.enter_context(
        nc.sbuf_tensor("wstage", [128, 6144], F32, side="right"))
    s1 = contextlib.ExitStack()
    wqkv16 = sb(s1, "wqkv16", [128, 8 * 6144], BF16)
    for dc in range(8):
        srcs = [(wq_d, 0, D), (wk_d, D, D), (wv_d, 2 * D, D), (wg_d, 3 * D, D),
                (wa_d, 4 * D, 2 * D)]
        rk = [f"wst:{dc}:{off}" for (_, off, _) in srcs]
        for (w, off, width) in srcs:
            p.dma("qsp",
                  lambda e, w=w, off=off, width=width, dc=dc:
                  e.dma_start(out=wstage[:, off:off + width],
                              in_=w[dc * 128:(dc + 1) * 128, :]),
                  reads=[f"wstfree:{j}" for j in range(3)] if dc else [],
                  writes=[f"wst:{dc}:{off}"], tag="wld")
        for j, eng in enumerate(["dve", "act", "pool"]):
            lo, hi = j * 2048, (j + 1) * 2048
            base = dc * 6144
            if eng == "act":
                emit = (lambda e, lo=lo, hi=hi, base=base:
                        e.activation(out=wqkv16[:, base + lo:base + hi],
                                     in_=wstage[:, lo:hi], func=AF.Copy))
            else:
                emit = (lambda e, lo=lo, hi=hi, base=base:
                        e.tensor_copy(out=wqkv16[:, base + lo:base + hi],
                                      in_=wstage[:, lo:hi]))
            p.op(eng, emit, reads=rk, writes=[f"wq16:{dc}", f"wstfree:{j}"],
                 tag="wcast")

    def _slab_src(kind, fc):
        if kind == "w1":
            return (w1t_d[fc].rearrange("(c p) j -> p c j", p=128),
                    wsl_f.ap().rearrange("p (c j) -> p c j", c=8))
        if kind == "w2":
            return (w2t_d[fc * 128:(fc + 1) * 128, :], wsl_f[:, :])
        return (wo_d[fc * 128:(fc + 1) * 128, :], wsl_f[:, :])

    sum_jobs = [("w1", fc) for fc in range(32)] + [("w2", fc) for fc in range(32)]
    sum_i = [0]

    def plan_sums(n):
        for _ in range(n):
            if sum_i[0] >= len(sum_jobs):
                return
            kind, fc = sum_jobs[sum_i[0]]
            sum_i[0] += 1
            srcd, ovw = _slab_src(kind, fc)
            acc = acc_w1 if kind == "w1" else acc_w2
            ak = "acc_" + kind
            p.dma("qsp", lambda e, srcd=srcd, ovw=ovw: e.dma_start(out=ovw, in_=srcd),
                  reads=["wabs", "wslq"], writes=["wslf"], tag="sma")
            p.op("act", lambda e: e.activation(out=wslq[:, :], in_=wsl_f[:, :],
                                               func=AF.Abs, accum_out=asum[:, :]),
                 reads=["wslf", "wslq"], writes=["wabs", "asum", "wslq"], tag="smb")
            p.op("dve", lambda e, acc=acc: e.tensor_add(out=acc[:, :],
                                                        in0=acc[:, :],
                                                        in1=asum[:, :]),
                 reads=[ak, "asum"], writes=[ak, "wslf"], tag="smc")

    def plan_wscale():
        for kind, acc, sw, mc, ps_off in [("w1", acc_w1, sw_w1, mc_w1, 0),
                                          ("w2", acc_w2, sw_w2, mc_w2, 1)]:
            ak, swk, mck = "acc_" + kind, "sw_" + kind, "mc_" + kind
            p.op("pe", lambda e, acc=acc, ps_off=ps_off: e.matmul(
                psAux[:, ps_off:ps_off + 1], onesf[:, :], acc[:, :],
                start=True, stop=True),
                reads=[ak, "onesf"], writes=["psA:0"], tag="wsc")
            p.op("dve", lambda e, mc=mc, ps_off=ps_off: e.tensor_scalar(
                out=mc[:, :], in0=psAux[:, ps_off:ps_off + 1],
                scalar1=1.0 / (4096.0 * 1024.0), scalar2=1e-5,
                op0=OP.mult, op1=OP.max),
                reads=["psA:0"], writes=[mck], tag="wsc")
            p.op("dve", lambda e, sw=sw, mc=mc: e.reciprocal(out=sw[:, :],
                                                             in_=mc[:, :]),
                 reads=[mck], writes=[swk], tag="wsc")

    slab_jobs = ([("w1", fc) for fc in range(32)] +
                 [("w2", fc) for fc in range(32)] +
                 [("wo", fc) for fc in range(8)])
    slab_i = [0]

    def plan_slabs(n):
        for _ in range(n):
            if slab_i[0] >= len(slab_jobs):
                return
            kind, fc = slab_jobs[slab_i[0]]
            slab_i[0] += 1
            srcd, ovw = _slab_src(kind, fc)
            if kind == "w1":
                dst = w1t16_d[fc]
            elif kind == "w2":
                dst = w2t16_d[fc * 128:(fc + 1) * 128, :]
            else:
                dst = wo16_d[fc * 128:(fc + 1) * 128, :]
            p.dma("qsp", lambda e, srcd=srcd, ovw=ovw: e.dma_start(out=ovw, in_=srcd),
                  reads=["wslb", "wslq"], writes=["wslf"], tag="slf")
            if kind == "wo":
                p.op("pool", lambda e: e.tensor_copy(out=wsl_b[:, :],
                                                     in_=wsl_f[:, :]),
                     reads=["wslf"], writes=["wslc", "wslq"], tag="slc")
            else:
                sw = sw_w1 if kind == "w1" else sw_w2
                mc = mc_w1 if kind == "w1" else mc_w2
                swk, mck = "sw_" + kind, "mc_" + kind
                p.op("act", lambda e, sw=sw: e.activation(
                    out=wslq[:, :], in_=wsl_f[:, :], func=AF.Identity,
                    scale=sw[:, :]),
                    reads=["wslf", swk, "wslq2"], writes=["wslq"], tag="slq")
                p.op("dve", lambda e: e.tensor_copy(
                    out=wslq2.ap().bitcast(I32), in_=wslq[:, :]),
                    reads=["wslq"], writes=["wslq2"], tag="slq")
                p.op("dve", lambda e: e.tensor_scalar(
                    out=wslq[:, :], in0=wslq2.ap().bitcast(I32),
                    scalar1=1.0, scalar2=-1.0, op0=OP.min, op1=OP.max),
                    reads=["wslq2"], writes=["wslq", "wslf"], tag="slq")
                p.op("act", lambda e, mc=mc: e.activation(
                    out=wsl_b[:, :], in_=wslq[:, :], func=AF.Identity,
                    scale=mc[:, :]),
                    reads=["wslq", mck], writes=["wslc"], tag="slq")
            p.dma("qpool", lambda e, dst=dst: e.dma_start(out=dst, in_=wsl_b[:, :]),
                  reads=["wslc"], writes=["wslb", f"slab:{kind}:{fc}"], tag="slw")

    s0.close()
    p.fence()
    xngfm = sb(s1, "xngfm0", [128, 1024], BF16)
    arf2 = [sb(s1, f"arf{i}", [128, D], F32) for i in range(2)]
    aif2 = [sb(s1, f"aif{i}", [128, D], F32) for i in range(2)]
    fC = sb(s1, "fC", [128, D], F32)
    fD = sb(s1, "fD", [128, D], F32)
    fE = sb(s1, "fE", [128, D], F32)
    q2 = [sb(s1, f"q2{i}", [128, D], BF16) for i in range(2)]
    g2 = [sb(s1, f"g2{i}", [128, D], BF16) for i in range(2)]
    u2 = [sb(s1, f"u2{i}", [128, D], BF16) for i in range(2)]
    bb = {j: sb(s1, f"bb{j}", [128, D], BF16) for j in (0, 1, 2, 4, 5, 6, 7, 9)}

    def fe(s):
        i = s % 2
        col = lambda a: a[:, s:s + 1]
        xsl = xngfm
        xk = "xngfm:0"
        p.dma("qsp", lambda e: e.dma_start(out=xs[:, :],
                                           in_=x_d[s * 128:(s + 1) * 128, :]),
              reads=["xs"], writes=["xs"], tag=f"x{s}")
        p.op("dve", lambda e: e.bn_stats(out=bn6[:, 0:6], in_=xs[:, 0:512]),
             reads=["xs"], writes=["bn6"])
        p.op("dve", lambda e: e.bn_stats(out=bn6[:, 6:12], in_=xs[:, 512:1024]),
             reads=["xs"], writes=["bn6"])
        p.op("dve", lambda e: e.bn_aggr(out=bn2[:, :], in_=bn6[:, :]),
             reads=["bn6"], writes=["bn2"])
        p.op("act", lambda e: e.activation(out=tA[:, :], in_=bn2[:, 1:2],
                                           func=AF.Ln, bias=cEPS[:, :]),
             reads=["bn2", "cEPS"], writes=["tA"])
        p.op("act", lambda e: e.activation(out=col(istd_a), in_=tA[:, :],
                                           func=AF.Exp, scale=-0.5),
             reads=["tA"], writes=[f"istd:{s}"])
        p.op("dve", lambda e: e.tensor_mul(out=tB[:, :], in0=bn2[:, 0:1],
                                           in1=col(istd_a)),
             reads=["bn2", f"istd:{s}"], writes=["tB"])
        p.op("dve", lambda e: e.tensor_scalar(out=col(nmu_a), in0=tB[:, :],
                                              scalar1=-1.0, scalar2=None,
                                              op0=OP.mult),
             reads=["tB"], writes=[f"nmu:{s}"])
        p.op("dve", lambda e: e.tensor_mul(out=tB[:, :], in0=bn2[:, 1:2],
                                           in1=col(istd_a)),
             reads=["bn2", f"istd:{s}"], writes=["tB"])
        p.op("dve", lambda e: e.tensor_mul(out=tB[:, :], in0=tB[:, :],
                                           in1=col(istd_a)),
             reads=["tB", f"istd:{s}"], writes=["tB"])
        p.op("act", lambda e: e.activation(out=tA[:, :], in_=tB[:, :],
                                           func=AF.Ln, scale=float(D),
                                           bias=cEPS[:, :]),
             reads=["tB", "cEPS"], writes=["tA"])
        p.op("act", lambda e: e.activation(out=tA[:, :], in_=tA[:, :],
                                           func=AF.Exp, scale=-0.5),
             reads=["tA"], writes=["tA"])
        p.op("dve", lambda e: e.tensor_scalar(out=col(scl_a), in0=tA[:, :],
                                              scalar1=32.0, scalar2=None,
                                              op0=OP.mult),
             reads=["tA"], writes=[f"scl:{s}"])
        p.op("act", lambda e: e.activation(out=tC[:, :], in_=tB[:, :],
                                           func=AF.Ln, bias=cEPS[:, :]),
             reads=["tB", "cEPS"], writes=["tC"])
        p.op("act", lambda e: e.activation(out=tC[:, :], in_=tC[:, :],
                                           func=AF.Exp, scale=-0.5),
             reads=["tC"], writes=["tC"])
        p.op("act", lambda e: e.activation(out=xs[:, :], in_=xs[:, :],
                                           func=AF.Identity, scale=col(istd_a),
                                           bias=col(nmu_a)),
             reads=["xs", f"istd:{s}", f"nmu:{s}"], writes=["xs"])
        p.op("dve", lambda e: e.tensor_reduce(out=tD_[:, :], in_=xs[:, :],
                                              axis=AX, op=OP.max,
                                              apply_absolute_value=True),
             reads=["xs"], writes=["tD"])
        p.op("dve", lambda e: e.tensor_mul(out=tB[:, :], in0=tD_[:, :], in1=tC[:, :]),
             reads=["tD", "tC"], writes=["tB"])
        p.op("dve", lambda e: e.tensor_scalar(out=tB[:, :], in0=tB[:, :],
                                              scalar1=1e-5, scalar2=None, op0=OP.max),
             reads=["tB"], writes=["tB"])
        p.op("dve", lambda e: e.tensor_scalar(out=col(invs_a), in0=tB[:, :],
                                              scalar1=1.0 / 127.0, scalar2=None,
                                              op0=OP.mult),
             reads=["tB"], writes=[f"invs:{s}"])
        p.op("dve", lambda e: e.reciprocal(out=tB[:, :], in_=tB[:, :]),
             reads=["tB"], writes=["tB"])
        p.op("dve", lambda e: e.tensor_mul(out=tB[:, :], in0=tB[:, :], in1=tC[:, :]),
             reads=["tB", "tC"], writes=["tB"])
        p.op("dve", lambda e: e.tensor_scalar(out=col(ct_a), in0=tB[:, :],
                                              scalar1=127.0, scalar2=None,
                                              op0=OP.mult),
             reads=["tB"], writes=[f"ct:{s}"])
        p.op("dve", lambda e: e.tensor_mul(out=xng_tm[:, :], in0=xs[:, :],
                                           in1=gr[:, :]),
             reads=["xs", "gr"], writes=["xng_tm"])
        for dc in range(8):
            p.op("pe", lambda e, dc=dc: e.transpose(
                psTrB[:, dc * 128:(dc + 1) * 128],
                xng_tm[:, dc * 128:(dc + 1) * 128], ident[:, :]),
                reads=["xng_tm", "ident"], writes=["psTrB"], tag=f"tr{s}")
        p.op("act", lambda e: e.activation(out=xsl[:, :], in_=psTrB[:, :],
                                           func=AF.Copy),
             reads=["psTrB"], writes=[xk], tag=f"tre{s}")

        def mm_pass(wos, tagp):
            for dc in range(8):
                for r_, wo_ in enumerate(wos):
                    p.op("pe", lambda e, dc=dc, r_=r_, wo_=wo_:
                         e.matmul(psMain[:, r_ * 512:(r_ + 1) * 512],
                                  xsl[:, dc * 128:(dc + 1) * 128],
                                  wqkv16[:, dc * 6144 + wo_: dc * 6144 + wo_ + 512],
                                  start=(dc == 0), stop=(dc == 7)),
                         reads=[xk, f"wq16:{dc}"], writes=[f"psM:{r_}"], tag=tagp)

        def evac(out_ap, lo, hi, wr):
            rk = [f"psM:{r_}" for r_ in range(lo // 512, (hi + 511) // 512)]
            p.op("act", lambda e: e.activation(out=out_ap, in_=psMain[:, lo:hi],
                                               func=AF.Identity, scale=col(scl_a)),
                 reads=rk + [f"scl:{s}"], writes=wr)

        mm_pass([0, 512, 1024], f"p1m{s}")
        evac(q2[i][:, :], 0, 1024, [f"q2:{i}"])
        evac(u2[i][:, 0:512], 1024, 1536, [f"u2:{i}"])
        mm_pass([1536, 2048, 2560], f"p2m{s}")
        evac(u2[i][:, 512:1024], 0, 512, [f"u2:{i}"])
        evac(xng_tm[:, :], 512, 1536, ["xng_tm"])
        p.op("pool", lambda e: e.tensor_mul(out=u2[i][:, :], in0=u2[i][:, :],
                                            in1=xng_tm[:, :]),
             reads=[f"u2:{i}", "xng_tm"], writes=[f"u2:{i}"])
        mm_pass([3072, 3584, 4096], f"p3m{s}")
        evac(g2[i][:, :], 0, 1024, [f"g2:{i}"])
        evac(arf2[i][:, 0:512], 1024, 1536, [f"arf:{i}"])
        mm_pass([4608, 5120, 5632], f"p4m{s}")
        evac(arf2[i][:, 512:1024], 0, 512, [f"arf:{i}"])
        evac(aif2[i][:, :], 512, 1536, [f"aif:{i}"])

    def be_a(s):
        i = s % 2
        p.op("act", lambda e: e.activation(out=fA[:, :], in_=arf2[i][:, :],
                                           func=AF.Square),
             reads=[f"arf:{i}"], writes=["fA"])
        p.op("act", lambda e: e.activation(out=fB[:, :], in_=aif2[i][:, :],
                                           func=AF.Square),
             reads=[f"aif:{i}"], writes=["fB"])
        p.op("pool", lambda e: e.tensor_add(out=fC[:, :], in0=fA[:, :],
                                            in1=fB[:, :]),
             reads=["fA", "fB"], writes=["fC"])
        p.op("act", lambda e: e.activation(out=fC[:, :], in_=fC[:, :],
                                           func=AF.Ln, bias=c38[:, :]),
             reads=["fC", "c38"], writes=["fC"])
        p.op("act", lambda e: e.activation(out=fC[:, :], in_=fC[:, :],
                                           func=AF.Exp, scale=0.5),
             reads=["fC"], writes=["fC"])
        p.op("dve", lambda e: e.tensor_add(out=fD[:, :], in0=fC[:, :],
                                           in1=arf2[i][:, :]),
             reads=["fC", f"arf:{i}"], writes=["fD"])
        p.op("act", lambda e: e.activation(out=fA[:, :], in_=fD[:, :],
                                           func=AF.Square),
             reads=["fD"], writes=["fA"])
        p.op("pool", lambda e: e.tensor_add(out=fA[:, :], in0=fA[:, :],
                                            in1=fB[:, :]),
             reads=["fA", "fB"], writes=["fA"])
        p.op("act", lambda e: e.activation(out=fA[:, :], in_=fA[:, :],
                                           func=AF.Ln, bias=c38[:, :]),
             reads=["fA", "c38"], writes=["fA"])
        p.op("act", lambda e: e.activation(out=fA[:, :], in_=fA[:, :],
                                           func=AF.Exp, scale=0.5),
             reads=["fA"], writes=["fA"])
        p.op("dve", lambda e: e.scalar_tensor_tensor(out=fD[:, :], in0=fA[:, :],
                                                     scalar=1e-12, in1=fD[:, :],
                                                     op0=OP.add, op1=OP.add),
             reads=["fA", "fD"], writes=["fD"])
        p.op("act", lambda e: e.activation(out=fE[:, :], in_=fC[:, :],
                                           func=AF.Exp, scale=-1.0),
             reads=["fC"], writes=["fE"])
        p.op("act", lambda e: e.activation(out=bb[4][:, :], in_=fE[:, :],
                                           func=AF.Ln, bias=1.0),
             reads=["fE"], writes=["b4"])
        p.op("act", lambda e: e.activation(out=fE[:, :], in_=fD[:, :], func=AF.Ln),
             reads=["fD"], writes=["fE"])
        p.op("act", lambda e: e.activation(out=fE[:, :], in_=fE[:, :],
                                           func=AF.Exp, scale=-1.0),
             reads=["fE"], writes=["fE"])
        p.op("dve", lambda e: e.tensor_mul(out=fC[:, :], in0=aif2[i][:, :],
                                           in1=fE[:, :]),
             reads=[f"aif:{i}", "fE"], writes=["fC"])
        p.op("dve", lambda e: e.tensor_copy(out=fD[:, :], in_=g2[i][:, :]),
             reads=[f"g2:{i}"], writes=["fD"])
        p.op("pool", lambda e: e.tensor_mul(out=bb[0][:, :], in0=q2[i][:, :],
                                            in1=g2[i][:, :]),
             reads=[f"q2:{i}", f"g2:{i}"], writes=["b0"])
        for h in range(2):
            p.op("pe", lambda e, h=h: e.matmul(psCum[:, h * 512:(h + 1) * 512],
                                               tri_m1[:, :],
                                               bb[4][:, h * 512:(h + 1) * 512],
                                               start=True, stop=True),
                 reads=["tri_m1", "b4"], writes=[f"psC:{h}"], tag=f"cs1{s}")
        for cc in range(8):
            p.op("pe", lambda e, cc=cc: e.matmul(
                psAux[:, cc * 2:cc * 2 + 2], bb[4][:, cc * 128:(cc + 1) * 128],
                cind[:, :], start=True, stop=True),
                reads=["b4", "cind"], writes=["psA:0"], tag=f"ag{s}")
        p.op("act", lambda e: e.activation(out=bb[7][:, :], in_=psCum[:, :],
                                           func=AF.Exp),
             reads=["psC:0", "psC:1"], writes=["b7"])
        p.op("act", lambda e: e.activation(out=fB[:, :], in_=psCum[:, :],
                                           func=AF.Exp, scale=-1.0),
             reads=["psC:0", "psC:1"], writes=["fB"])
        p.op("act", lambda e: e.activation(
            out=magt.ap().rearrange("p (k cc) -> p k cc", k=2),
            in_=psAux[:, 0:16].rearrange("p (cc k) -> p k cc", cc=8),
            func=AF.Exp, scale=-1.0),
            reads=["psA:0"], writes=["magt"])
        p.op("dve", lambda e: e.tensor_mul(out=fB[:, :], in0=fB[:, :],
                                           in1=u2[i][:, :]),
             reads=[f"u2:{i}", "fB"], writes=["fB"])

    def be_b(s):
        p.op("act", lambda e: e.activation(out=fC[:, :], in_=fC[:, :],
                                           func=AF.Arctan),
             reads=["fC"], writes=["fC"])
        p.op("act", lambda e: e.activation(out=fD[:, :], in_=fD[:, :],
                                           func=AF.Sigmoid),
             reads=["fD"], writes=["fD"])
        p.op("dve", lambda e: e.tensor_mul(out=bb[0][:, :], in0=bb[0][:, :],
                                           in1=fD[:, :]),
             reads=["b0", "fD"], writes=["b0"])
        p.op("dve", lambda e: e.tensor_copy(out=bb[5][:, :], in_=fC[:, :]),
             reads=["fC"], writes=["b5"])
        p.op("dve", lambda e: e.scalar_tensor_tensor(out=bb[6][:, :],
                                                     in0=bb[5][:, :], scalar=-1.0,
                                                     in1=fC[:, :], op0=OP.mult,
                                                     op1=OP.add),
             reads=["b5", "fC"], writes=["b6"])
        for cc in range(8):
            p.op("pe", lambda e, cc=cc: e.matmul(
                psAux[:, 16 + cc * 2:16 + cc * 2 + 2],
                bb[5][:, cc * 128:(cc + 1) * 128],
                cind[:, :], start=True, stop=False),
                reads=["b5", "cind"], writes=["psA:0"], tag=f"agt{s}")
            p.op("pe", lambda e, cc=cc: e.matmul(
                psAux[:, 16 + cc * 2:16 + cc * 2 + 2],
                bb[6][:, cc * 128:(cc + 1) * 128],
                cind[:, :], start=False, stop=True),
                reads=["b6", "cind"], writes=["psA:0"], tag=f"agt{s}")
        p.op("act", lambda e: e.activation(
            out=aggS[:, 16:32].rearrange("p (k cc) -> p k cc", k=2),
            in_=psAux[:, 16:32].rearrange("p (cc k) -> p k cc", cc=8),
            func=AF.Copy),
            reads=["psA:0"], writes=["aggS"])
        for h in range(2):
            p.op("pe", lambda e, h=h: e.matmul(psCum[:, h * 512:(h + 1) * 512],
                                               tri_p4[:, :],
                                               bb[5][:, h * 512:(h + 1) * 512],
                                               start=True, stop=False),
                 reads=["tri_p4", "b5"], writes=[f"psC:{h}"], tag=f"cs2{s}")
            p.op("pe", lambda e, h=h: e.matmul(psCum[:, h * 512:(h + 1) * 512],
                                               tri_p4[:, :],
                                               bb[6][:, h * 512:(h + 1) * 512],
                                               start=False, stop=True),
                 reads=["tri_p4", "b6"], writes=[f"psC:{h}"], tag=f"cs2{s}")
        p.op("dve", lambda e: e.tensor_scalar(out=fA.ap().bitcast(I32),
                                              in0=psCum[:, :],
                                              scalar1=1.0 / TWO_PI, scalar2=None,
                                              op0=OP.mult),
             reads=["psC:0", "psC:1"], writes=["fA"])
        p.op("dve", lambda e: e.scalar_tensor_tensor(out=fD[:, :],
                                                     in0=fA.ap().bitcast(I32),
                                                     scalar=-TWO_PI,
                                                     in1=psCum[:, :],
                                                     op0=OP.mult, op1=OP.add),
             reads=["fA", "psC:0", "psC:1"], writes=["fD"])
        p.op("dve", lambda e: e.tensor_scalar(out=fD[:, :], in0=fD[:, :],
                                              scalar1=6.2821, scalar2=-6.2821,
                                              op0=OP.min, op1=OP.max),
             reads=["fD"], writes=["fD"])
        p.op("dve", lambda e: e.tensor_scalar(out=tki[:, :], in0=aggS[:, 16:32],
                                              scalar1=4.0 / TWO_PI, scalar2=None,
                                              op0=OP.mult),
             reads=["aggS"], writes=["tki"])
        p.op("dve", lambda e: e.tensor_scalar(out=tt1[:, :], in0=aggS[:, 16:32],
                                              scalar1=4.0, scalar2=None,
                                              op0=OP.mult),
             reads=["aggS"], writes=["tt1"])
        p.op("dve", lambda e: e.scalar_tensor_tensor(out=tt1[:, :], in0=tki[:, :],
                                                     scalar=-TWO_PI, in1=tt1[:, :],
                                                     op0=OP.mult, op1=OP.add),
             reads=["tki", "tt1"], writes=["tt1"])
        p.op("dve", lambda e: e.tensor_scalar(out=tt1[:, :], in0=tt1[:, :],
                                              scalar1=6.2821, scalar2=-6.2821,
                                              op0=OP.min, op1=OP.max),
             reads=["tt1"], writes=["tt1"])
        p.op("act", lambda e: e.activation(out=fC[:, :], in_=fD[:, :],
                                           func=AF.Sin, scale=0.25),
             reads=["fD"], writes=["fC"])
        p.op("act", lambda e: e.activation(out=fE[:, :], in_=fD[:, :],
                                           func=AF.Sin, scale=0.25, bias=cPIH[:, :]),
             reads=["fD", "cPIH"], writes=["fE"])
        p.op("act", lambda e: e.activation(out=tt2[:, :], in_=tt1[:, :],
                                           func=AF.Sin, scale=0.25),
             reads=["tt1"], writes=["tt2"])
        p.op("act", lambda e: e.activation(out=tt3[:, :], in_=tt1[:, :],
                                           func=AF.Sin, scale=0.25, bias=cPIH[:, :]),
             reads=["tt1", "cPIH"], writes=["tt3"])
        p.op("act", lambda e: e.activation(out=fA[:, :], in_=fC[:, :],
                                           func=AF.Square),
             reads=["fC"], writes=["fA"])
        p.op("dve", lambda e: e.tensor_mul(out=fC[:, :], in0=fC[:, :],
                                           in1=fE[:, :]),
             reads=["fC", "fE"], writes=["fC"])
        p.op("dve", lambda e: e.tensor_scalar(out=fA[:, :], in0=fA[:, :],
                                              scalar1=-2.0, scalar2=1.0,
                                              op0=OP.mult, op1=OP.add),
             reads=["fA"], writes=["fA"])
        p.op("pool", lambda e: e.tensor_mul(out=fE[:, :], in0=fC[:, :],
                                            in1=fA[:, :]),
             reads=["fC", "fA"], writes=["fE"])
        p.op("dve", lambda e: e.tensor_scalar(out=bb[5][:, :], in0=fE[:, :],
                                              scalar1=4.0, scalar2=None,
                                              op0=OP.mult),
             reads=["fE"], writes=["b5"])
        p.op("act", lambda e: e.activation(out=fA[:, :], in_=fC[:, :],
                                           func=AF.Square),
             reads=["fC"], writes=["fA"])
        p.op("dve", lambda e: e.tensor_scalar(out=bb[6][:, :], in0=fA[:, :],
                                              scalar1=-8.0, scalar2=1.0,
                                              op0=OP.mult, op1=OP.add),
             reads=["fA"], writes=["b6"])
        p.op("act", lambda e: e.activation(out=tt4[:, :], in_=tt2[:, :],
                                           func=AF.Square),
             reads=["tt2"], writes=["tt4"])
        p.op("dve", lambda e: e.tensor_mul(out=tt2[:, :], in0=tt2[:, :],
                                           in1=tt3[:, :]),
             reads=["tt2", "tt3"], writes=["tt2"])
        p.op("dve", lambda e: e.tensor_scalar(out=tt3[:, :], in0=tt4[:, :],
                                              scalar1=-2.0, scalar2=1.0,
                                              op0=OP.mult, op1=OP.add),
             reads=["tt4"], writes=["tt3"])
        p.op("dve", lambda e: e.tensor_mul(out=tt1[:, :], in0=tt2[:, :],
                                           in1=tt3[:, :]),
             reads=["tt2", "tt3"], writes=["tt1"])
        p.op("dve", lambda e: e.tensor_scalar(out=tt1[:, :], in0=tt1[:, :],
                                              scalar1=4.0, scalar2=None,
                                              op0=OP.mult),
             reads=["tt1"], writes=["tt1"])
        p.op("act", lambda e: e.activation(out=tt4[:, :], in_=tt2[:, :],
                                           func=AF.Square),
             reads=["tt2"], writes=["tt4"])
        p.op("dve", lambda e: e.tensor_scalar(out=tt4[:, :], in0=tt4[:, :],
                                              scalar1=-8.0, scalar2=1.0,
                                              op0=OP.mult, op1=OP.add),
             reads=["tt4"], writes=["tt4"])
        p.op("dve", lambda e: e.tensor_mul(out=bb[4][:, :], in0=fB[:, :],
                                           in1=bb[6][:, :]),
             reads=["fB", "b6"], writes=["b4"])
        p.op("dve", lambda e: e.tensor_mul(out=bb[9][:, :], in0=fB[:, :],
                                           in1=bb[5][:, :]),
             reads=["fB", "b5"], writes=["b9"])
        p.op("pool", lambda e: e.tensor_mul(out=bb[1][:, :], in0=bb[7][:, :],
                                            in1=bb[6][:, :]),
             reads=["b7", "b6"], writes=["b1"])
        p.op("pool", lambda e: e.tensor_mul(out=bb[2][:, :], in0=bb[7][:, :],
                                            in1=bb[5][:, :]),
             reads=["b7", "b5"], writes=["b2"])
        for cc in range(8):
            p.op("pe", lambda e, cc=cc: e.matmul(
                psAux[:, 32 + cc * 2:32 + cc * 2 + 2],
                bb[4][:, cc * 128:(cc + 1) * 128], cind[:, :],
                start=True, stop=True),
                reads=["b4", "cind"], writes=["psA:0"], tag=f"ag2{s}")
            p.op("pe", lambda e, cc=cc: e.matmul(
                psAux[:, 48 + cc * 2:48 + cc * 2 + 2],
                bb[9][:, cc * 128:(cc + 1) * 128], cind[:, :],
                start=True, stop=True),
                reads=["b9", "cind"], writes=["psA:0"], tag=f"ag2{s}")
        p.op("act", lambda e: e.activation(
            out=aggS[:, 32:64].rearrange("p (g k cc) -> p g k cc", g=2, k=2),
            in_=psAux[:, 32:64].rearrange("p (g cc k) -> p g k cc", g=2, cc=8),
            func=AF.Copy),
            reads=["psA:0"], writes=["aggS"])
        for h in range(2):
            p.op("pe", lambda e, h=h: e.matmul(psCum[:, h * 512:(h + 1) * 512],
                                               tri_p1[:, :],
                                               bb[4][:, h * 512:(h + 1) * 512],
                                               start=True, stop=True),
                 reads=["tri_p1", "b4"], writes=[f"psC:{h}"], tag=f"cs3{s}")
            p.op("pe", lambda e, h=h: e.matmul(psAux[:, h * 512:(h + 1) * 512],
                                               tri_m1[:, :],
                                               bb[9][:, h * 512:(h + 1) * 512],
                                               start=True, stop=True),
                 reads=["tri_m1", "b9"],
                 writes=["psA:0"] if h == 0 else ["psA:1"], tag=f"cs3{s}")
        p.op("dve", lambda e: e.tensor_mul(out=fA[:, :], in0=bb[1][:, :],
                                           in1=psCum[:, :]),
             reads=["b1", "psC:0", "psC:1"], writes=["fA"])
        p.op("dve", lambda e: e.tensor_mul(out=fB[:, :], in0=bb[2][:, :],
                                           in1=psAux[:, 0:1024]),
             reads=["b2", "psA:0", "psA:1"], writes=["fB"])
        p.op("dve", lambda e: e.tensor_sub(out=fA[:, :], in0=fA[:, :],
                                           in1=fB[:, :]),
             reads=["fA", "fB"], writes=["fA"])
        p.op("pool", lambda e: e.tensor_mul(out=bb[5][:, :], in0=bb[0][:, :],
                                            in1=bb[1][:, :]),
             reads=["b0", "b1"], writes=["b5"])
        p.op("pool", lambda e: e.tensor_mul(out=bb[7][:, :], in0=bb[0][:, :],
                                            in1=bb[2][:, :]),
             reads=["b0", "b2"], writes=["b7"])
        p.op("dve", lambda e: e.tensor_mul(out=bb[0][:, :], in0=bb[0][:, :],
                                           in1=fA[:, :]),
             reads=["b0", "fA"], writes=["b0"])
        p.dma("qpool", lambda e: e.dma_start(out=ylocD[s], in_=bb[0][:, :]),
              reads=["b0"], writes=[f"ylD:{s}"], tag=f"sp{s}")
        p.dma("qpool", lambda e: e.dma_start(out=a1D[s], in_=bb[5][:, :]),
              reads=["b5"], writes=[f"a1D:{s}"], tag=f"sp{s}")
        p.dma("qpool", lambda e: e.dma_start(out=a2D[s], in_=bb[7][:, :]),
              reads=["b7"], writes=[f"a2D:{s}"], tag=f"sp{s}")
        p.op("dve", lambda e: e.tensor_mul(out=acr_t[:, :], in0=magt[:, :],
                                           in1=tt4[:, :]),
             reads=["magt", "tt4"], writes=["acr_t"])
        p.op("dve", lambda e: e.tensor_mul(out=aci_t[:, :], in0=magt[:, :],
                                           in1=tt1[:, :]),
             reads=["magt", "tt1"], writes=["aci_t"])
        p.op("dve", lambda e: e.tensor_mul(out=slr_t[:, :], in0=acr_t[:, :],
                                           in1=aggS[:, 32:48]),
             reads=["acr_t", "aggS"], writes=["slr_t"])
        p.op("dve", lambda e: e.tensor_mul(out=tt1[:, :], in0=aci_t[:, :],
                                           in1=aggS[:, 48:64]),
             reads=["aci_t", "aggS"], writes=["tt1"])
        p.op("dve", lambda e: e.tensor_add(out=slr_t[:, :], in0=slr_t[:, :],
                                           in1=tt1[:, :]),
             reads=["slr_t", "tt1"], writes=["slr_t"])
        p.op("dve", lambda e: e.tensor_mul(out=sli_t[:, :], in0=aci_t[:, :],
                                           in1=aggS[:, 32:48]),
             reads=["aci_t", "aggS"], writes=["sli_t"])
        p.op("dve", lambda e: e.tensor_mul(out=tt1[:, :], in0=acr_t[:, :],
                                           in1=aggS[:, 48:64]),
             reads=["acr_t", "aggS"], writes=["tt1"])
        p.op("dve", lambda e: e.tensor_sub(out=sli_t[:, :], in0=sli_t[:, :],
                                           in1=tt1[:, :]),
             reads=["sli_t", "tt1"], writes=["sli_t"])
        for k in range(2):
            c = 2 * s + k
            lo, hi = k * 8, (k + 1) * 8
            co, co1 = c * 8, (c + 1) * 8

            def cop(fn, out_ap, a_ap, b_ap, rd, wr):
                p.op("dve",
                     lambda e, fn=fn, out_ap=out_ap, a_ap=a_ap, b_ap=b_ap:
                     getattr(e, fn)(out=out_ap, in0=a_ap, in1=b_ap),
                     reads=rd, writes=wr)

            cop("tensor_mul", ch1[:, :], acr_t[:, lo:hi], hP_r[:, co:co + 8],
                ["acr_t", f"hP:{c}"], ["ch1"])
            cop("tensor_mul", ch2[:, :], aci_t[:, lo:hi], hP_i[:, co:co + 8],
                ["aci_t", f"hPi:{c}"], ["ch2"])
            cop("tensor_sub", ch1[:, :], ch1[:, :], ch2[:, :],
                ["ch1", "ch2"], ["ch1"])
            cop("tensor_add", hP_r[:, co1:co1 + 8], ch1[:, :], slr_t[:, lo:hi],
                ["ch1", "slr_t"], [f"hP:{c + 1}"])
            cop("tensor_mul", ch1[:, :], acr_t[:, lo:hi], hP_i[:, co:co + 8],
                ["acr_t", f"hPi:{c}"], ["ch1"])
            cop("tensor_mul", ch2[:, :], aci_t[:, lo:hi], hP_r[:, co:co + 8],
                ["aci_t", f"hP:{c}"], ["ch2"])
            cop("tensor_add", ch1[:, :], ch1[:, :], ch2[:, :],
                ["ch1", "ch2"], ["ch1"])
            cop("tensor_add", hP_i[:, co1:co1 + 8], ch1[:, :], sli_t[:, lo:hi],
                ["ch1", "sli_t"], [f"hPi:{c + 1}"])
            if c + 1 < NCH:
                cop("tensor_mul", ch1[:, :], acr_t[:, lo:hi], PP_r[:, co:co + 8],
                    ["acr_t", f"PP:{c}"], ["ch1"])
                cop("tensor_mul", ch2[:, :], aci_t[:, lo:hi], PP_i[:, co:co + 8],
                    ["aci_t", f"PPi:{c}"], ["ch2"])
                cop("tensor_sub", PP_r[:, co1:co1 + 8], ch1[:, :], ch2[:, :],
                    ["ch1", "ch2"], [f"PP:{c + 1}"])
                cop("tensor_mul", ch1[:, :], acr_t[:, lo:hi], PP_i[:, co:co + 8],
                    ["acr_t", f"PPi:{c}"], ["ch1"])
                cop("tensor_mul", ch2[:, :], aci_t[:, lo:hi], PP_r[:, co:co + 8],
                    ["aci_t", f"PP:{c}"], ["ch2"])
                cop("tensor_add", PP_i[:, co1:co1 + 8], ch1[:, :], ch2[:, :],
                    ["ch1", "ch2"], [f"PPi:{c + 1}"])

    fe(0)
    fe(1)
    be_a(0)
    for s in range(NSL):
        if s + 2 < NSL:
            fe(s + 2)
        be_b(s)
        if s + 1 < NSL:
            be_a(s + 1)
        if s < 8:
            plan_sums(8)
            if s == 7:
                plan_sums(100)
                plan_wscale()
        else:
            plan_slabs(9)
    plan_slabs(100)

    p.op("dve", lambda e: e.tensor_copy(out=Hpack[:, 0:8],
                                        in_=hP_r[:, NCH * 8:(NCH + 1) * 8]),
         reads=[f"hP:{NCH}"], writes=["Hpack"])
    p.op("dve", lambda e: e.tensor_copy(out=Hpack[:, 8:16],
                                        in_=hP_i[:, NCH * 8:(NCH + 1) * 8]),
         reads=[f"hPi:{NCH}"], writes=["Hpack"])
    p.dma("qpool", lambda e: e.dma_start(out=ccin_d, in_=Hpack[:, :]),
          reads=["Hpack"], writes=["ccin"], tag="cc")
    p.op("pool", lambda e: e.collective_compute(
        "AllGather", OP.bypass, ins=[ccin_d], outs=[ccout_d],
        replica_groups=[[0, 1, 2, 3, 4, 5, 6, 7]]),
        reads=["ccin"], writes=["ccout"], tag="cc")
    p.dma("qpool", lambda e: e.dma_start(
        out=gatc.ap().rearrange("p (r j) -> p r j", r=8),
        in_=ccout_d.rearrange("r (p j) -> p r j", p=128)),
        reads=["ccout"], writes=["gatc"], tag="cc")
    p.op("dve", lambda e: e.tensor_mul(out=gatc[:, :], in0=gatc[:, :],
                                       in1=selrep[:, :]),
         reads=["gatc", "selrep"], writes=["gatc"])
    p.op("dve", lambda e: e.tensor_copy(out=Hprev[:, :], in_=gatc[:, 0:16]),
         reads=["gatc"], writes=["Hprev"])
    for r_ in range(1, 8):
        p.op("dve", lambda e, r_=r_: e.tensor_add(
            out=Hprev[:, :], in0=Hprev[:, :], in1=gatc[:, r_ * 16:(r_ + 1) * 16]),
            reads=["gatc", "Hprev"], writes=["Hprev"])
    for c in range(NCH):
        p.op("dve", lambda e, c=c: e.tensor_copy(out=HrR[:, c * 8:(c + 1) * 8],
                                                 in_=Hprev[:, 0:8]),
             reads=["Hprev"], writes=["HrR"])
        p.op("dve", lambda e, c=c: e.tensor_copy(out=HiR[:, c * 8:(c + 1) * 8],
                                                 in_=Hprev[:, 8:16]),
             reads=["Hprev"], writes=["HiR"])
    hp_reads = ([f"hP:{c}" for c in range(NCH)] +
                [f"hPi:{c}" for c in range(NCH)] +
                [f"PP:{c}" for c in range(NCH)] +
                [f"PPi:{c}" for c in range(NCH)])
    httR_o = httR.ap().rearrange("p (cc c) -> p cc c", cc=8).transpose([0, 2, 1])
    httI_o = httI.ap().rearrange("p (cc c) -> p cc c", cc=8).transpose([0, 2, 1])
    fA3 = fA.ap().rearrange("p (c cc) -> p c cc", cc=8)
    hPr3 = hP_r.ap().rearrange("p (c cc) -> p c cc", cc=8)
    hPi3 = hP_i.ap().rearrange("p (c cc) -> p c cc", cc=8)
    p.op("dve", lambda e: e.tensor_mul(out=fA[:, 0:256], in0=PP_r[:, 0:256],
                                       in1=HrR[:, :]),
         reads=hp_reads + ["HrR"], writes=["fA"])
    p.op("dve", lambda e: e.tensor_mul(out=fB[:, 0:256], in0=PP_i[:, 0:256],
                                       in1=HiR[:, :]),
         reads=["HiR"] + hp_reads, writes=["fB"])
    p.op("dve", lambda e: e.tensor_sub(out=fA[:, 0:256], in0=fA[:, 0:256],
                                       in1=fB[:, 0:256]),
         reads=["fA", "fB"], writes=["fA"])
    p.op("dve", lambda e: e.tensor_tensor(out=httR_o, in0=fA3[:, 0:32, :],
                                          in1=hPr3[:, 0:32, :], op=OP.add),
         reads=["fA"] + hp_reads, writes=["httR"])
    p.op("dve", lambda e: e.tensor_mul(out=fA[:, 0:256], in0=PP_r[:, 0:256],
                                       in1=HiR[:, :]),
         reads=hp_reads + ["HiR", "httR"], writes=["fA"])
    p.op("dve", lambda e: e.tensor_mul(out=fB[:, 0:256], in0=PP_i[:, 0:256],
                                       in1=HrR[:, :]),
         reads=["HrR"] + hp_reads, writes=["fB"])
    p.op("dve", lambda e: e.tensor_add(out=fA[:, 0:256], in0=fA[:, 0:256],
                                       in1=fB[:, 0:256]),
         reads=["fA", "fB"], writes=["fA"])
    p.op("dve", lambda e: e.tensor_tensor(out=httI_o, in0=fA3[:, 0:32, :],
                                          in1=hPi3[:, 0:32, :], op=OP.add),
         reads=["fA"] + hp_reads, writes=["httI"])
    for cc in range(8):
        p.op("pe", lambda e, cc=cc: e.transpose(
            psTrB[0:32, 0:128], httR[:, cc * 32:(cc + 1) * 32], ident[:, :]),
            reads=["httR", "ident", "psTrB"], writes=["psTrB"], tag="htr")
        p.op("act", lambda e, cc=cc: e.activation(
            out=hrt2R[0:32, cc * 128:(cc + 1) * 128], in_=psTrB[0:32, 0:128],
            func=AF.Copy),
            reads=["psTrB"], writes=["hrt2R"], tag="htr")
        p.op("pe", lambda e, cc=cc: e.transpose(
            psTrB[0:32, 0:128], httI[:, cc * 32:(cc + 1) * 32], ident[:, :]),
            reads=["httI", "ident", "psTrB"], writes=["psTrB"], tag="htr")
        p.op("act", lambda e, cc=cc: e.activation(
            out=hrt2I[0:32, cc * 128:(cc + 1) * 128], in_=psTrB[0:32, 0:128],
            func=AF.Copy),
            reads=["psTrB"], writes=["hrt2I"], tag="htr")

    s1.close()
    p.fence()
    s2 = contextlib.ExitStack()
    wo16s = sb(s2, "wo16s", [128, 8 * D], BF16)
    b2r16 = sb(s2, "b2r16", [1, D], BF16)
    p.dma("qsp", lambda e: e.dma_start(out=wsl_f[0:1, :], in_=b2r_d),
          reads=["wslf"], writes=["wslf"], tag="b2l")
    p.op("dve", lambda e: e.tensor_copy(out=b2r16[:, :], in_=wsl_f[0:1, :]),
         reads=["wslf"], writes=["b2r16"])
    yattn = sb(s2, "yattn", [128, NSL * D], BF16)
    ylb = sb(s2, "ylb", [128, D], BF16)
    a1b = sb(s2, "a1b", [128, D], BF16)
    a2b = sb(s2, "a2b", [128, D], BF16)
    ypb = sb(s2, "ypb", [128, D], BF16)
    ypfm = sb(s2, "ypfm", [128, D], BF16)
    fCC = sb(s2, "fCC", [128, D], F32)

    for fc in range(8):
        p.dma("qsp", lambda e, fc=fc: e.dma_start(
            out=wo16s[:, fc * D:(fc + 1) * D],
            in_=wo16_d[fc * 128:(fc + 1) * 128, :]),
            reads=[f"slab:wo:{fc}"], writes=["wo16s"], tag="wold")

    _KSKIP = set(_os.environ.get("KSKIP", "").split(","))
    _KP4 = False

    def p4_slice(s):
        p.dma("qsp", lambda e: e.dma_start(out=ylb[:, :], in_=ylocD[s]),
              reads=[f"ylD:{s}", "ylb"], writes=["ylb"], tag=f"rl{s}")
        p.dma("qsp", lambda e: e.dma_start(out=a1b[:, :], in_=a1D[s]),
              reads=[f"a1D:{s}", "a1b"], writes=["a1b"], tag=f"rl{s}")
        p.dma("qsp", lambda e: e.dma_start(out=a2b[:, :], in_=a2D[s]),
              reads=[f"a2D:{s}", "a2b"], writes=["a2b"], tag=f"rl{s}")
        for h in range(2):
            p.op("pe", lambda e, h=h: e.matmul(
                psCum[:, h * 512:(h + 1) * 512], chselA[:, s * 128:(s + 1) * 128],
                hrt2R[:, h * 512:(h + 1) * 512], start=True, stop=True),
                reads=["chselA", "hrt2R"], writes=[f"psC:{h}"], tag=f"hr{s}")
            p.op("pe", lambda e, h=h: e.matmul(
                psMain[:, h * 512:(h + 1) * 512], chselA[:, s * 128:(s + 1) * 128],
                hrt2I[:, h * 512:(h + 1) * 512], start=True, stop=True),
                reads=["chselA", "hrt2I"], writes=[f"psM:{h}"], tag=f"hr{s}")
        p.op("dve", lambda e: e.tensor_mul(out=fA[:, :], in0=a1b[:, :],
                                           in1=psCum[:, :]),
             reads=["a1b", "psC:0", "psC:1"], writes=["fA"])
        p.op("dve", lambda e: e.tensor_mul(out=fB[:, :], in0=a2b[:, :],
                                           in1=psMain[:, 0:1024]),
             reads=["a2b", "psM:0", "psM:1"], writes=["fB"])
        p.op("dve", lambda e: e.tensor_sub(out=fA[:, :], in0=fA[:, :],
                                           in1=fB[:, :]),
             reads=["fA", "fB"], writes=["fA"])
        if "corr" in _KSKIP:
            p.op("dve", lambda e: e.tensor_copy(out=ypb[:, :], in_=ylb[:, :]),
                 reads=["ylb"], writes=["ypb"])
        else:
            p.op("dve", lambda e: e.tensor_add(out=ypb[:, :], in0=ylb[:, :],
                                               in1=fA[:, :]),
                 reads=["ylb", "fA"], writes=["ypb"])
        for dc in range(8):
            p.op("pe", lambda e, dc=dc: e.transpose(
                psTrB[:, dc * 128:(dc + 1) * 128],
                ypb[:, dc * 128:(dc + 1) * 128], ident[:, :]),
                reads=["ypb", "ident", "psTrB"], writes=["psTrB"], tag=f"ytr{s}")
        p.op("act", lambda e: e.activation(out=ypfm[:, :], in_=psTrB[:, :],
                                           func=AF.Copy),
             reads=["psTrB"], writes=["ypfm"], tag=f"ytre{s}")
        for fc in range(8):
            for h in range(2):
                p.op("pe", lambda e, fc=fc, h=h: e.matmul(
                    psAux[:, h * 512:(h + 1) * 512],
                    ypfm[:, fc * 128:(fc + 1) * 128],
                    wo16s[:, fc * D + h * 512: fc * D + (h + 1) * 512],
                    start=(fc == 0), stop=(fc == 7)),
                    reads=["ypfm", "wo16s"], writes=[f"psA:{h}"], tag=f"wo{s}")
        p.op("act", lambda e: e.activation(out=yattn[:, s * D:(s + 1) * D],
                                           in_=psAux[:, :], func=AF.Copy),
             reads=["psA:0", "psA:1"], writes=[f"yattn:{s}"], tag=f"woe{s}")

    for s in range(NSL):
        p4_slice(s)

    xqfm = sb(s2, "xqfm", [128, 8 * 512], BF16)
    hg_g = sb(s2, "hg_g", [128, 32 * 512], BF16)
    hsqb = sb(s2, "hsqb", [128, 512], BF16)
    mxb = sb(s2, "mxb", [128, 512], BF16)
    mnb = sb(s2, "mnb", [128, 512], BF16)
    w1sl = sb(s2, "w1sl", [128, 2 * 1024], BF16)
    w2sl = sb(s2, "w2sl", [128, 2 * 1024], BF16)
    xq_tm = sb(s2, "xq_tm", [128, D], BF16)
    xs2 = sb(s2, "xs2", [128, D], F32)
    mrow = sb(s2, "mrow", [1, 512], F32)
    mrowb = sb(s2, "mrowb", [1, 512], BF16)
    mrowb2 = sb(s2, "mrowb2", [1, 512], BF16)
    r1 = sb(s2, "r1", [1, 512], F32)
    r2 = sb(s2, "r2", [1, 512], F32)
    c2rep = sb(s2, "c2rep", [128, 512], BF16)
    i2rep = sb(s2, "i2rep", [128, 512], BF16)
    q16i = sb(s2, "q16i", [128, 512], I32)
    of1 = sb(s2, "of1", [128, D], F32)
    mxcol = sb(s2, "mxcol", [128, 4], F32)

    _KSKIP = set(_os.environ.get("KSKIP", "").split(","))

    def p5_group(g):
        for j in range(4):
            s = g * 4 + j
            p.dma("qsp", lambda e, s=s: e.dma_start(
                out=xs[:, :], in_=x_d[s * 128:(s + 1) * 128, :]),
                reads=["xs"], writes=["xs"], tag=f"x2{s}")
            p.op("act", lambda e, s=s: e.activation(
                out=xs[:, :], in_=xs[:, :], func=AF.Identity,
                scale=istd_a[:, s:s + 1], bias=nmu_a[:, s:s + 1]),
                reads=["xs", f"istd:{s}", f"nmu:{s}"], writes=["xs"])
            p.op("act", lambda e, s=s: e.activation(
                out=fA[:, :], in_=xs[:, :], func=AF.Identity,
                scale=ct_a[:, s:s + 1]),
                reads=["xs", f"ct:{s}"], writes=["fA"])
            p.op("dve", lambda e: e.tensor_copy(out=fB.ap().bitcast(I32),
                                                in_=fA[:, :]),
                 reads=["fA", "fB"], writes=["fB"])
            p.op("dve", lambda e: e.tensor_scalar(out=fA[:, :],
                                                  in0=fB.ap().bitcast(I32),
                                                  scalar1=127.0, scalar2=None,
                                                  op0=OP.min),
                 reads=["fB"], writes=["fA"])
            p.op("act", lambda e, s=s: e.activation(
                out=xq_tm[:, :], in_=fA[:, :], func=AF.Identity,
                scale=invs_a[:, s:s + 1]),
                reads=["fA", f"invs:{s}"], writes=["xq_tm"])
            for dc in range(8):
                p.op("pe", lambda e, dc=dc: e.transpose(
                    psTrB[:, dc * 128:(dc + 1) * 128],
                    xq_tm[:, dc * 128:(dc + 1) * 128], ident[:, :]),
                    reads=["xq_tm", "ident", "psTrB"], writes=["psTrB"],
                    tag=f"xqt{s}")
            p.op("act", lambda e, j=j: e.activation(
                out=xqfm.ap().rearrange("p (c t) -> p c t", c=8)[:, :,
                                                                j * 128:(j + 1) * 128],
                in_=psTrB.ap().rearrange("p (c t) -> p c t", c=8),
                func=AF.Copy),
                reads=["psTrB"], writes=[f"xqfm:{j}"], tag=f"xqe{s}")
        xq_reads = [f"xqfm:{j}" for j in range(4)]
        for fc in range(32 if "ffn1" not in _KSKIP else 0):
            sl = fc % 2
            p.dma("qsp", lambda e, fc=fc, sl=sl: e.dma_start(
                out=w1sl[:, sl * 1024:(sl + 1) * 1024], in_=w1t16_d[fc]),
                reads=[f"slab:w1:{fc}", f"w1sl:{sl}"], writes=[f"w1sl:{sl}"],
                tag=f"w1l{g}")
            for dc in range(8):
                p.op("pe", lambda e, fc=fc, sl=sl, dc=dc: e.matmul(
                    psMain[:, 0:512],
                    w1sl[:, sl * 1024 + dc * 128: sl * 1024 + (dc + 1) * 128],
                    xqfm[:, dc * 512:(dc + 1) * 512],
                    start=(dc == 0), stop=(dc == 7)),
                    reads=[f"w1sl:{sl}"] + xq_reads, writes=["psM:0"],
                    tag=f"f1m{g}")
            p.op("act", lambda e, fc=fc: e.activation(
                out=hg_g[:, fc * 512:(fc + 1) * 512], in_=psMain[:, 0:512],
                func=AF.Gelu, bias=b1c[:, fc:fc + 1]),
                reads=["psM:0", "b1c"], writes=[f"hg:{fc}"], tag=f"f1e{g}")
            p.op("act", lambda e, fc=fc: e.activation(
                out=hsqb[:, :], in_=hg_g[:, fc * 512:(fc + 1) * 512],
                func=AF.Square),
                reads=[f"hg:{fc}", "hsqb"], writes=["hsqb"])
            p.op("pe", lambda e, fc=fc: e.matmul(
                psMain[:, 512:1024], onesr[:, :], hsqb[:, :],
                start=(fc == 0), stop=(fc == 31)),
                reads=["hsqb", "onesr"], writes=["psM:1"], tag=f"sq{g}")
            if fc == 0:
                p.op("dve", lambda e: e.tensor_copy(out=mxb[:, :],
                                                    in_=hg_g[:, 0:512]),
                     reads=["hg:0", "mxb"], writes=["mxb"])
                p.op("dve", lambda e: e.tensor_copy(out=mnb[:, :],
                                                    in_=hg_g[:, 0:512]),
                     reads=["hg:0", "mnb"], writes=["mnb"])
            else:
                p.op("dve", lambda e, fc=fc: e.tensor_tensor(
                    out=mxb[:, :], in0=mxb[:, :],
                    in1=hg_g[:, fc * 512:(fc + 1) * 512], op=OP.max),
                    reads=["mxb", f"hg:{fc}"], writes=["mxb"])
                p.op("dve", lambda e, fc=fc: e.tensor_tensor(
                    out=mnb[:, :], in0=mnb[:, :],
                    in1=hg_g[:, fc * 512:(fc + 1) * 512], op=OP.min),
                    reads=["mnb", f"hg:{fc}"], writes=["mnb"])
        if "stats" in _KSKIP:
            return
        p.op("dve", lambda e: e.tensor_scalar(out=mnb[:, :], in0=mnb[:, :],
                                              scalar1=-1.0, scalar2=None,
                                              op0=OP.mult),
             reads=["mnb"], writes=["mnb"])
        p.op("dve", lambda e: e.tensor_max(out=mxb[:, :], in0=mxb[:, :],
                                           in1=mnb[:, :]),
             reads=["mxb", "mnb"], writes=["mxb"])
        for t_ in range(4):
            p.op("pe", lambda e, t_=t_: e.transpose(
                psTrB[:, 0:128], mxb[:, t_ * 128:(t_ + 1) * 128], ident[:, :]),
                reads=["mxb", "ident", "psTrB"], writes=["psTrB"], tag=f"mxt{g}")
            p.op("dve", lambda e, t_=t_: e.tensor_reduce(
                out=mxcol[:, t_:t_ + 1], in_=psTrB[:, 0:128], axis=AX,
                op=OP.max),
                reads=["psTrB"], writes=["mxcol"])
        for t_ in range(4):
            p.dma("qpool", lambda e, t_=t_: e.dma_start(
                out=mrow[0:1, t_ * 128:(t_ + 1) * 128],
                in_=mxcol[:, t_:t_ + 1]),
                reads=["mxcol"], writes=["mrow"], tag=f"mx{g}")
        p.op("act", lambda e: e.activation(out=r1[:, :], in_=psMain[0:1, 512:1024],
                                           func=AF.Sqrt, scale=1.0 / FF, bias=cEPS[0:1, :]),
             reads=["psM:1", "cEPS"], writes=["r1"])
        p.op("dve", lambda e: e.reciprocal(out=r1[:, :], in_=r1[:, :]),
             reads=["r1"], writes=["r1"])
        p.op("dve", lambda e: e.tensor_mul(out=r2[:, :], in0=mrow[:, :],
                                           in1=r1[:, :]),
             reads=["mrow", "r1"], writes=["r2"])
        p.op("dve", lambda e: e.tensor_scalar(out=r2[:, :], in0=r2[:, :],
                                              scalar1=1e-5, scalar2=None,
                                              op0=OP.max),
             reads=["r2"], writes=["r2"])
        p.op("dve", lambda e: e.tensor_scalar(out=mrowb2[:, :], in0=r2[:, :],
                                              scalar1=1.0 / 127.0, scalar2=None,
                                              op0=OP.mult),
             reads=["r2"], writes=["mrowb2"])
        p.op("dve", lambda e: e.reciprocal(out=r2[:, :], in_=r2[:, :]),
             reads=["r2"], writes=["r2"])
        p.op("dve", lambda e: e.tensor_mul(out=r2[:, :], in0=r2[:, :],
                                           in1=r1[:, :]),
             reads=["r2", "r1"], writes=["r2"])
        p.op("dve", lambda e: e.tensor_scalar(out=mrowb[:, :], in0=r2[:, :],
                                              scalar1=127.0, scalar2=None,
                                              op0=OP.mult),
             reads=["r2"], writes=["mrowb"])
        p.op("pe", lambda e: e.matmul(psAux[:, 0:512], onesr[0:1, :],
                                      mrowb[:, :], start=True, stop=True),
             reads=["onesr", "mrowb"], writes=["psA:0"], tag=f"rep{g}")
        p.op("pe", lambda e: e.matmul(psAux[:, 512:1024], onesr[0:1, :],
                                      mrowb2[:, :], start=True, stop=True),
             reads=["onesr", "mrowb2"], writes=["psA:1"], tag=f"rep{g}")
        p.op("act", lambda e: e.activation(out=c2rep[:, :], in_=psAux[:, 0:512],
                                           func=AF.Copy),
             reads=["psA:0"], writes=["c2rep"])
        p.op("act", lambda e: e.activation(out=i2rep[:, :], in_=psAux[:, 512:1024],
                                           func=AF.Copy),
             reads=["psA:1"], writes=["i2rep"])
        for fc in range(32 if "quant" not in _KSKIP else 0):
            p.op("dve", lambda e, fc=fc: e.tensor_tensor(
                out=q16i[:, :], in0=hg_g[:, fc * 512:(fc + 1) * 512],
                in1=c2rep[:, :], op=OP.mult),
                reads=[f"hg:{fc}", "c2rep"], writes=["q16i"])
            p.op("dve", lambda e, fc=fc: e.scalar_tensor_tensor(
                out=hg_g[:, fc * 512:(fc + 1) * 512], in0=q16i[:, :],
                scalar=127.0, in1=i2rep[:, :], op0=OP.min, op1=OP.mult),
                reads=["q16i", "i2rep"], writes=[f"hg:{fc}"])
        for half in range(2 if "ffn2" not in _KSKIP else 0):
            psO = [psCum, psAux]
            pks = ["psC", "psA"]
            for fc in range(32):
                sl = fc % 2
                p.dma("qsp", lambda e, fc=fc, sl=sl: e.dma_start(
                    out=w2sl[:, sl * 1024:(sl + 1) * 1024],
                    in_=w2t16_d[fc * 128:(fc + 1) * 128, :]),
                    reads=[f"slab:w2:{fc}", f"w2sl:{sl}"], writes=[f"w2sl:{sl}"],
                    tag=f"w2l{g}{half}")
                for j2 in range(2):
                    j = half * 2 + j2
                    for h in range(2):
                        p.op("pe", lambda e, fc=fc, sl=sl, j=j, j2=j2, h=h:
                             e.matmul(
                                 psO[j2][:, h * 512:(h + 1) * 512],
                                 hg_g[:, fc * 512 + j * 128:
                                      fc * 512 + (j + 1) * 128],
                                 w2sl[:, sl * 1024 + h * 512:
                                      sl * 1024 + (h + 1) * 512],
                                 start=(fc == 0), stop=False),
                             reads=[f"w2sl:{sl}", f"hg:{fc}"],
                             writes=[pks[j2] + f":{h}"], tag=f"f2m{g}")
            for j2 in range(2):
                j = half * 2 + j2
                s = g * 4 + j
                pk = pks[j2]
                for h in range(2):
                    p.op("pe", lambda e, j2=j2, h=h: e.matmul(
                        psO[j2][:, h * 512:(h + 1) * 512], onesr[0:1, :],
                        b2r16[:, h * 512:(h + 1) * 512], start=False, stop=True),
                        reads=["onesr", "b2r16"], writes=[pk + f":{h}"],
                        tag=f"b2{g}")
                p.dma("qsp", lambda e, s=s: e.dma_start(
                    out=xs2[:, :], in_=x_d[s * 128:(s + 1) * 128, :]),
                    reads=["xs2"], writes=["xs2"], tag=f"x3{s}")
                DBG = _os.environ.get("KDBG", "")
                if DBG == "attn":
                    p.op("dve", lambda e, s=s: e.tensor_copy(
                        out=of1[:, :], in_=yattn[:, s * D:(s + 1) * D]),
                        reads=[f"yattn:{s}", "of1", pk + ":0", pk + ":1", "xs2"],
                        writes=["of1"])
                elif DBG == "ffn":
                    p.op("dve", lambda e, j2=j2: e.tensor_add(
                        out=of1[:, :], in0=psO[j2][:, :], in1=xs2[:, :]),
                        reads=[pk + ":0", pk + ":1", "xs2", "of1"], writes=["of1"])
                else:
                    p.op("dve", lambda e, j2=j2: e.tensor_add(
                        out=of1[:, :], in0=psO[j2][:, :], in1=xs2[:, :]),
                        reads=[pk + ":0", pk + ":1", "xs2", "of1"], writes=["of1"])
                    p.op("dve", lambda e, s=s: e.tensor_add(
                        out=of1[:, :], in0=of1[:, :], in1=yattn[:, s * D:(s + 1) * D]),
                        reads=["of1", f"yattn:{s}"], writes=["of1"])
                p.dma("qpool", lambda e, s=s: e.dma_start(
                    out=out_d[s * 128:(s + 1) * 128, :], in_=of1[:, :]),
                    reads=["of1"], writes=[f"out:{s}"], tag=f"o{s}")

    for g in range(4):
        p5_group(g)

    p.barrier_read("sp", [f"out:{s}" for s in range(NSL)])
    rem = nc.sbuf_bytes_remaining
    import sys
    print(f"[kernel-build] sbuf remaining {rem}B/partition, "
          f"{len(p.ops)} ops", file=sys.stderr)
    p.emit()
    s2.close()
    gstack.close()
    return nc, rem


_NC_CACHE = [None]
kernel_last_res = [None]


def kernel(L0_attn_x, L0_attn_gamma, L0_attn_wq, L0_attn_wk, L0_attn_wv,
           L0_attn_wa, L0_attn_wg, L0_attn_wo, L0_ffn_w1, L0_ffn_b1,
           L0_ffn_w2, L0_ffn_b2):
    x = np.ascontiguousarray(np.asarray(L0_attn_x, np.float32))
    gamma = np.asarray(L0_attn_gamma, np.float32)
    wq = np.ascontiguousarray(np.asarray(L0_attn_wq, np.float32))
    wk = np.ascontiguousarray(np.asarray(L0_attn_wk, np.float32))
    wv = np.ascontiguousarray(np.asarray(L0_attn_wv, np.float32))
    wa = np.ascontiguousarray(np.asarray(L0_attn_wa, np.float32))
    wg = np.ascontiguousarray(np.asarray(L0_attn_wg, np.float32))
    wo = np.ascontiguousarray(np.asarray(L0_attn_wo, np.float32))
    w1 = np.asarray(L0_ffn_w1, np.float32)
    b1 = np.asarray(L0_ffn_b1, np.float32)
    w2 = np.asarray(L0_ffn_w2, np.float32)
    b2 = np.asarray(L0_ffn_b2, np.float32)

    w1t = np.ascontiguousarray(w1.T.reshape(D, 32, 128).transpose(1, 0, 2))
    w2t = np.ascontiguousarray(w2.T)
    b1c = np.ascontiguousarray(b1.reshape(32, 128).T)
    b2r = np.ascontiguousarray(b2.reshape(1, D))
    gr = np.ascontiguousarray(np.broadcast_to(gamma, (128, D)))

    BF = ml_dtypes.bfloat16
    tri = np.zeros((128, 128), np.float32)
    for b_ in range(2):
        for s_ in range(64):
            tri[b_ * 64 + s_, b_ * 64 + s_:(b_ + 1) * 64] = 1.0
    consts = {
        "tri_m1": (-tri).astype(BF),
        "tri_p4": (4.0 * tri).astype(BF),
        "tri_p1": tri.astype(BF),
        "ident": np.eye(128, dtype=np.float32).astype(BF),
        "onesr": np.ones((128, 128), np.float32).astype(BF),
    }
    cind = np.zeros((128, 2), np.float32)
    cind[0:64, 0] = 1.0
    cind[64:128, 1] = 1.0
    consts["cind"] = cind.astype(BF)
    chsel = np.zeros((32, NSL * 128), np.float32)
    for s_ in range(NSL):
        chsel[2 * s_, s_ * 128: s_ * 128 + 64] = 1.0
        chsel[2 * s_ + 1, s_ * 128 + 64: (s_ + 1) * 128] = 1.0
    consts["chsel"] = chsel.astype(BF)

    if _NC_CACHE[0] is None:
        _NC_CACHE[0] = build()[0]
    nc = _NC_CACHE[0]

    in_maps = []
    for c in range(8):
        b_, h_ = c // 2, c % 2
        sel = np.zeros(8, np.float32)
        if h_ == 1:
            sel[c - 1] = 1.0
        selrep = np.ascontiguousarray(
            np.broadcast_to(np.repeat(sel, 16)[None, :], (128, 128)))
        m = {"x": np.ascontiguousarray(x[b_, h_ * T:(h_ + 1) * T]),
             "wq": wq, "wk": wk, "wv": wv, "wg": wg, "wa": wa, "wo": wo,
             "w1t": w1t, "w2t": w2t, "b1c": b1c, "b2r": b2r, "gr": gr,
             "selrep": selrep}
        m.update(consts)
        in_maps.append(m)
    res = run_bass_kernel_spmd(nc, in_maps, core_ids=list(range(8)))
    kernel_last_res[0] = res
    out = np.empty((4, 4096, D), np.float32)
    for c in range(8):
        b_, h_ = c // 2, c % 2
        out[b_, h_ * T:(h_ + 1) * T] = res.results[c]["out"]
    return out


# version 07
# speedup vs baseline: 1.0910x; 1.0846x over previous
import math
import os as _os
import contextlib
from collections import defaultdict

import numpy as np
import ml_dtypes
import concourse.bass as bass
import concourse.mybir as mybir
from concourse.bass_utils import run_bass_kernel_spmd

F32 = mybir.dt.float32
BF16 = mybir.dt.bfloat16
I32 = mybir.dt.int32
I16 = mybir.dt.int16
AF = mybir.ActivationFunctionType
OP = mybir.AluOpType
PI = math.pi
TWO_PI = 2.0 * math.pi

T = 2048
D = 1024
FF = 4096
NSL = 16
C = 64
NCH = T // C
EPS = 1e-5
AX = mybir.AxisListType.X


class _Op:
    __slots__ = ("eng", "emit", "reads", "writes", "dma", "idx", "need_inc",
                 "inc_val", "waits", "tag")

    def __init__(self, eng, emit, reads, writes, dma, tag):
        self.eng, self.emit, self.reads, self.writes = eng, emit, reads, writes
        self.dma = dma
        self.idx = None
        self.need_inc = False
        self.inc_val = None
        self.waits = []
        self.tag = tag


_Q_OF = {"qsp": "sp", "qpool": "pool", "qact": "act", "qdve": "dve"}


class Prog:
    def __init__(self, nc):
        self.nc = nc
        self.ops = []
        self.last_w = {}
        self.readers = defaultdict(dict)
        self.pending_fence = None
        self._fenced = set()

    def op(self, eng, emit, reads=(), writes=(), tag=""):
        return self._plan(eng, emit, reads, writes, None, tag)

    def dma(self, queue, emit, reads=(), writes=(), tag=""):
        return self._plan(_Q_OF[queue], emit, reads, writes, queue, tag)

    def barrier_read(self, eng, keys, tag="fin"):
        return self._plan(eng, None, tuple(keys), (), None, tag)

    def fence(self):
        snap = {}
        for o in self.ops:
            snap[o.dma if o.dma else o.eng] = o
        for o in snap.values():
            o.need_inc = True
        self.pending_fence = dict(snap)
        self._fenced = set()

    def _plan(self, eng, emit, reads, writes, dma, tag):
        o = _Op(eng, emit, tuple(reads), tuple(writes), dma, tag)
        sem_name = dma if dma else eng
        deps = {}

        def add_dep(p_):
            if p_ is None:
                return
            psem = p_.dma if p_.dma else p_.eng
            cur = deps.get(psem)
            if cur is None or p_.idx > cur.idx:
                deps[psem] = p_

        if self.pending_fence is not None and sem_name not in self._fenced:
            self._fenced.add(sem_name)
            for p_ in self.pending_fence.values():
                add_dep(p_)
        for r in o.reads:
            add_dep(self.last_w.get(r))
        for w in o.writes:
            add_dep(self.last_w.get(w))
            for p_ in self.readers.get(w, {}).values():
                add_dep(p_)

        o.idx = len(self.ops)
        if dma:
            o.need_inc = True
        self.ops.append(o)
        for p_ in deps.values():
            p_.need_inc = True
            o.waits.append(p_)
        if emit is not None:
            for w in o.writes:
                self.last_w[w] = o
                self.readers[w] = {}
            for r in o.reads:
                self.readers[r][sem_name] = o
        return o

    def emit(self):
        nc = self.nc
        counts = defaultdict(int)
        for o in self.ops:
            if o.need_inc:
                sem = o.dma if o.dma else o.eng
                counts[sem] += 16 if o.dma else 1
                o.inc_val = counts[sem]
        per_eng = defaultdict(list)
        for o in self.ops:
            per_eng[o.eng].append(o)
        stack = contextlib.ExitStack()
        sems = {s: stack.enter_context(nc.semaphore("sem_" + s))
                for s in sorted(counts)}

        def run_engine(eng_obj, eng_name):
            waited = {}
            for o in per_eng[eng_name]:
                for p_ in o.waits:
                    psem = p_.dma if p_.dma else p_.eng
                    v = p_.inc_val
                    if waited.get(psem, -1) < v:
                        eng_obj.wait_ge(sems[psem], v)
                        waited[psem] = v
                if o.emit is None:
                    continue
                inst = o.emit(eng_obj)
                if o.need_inc:
                    sem = o.dma if o.dma else o.eng
                    inst.then_inc(sems[sem], 16 if o.dma else 1)

        with stack:
            with nc.Block() as block:
                for name, attr in [("pe", "tensor"), ("act", "scalar"),
                                   ("dve", "vector"), ("pool", "gpsimd"),
                                   ("sp", "sync")]:
                    if per_eng[name]:
                        getattr(block, attr)(
                            lambda e, n=name: run_engine(e, n))


def build():
    nc = bass.Bass("TRN2", target_bir_lowering=False)
    p = Prog(nc)

    dt = nc.dram_tensor
    x_d = dt("x", [T, D], F32, kind="ExternalInput").ap()
    wq_d = dt("wq", [D, D], F32, kind="ExternalInput").ap()
    wk_d = dt("wk", [D, D], F32, kind="ExternalInput").ap()
    wv_d = dt("wv", [D, D], F32, kind="ExternalInput").ap()
    wg_d = dt("wg", [D, D], F32, kind="ExternalInput").ap()
    wa_d = dt("wa", [D, 2 * D], F32, kind="ExternalInput").ap()
    wo_d = dt("wo", [D, D], F32, kind="ExternalInput").ap()
    w1t_d = dt("w1t", [32, D, 128], F32, kind="ExternalInput").ap()
    w2t_d = dt("w2t", [FF, D], F32, kind="ExternalInput").ap()
    b1c_d = dt("b1c", [128, 32], F32, kind="ExternalInput").ap()
    b2r_d = dt("b2r", [1, D], F32, kind="ExternalInput").ap()
    gr_d = dt("gr", [128, D], F32, kind="ExternalInput").ap()
    tri_m1_d = dt("tri_m1", [128, 128], BF16, kind="ExternalInput").ap()
    tri_p4_d = dt("tri_p4", [128, 128], BF16, kind="ExternalInput").ap()
    tri_p1_d = dt("tri_p1", [128, 128], BF16, kind="ExternalInput").ap()
    ident_d = dt("ident", [128, 128], BF16, kind="ExternalInput").ap()
    cind_d = dt("cind", [128, 2], BF16, kind="ExternalInput").ap()
    ones_d = dt("onesr", [128, 128], BF16, kind="ExternalInput").ap()
    chsel_d = dt("chsel", [32, NSL * 128], BF16, kind="ExternalInput").ap()
    selrep_d = dt("selrep", [128, 128], F32, kind="ExternalInput").ap()
    out_d = dt("out", [T, D], F32, kind="ExternalOutput").ap()
    w1t16_d = dt("w1t16", [32, 128, 1024], BF16, kind="Internal").ap()
    w2t16_d = dt("w2t16", [FF, D], BF16, kind="Internal").ap()
    ylocD = dt("ylocD", [NSL, 128, D], BF16, kind="Internal").ap()
    a1D = dt("a1D", [NSL, 128, D], BF16, kind="Internal").ap()
    a2D = dt("a2D", [NSL, 128, D], BF16, kind="Internal").ap()
    _KDQ = _os.environ.get("KDBG", "") == "q"
    if _KDQ:
        dbgq_d = dt("dbgq", [4, 128, 1024], F32, kind="ExternalOutput").ap()
    ccin_d = dt("ccin", [1, 2048], F32, kind="Internal").ap()
    ccout_d = dt("ccout", [8, 2048], F32, kind="Internal").ap()

    gstack = contextlib.ExitStack()

    def sb(stk, n, shape, dty):
        return stk.enter_context(nc.sbuf_tensor(n, shape, dty))

    def psum(stk, n, shape, dty):
        return stk.enter_context(nc.psum_tensor(n, shape, dty))

    tri_m1 = sb(gstack, "s_tri_m1", [128, 128], BF16)
    tri_p4 = sb(gstack, "s_tri_p4", [128, 128], BF16)
    tri_p1 = sb(gstack, "s_tri_p1", [128, 128], BF16)
    ident = sb(gstack, "s_ident", [128, 128], BF16)
    cind = sb(gstack, "s_cind", [128, 2], BF16)
    onesr = sb(gstack, "s_onesr", [128, 128], BF16)
    chselA = sb(gstack, "s_chsel", [32, NSL * 128], BF16)
    selrep = sb(gstack, "s_selrep", [128, 128], F32)
    gr = sb(gstack, "s_gr", [128, D], F32)
    b1c = sb(gstack, "s_b1c", [128, 32], F32)
    bn6 = sb(gstack, "bn6", [128, 12], F32)
    bn2 = sb(gstack, "bn2", [128, 2], F32)
    istd_a = sb(gstack, "istd_a", [128, NSL], F32)
    nmu_a = sb(gstack, "nmu_a", [128, NSL], F32)
    scl_a = sb(gstack, "scl_a", [128, NSL], F32)
    ct_a = sb(gstack, "ct_a", [128, NSL], F32)
    invs_a = sb(gstack, "invs_a", [128, NSL], F32)
    cEPS = sb(gstack, "cEPS", [128, 1], F32)
    cPIH = sb(gstack, "cPIH", [128, 1], F32)
    c38 = sb(gstack, "c38", [128, 1], F32)
    tA = sb(gstack, "tA", [128, 1], F32)
    tB = sb(gstack, "tB", [128, 1], F32)
    tC = sb(gstack, "tC", [128, 1], F32)
    tD_ = sb(gstack, "tD", [128, 1], F32)
    aggS = sb(gstack, "aggS", [128, 80], F32)
    hP_r = sb(gstack, "hP_r", [128, 8 * (NCH + 1)], F32)
    hP_i = sb(gstack, "hP_i", [128, 8 * (NCH + 1)], F32)
    PP_r = sb(gstack, "PP_r", [128, 8 * NCH], F32)
    PP_i = sb(gstack, "PP_i", [128, 8 * NCH], F32)
    magt = sb(gstack, "magt", [128, 16], F32)
    tt1 = sb(gstack, "tt1", [128, 16], F32)
    tt2 = sb(gstack, "tt2", [128, 16], F32)
    tt3 = sb(gstack, "tt3", [128, 16], F32)
    tt4 = sb(gstack, "tt4", [128, 16], F32)
    tki = sb(gstack, "tki", [128, 16], I32)
    acr_t = sb(gstack, "acr_t", [128, 16], F32)
    aci_t = sb(gstack, "aci_t", [128, 16], F32)
    slr_t = sb(gstack, "slr_t", [128, 16], F32)
    sli_t = sb(gstack, "sli_t", [128, 16], F32)
    ch1 = sb(gstack, "ch1", [128, 8], F32)
    ch2 = sb(gstack, "ch2", [128, 8], F32)
    Hpack = sb(gstack, "Hpack", [128, 16], F32)
    gatc = sb(gstack, "gatc", [128, 128], F32)
    Hprev = sb(gstack, "Hprev", [128, 16], F32)
    HrR = sb(gstack, "HrR", [128, 8 * NCH], F32)
    HiR = sb(gstack, "HiR", [128, 8 * NCH], F32)
    httR = sb(gstack, "httR", [128, 8 * NCH], BF16)
    httI = sb(gstack, "httI", [128, 8 * NCH], BF16)
    hrt2R = sb(gstack, "hrt2R", [32, D], BF16)
    hrt2I = sb(gstack, "hrt2I", [32, D], BF16)
    xs = sb(gstack, "xs", [128, D], F32)
    xng_tm = sb(gstack, "xng_tm", [128, D], BF16)
    fA = sb(gstack, "fA", [128, D], F32)
    fB = sb(gstack, "fB", [128, D], F32)
    acc_w1 = sb(gstack, "acc_w1", [128, 1], F32)
    acc_w2 = sb(gstack, "acc_w2", [128, 1], F32)
    asum = sb(gstack, "asum", [128, 1], F32)
    sw_w1 = sb(gstack, "sw_w1", [128, 1], F32)
    sw_w2 = sb(gstack, "sw_w2", [128, 1], F32)
    mc_w1 = sb(gstack, "mc_w1", [128, 1], F32)
    mc_w2 = sb(gstack, "mc_w2", [128, 1], F32)

    psMain = psum(gstack, "psMain", [128, 1536], F32)
    psAux = psum(gstack, "psAux", [128, 1024], F32)
    psCum = psum(gstack, "psCum", [128, 1024], F32)
    psTrB = psum(gstack, "psTrB", [128, 1024], BF16)

    for nm, dst, src in [("tri_m1", tri_m1, tri_m1_d), ("tri_p4", tri_p4, tri_p4_d),
                         ("tri_p1", tri_p1, tri_p1_d), ("ident", ident, ident_d),
                         ("cind", cind, cind_d), ("onesr", onesr, ones_d),
                         ("chselA", chselA, chsel_d), ("selrep", selrep, selrep_d),
                         ("gr", gr, gr_d), ("b1c", b1c, b1c_d)]:
        p.dma("qsp", lambda e, dst=dst, src=src: e.dma_start(out=dst[:, :], in_=src),
              writes=[nm], tag="cst")
    p.op("dve", lambda e: e.memset(cEPS[:, :], EPS), writes=["cEPS"])
    p.op("dve", lambda e: e.memset(cPIH[:, :], PI / 2), writes=["cPIH"])
    p.op("dve", lambda e: e.memset(c38[:, :], 1e-38), writes=["c38"])
    p.op("dve", lambda e: e.memset(hP_r[:, 0:8], 0.0), writes=["hP:0"])
    p.op("dve", lambda e: e.memset(hP_i[:, 0:8], 0.0), writes=["hPi:0"])
    p.op("dve", lambda e: e.memset(PP_r[:, 0:8], 1.0), writes=["PP:0"])
    p.op("dve", lambda e: e.memset(PP_i[:, 0:8], 0.0), writes=["PPi:0"])
    p.op("dve", lambda e: e.memset(acc_w1[:, :], 0.0), writes=["acc_w1"])
    p.op("dve", lambda e: e.memset(acc_w2[:, :], 0.0), writes=["acc_w2"])

    s1 = contextlib.ExitStack()
    wqkv16 = sb(s1, "wqkv16", [128, 8 * 6144], BF16)
    wsl2 = [sb(s1, f"wsl_f{i}", [128, 1024], F32) for i in range(2)]
    wsl_b = sb(s1, "wsl_b", [128, 1024], BF16)
    wslq = sb(s1, "wslq", [128, 1024], F32)
    onesf = sb(s1, "onesf", [128, 128], F32)
    p.op("dve", lambda e: e.memset(onesf[:, :], 1.0), writes=["onesf"])
    for grp, (w, off, width) in [("wlA", (wq_d, 0, D)), ("wlA", (wk_d, D, D)),
                                 ("wlB", (wv_d, 2 * D, D)),
                                 ("wlC", (wg_d, 3 * D, D)),
                                 ("wlD", (wa_d, 4 * D, 2 * D))]:
        for dc in range(8):
            p.dma("qpool",
                  lambda e, w=w, off=off, width=width, dc=dc:
                  e.dma_start(out=wqkv16[:, dc * 6144 + off:
                                         dc * 6144 + off + width],
                              in_=w[dc * 128:(dc + 1) * 128, :]),
                  writes=[grp], tag="wld")

    def _slab_src(kind, fc, buf):
        if kind == "w1":
            return (w1t_d[fc].rearrange("(c p) j -> p c j", p=128),
                    buf.ap().rearrange("p (c j) -> p c j", c=8))
        return (w2t_d[fc * 128:(fc + 1) * 128, :], buf[:, :])

    sum_jobs = [("w1", fc) for fc in range(32)] + [("w2", fc) for fc in range(32)]
    sum_i = [0]

    def plan_sums(n):
        for _ in range(n):
            if sum_i[0] >= len(sum_jobs):
                return
            j = sum_i[0]
            kind, fc = sum_jobs[j]
            sum_i[0] += 1
            b = j % 2
            srcd, ovw = _slab_src(kind, fc, wsl2[b])
            acc = acc_w1 if kind == "w1" else acc_w2
            ak = "acc_" + kind
            p.dma(["qsp", "qact"][b],
                  lambda e, srcd=srcd, ovw=ovw: e.dma_start(out=ovw, in_=srcd),
                  writes=[f"wslf:{b}"], tag="sma")
            p.op("act", lambda e, b=b: e.activation(out=wslq[:, :],
                                                    in_=wsl2[b][:, :],
                                                    func=AF.Abs,
                                                    accum_out=asum[:, :]),
                 reads=[f"wslf:{b}", "asum"], writes=["asum", "wslq"], tag="smb")
            p.op("dve", lambda e, acc=acc: e.tensor_add(out=acc[:, :],
                                                        in0=acc[:, :],
                                                        in1=asum[:, :]),
                 reads=[ak, "asum"], writes=[ak], tag="smc")

    def plan_wscale():
        for kind, acc, sw, mc, ps_off in [("w1", acc_w1, sw_w1, mc_w1, 0),
                                          ("w2", acc_w2, sw_w2, mc_w2, 1)]:
            ak, swk, mck = "acc_" + kind, "sw_" + kind, "mc_" + kind
            p.op("pe", lambda e, acc=acc, ps_off=ps_off: e.matmul(
                psAux[:, ps_off:ps_off + 1], onesf[:, :], acc[:, :],
                start=True, stop=True),
                reads=[ak, "onesf"], writes=["psA:0"], tag="wsc")
            p.op("dve", lambda e, mc=mc, ps_off=ps_off: e.tensor_scalar(
                out=mc[:, :], in0=psAux[:, ps_off:ps_off + 1],
                scalar1=1.0 / (4096.0 * 1024.0), scalar2=1e-5,
                op0=OP.mult, op1=OP.max),
                reads=["psA:0"], writes=[mck], tag="wsc")
            p.op("dve", lambda e, sw=sw, mc=mc: e.reciprocal(out=sw[:, :],
                                                             in_=mc[:, :]),
                 reads=[mck], writes=[swk], tag="wsc")

    slab_jobs = ([("w1", fc) for fc in range(32)] +
                 [("w2", fc) for fc in range(32)])
    slab_i = [0]

    def plan_slabs(n):
        for _ in range(n):
            if slab_i[0] >= len(slab_jobs):
                return
            j = slab_i[0]
            kind, fc = slab_jobs[j]
            slab_i[0] += 1
            b = j % 2
            srcd, ovw = _slab_src(kind, fc, wsl2[b])
            dst = (w1t16_d[fc] if kind == "w1"
                   else w2t16_d[fc * 128:(fc + 1) * 128, :])
            sw = sw_w1 if kind == "w1" else sw_w2
            swk = "sw_" + kind
            p.dma(["qsp", "qact"][b],
                  lambda e, srcd=srcd, ovw=ovw: e.dma_start(out=ovw, in_=srcd),
                  writes=[f"wslf:{b}"], tag="slf")
            p.op("act", lambda e, b=b, sw=sw: e.activation(
                out=wslq[:, :], in_=wsl2[b][:, :], func=AF.Identity,
                scale=sw[:, :]),
                reads=[f"wslf:{b}", swk], writes=["wslq"], tag="slq")
            p.op("dve", lambda e: e.tensor_copy(
                out=wslq.ap().bitcast(I32), in_=wslq[:, :]),
                reads=["wslq"], writes=["wslq"], tag="slq")
            p.op("dve", lambda e: e.tensor_scalar(
                out=wsl_b[:, :], in0=wslq.ap().bitcast(I32),
                scalar1=1.0, scalar2=-1.0, op0=OP.min, op1=OP.max),
                reads=["wslq"], writes=["wslc"], tag="slq")
            p.dma(["qact", "qsp"][b],
                  lambda e, dst=dst: e.dma_start(out=dst, in_=wsl_b[:, :]),
                  reads=["wslc"], writes=[f"slab:{kind}:{fc}"], tag="slw")

    xngfm = sb(s1, "xngfm0", [128, 1024], BF16)
    arf2 = [sb(s1, f"arf{i}", [128, D], F32) for i in range(2)]
    aif2 = [sb(s1, f"aif{i}", [128, D], F32) for i in range(2)]
    fC = sb(s1, "fC", [128, D], F32)
    fD = sb(s1, "fD", [128, D], F32)
    fE = sb(s1, "fE", [128, D], F32)
    q2 = [sb(s1, f"q2{i}", [128, D], BF16) for i in range(2)]
    g2 = [sb(s1, f"g2{i}", [128, D], BF16) for i in range(2)]
    u2 = [sb(s1, f"u2{i}", [128, D], BF16) for i in range(2)]
    bb = {j: sb(s1, f"bb{j}", [128, D], BF16) for j in (0, 1, 2, 4, 5, 6, 7, 9)}

    def fe(s):
        i = s % 2
        col = lambda a: a[:, s:s + 1]
        xsl = xngfm
        xk = "xngfm:0"
        p.dma(["qsp", "qact"][s % 2],
              lambda e: e.dma_start(out=xs[:, :],
                                    in_=x_d[s * 128:(s + 1) * 128, :]),
              reads=["xs"], writes=["xs"], tag=f"x{s}")
        p.op("dve", lambda e: e.bn_stats(out=bn6[:, 0:6], in_=xs[:, 0:512]),
             reads=["xs"], writes=["bn6"])
        p.op("dve", lambda e: e.bn_stats(out=bn6[:, 6:12], in_=xs[:, 512:1024]),
             reads=["xs"], writes=["bn6"])
        p.op("dve", lambda e: e.bn_aggr(out=bn2[:, :], in_=bn6[:, :]),
             reads=["bn6"], writes=["bn2"])
        p.op("act", lambda e: e.activation(out=tA[:, :], in_=bn2[:, 1:2],
                                           func=AF.Ln, bias=cEPS[:, :]),
             reads=["bn2", "cEPS"], writes=["tA"])
        p.op("act", lambda e: e.activation(out=col(istd_a), in_=tA[:, :],
                                           func=AF.Exp, scale=-0.5),
             reads=["tA"], writes=[f"istd:{s}"])
        p.op("dve", lambda e: e.tensor_mul(out=tB[:, :], in0=bn2[:, 0:1],
                                           in1=col(istd_a)),
             reads=["bn2", f"istd:{s}"], writes=["tB"])
        p.op("dve", lambda e: e.tensor_scalar(out=col(nmu_a), in0=tB[:, :],
                                              scalar1=-1.0, scalar2=None,
                                              op0=OP.mult),
             reads=["tB"], writes=[f"nmu:{s}"])
        p.op("dve", lambda e: e.tensor_mul(out=tB[:, :], in0=bn2[:, 1:2],
                                           in1=col(istd_a)),
             reads=["bn2", f"istd:{s}"], writes=["tB"])
        p.op("dve", lambda e: e.tensor_mul(out=tB[:, :], in0=tB[:, :],
                                           in1=col(istd_a)),
             reads=["tB", f"istd:{s}"], writes=["tB"])
        p.op("act", lambda e: e.activation(out=tA[:, :], in_=tB[:, :],
                                           func=AF.Ln, scale=float(D),
                                           bias=cEPS[:, :]),
             reads=["tB", "cEPS"], writes=["tA"])
        p.op("act", lambda e: e.activation(out=tA[:, :], in_=tA[:, :],
                                           func=AF.Exp, scale=-0.5),
             reads=["tA"], writes=["tA"])
        p.op("dve", lambda e: e.tensor_scalar(out=col(scl_a), in0=tA[:, :],
                                              scalar1=32.0, scalar2=None,
                                              op0=OP.mult),
             reads=["tA"], writes=[f"scl:{s}"])
        p.op("act", lambda e: e.activation(out=tC[:, :], in_=tB[:, :],
                                           func=AF.Ln, bias=cEPS[:, :]),
             reads=["tB", "cEPS"], writes=["tC"])
        p.op("act", lambda e: e.activation(out=tC[:, :], in_=tC[:, :],
                                           func=AF.Exp, scale=-0.5),
             reads=["tC"], writes=["tC"])
        p.op("act", lambda e: e.activation(out=xs[:, :], in_=xs[:, :],
                                           func=AF.Identity, scale=col(istd_a),
                                           bias=col(nmu_a)),
             reads=["xs", f"istd:{s}", f"nmu:{s}"], writes=["xs"])
        p.op("dve", lambda e: e.tensor_reduce(out=tD_[:, :], in_=xs[:, :],
                                              axis=AX, op=OP.max,
                                              apply_absolute_value=True),
             reads=["xs"], writes=["tD"])
        p.op("dve", lambda e: e.tensor_mul(out=tB[:, :], in0=tD_[:, :], in1=tC[:, :]),
             reads=["tD", "tC"], writes=["tB"])
        p.op("dve", lambda e: e.tensor_scalar(out=tB[:, :], in0=tB[:, :],
                                              scalar1=1e-5, scalar2=None, op0=OP.max),
             reads=["tB"], writes=["tB"])
        p.op("dve", lambda e: e.tensor_scalar(out=col(invs_a), in0=tB[:, :],
                                              scalar1=1.0 / 127.0, scalar2=None,
                                              op0=OP.mult),
             reads=["tB"], writes=[f"invs:{s}"])
        p.op("dve", lambda e: e.reciprocal(out=tB[:, :], in_=tB[:, :]),
             reads=["tB"], writes=["tB"])
        p.op("dve", lambda e: e.tensor_mul(out=tB[:, :], in0=tB[:, :], in1=tC[:, :]),
             reads=["tB", "tC"], writes=["tB"])
        p.op("dve", lambda e: e.tensor_scalar(out=col(ct_a), in0=tB[:, :],
                                              scalar1=127.0, scalar2=None,
                                              op0=OP.mult),
             reads=["tB"], writes=[f"ct:{s}"])
        p.op("dve", lambda e: e.tensor_mul(out=xng_tm[:, :], in0=xs[:, :],
                                           in1=gr[:, :]),
             reads=["xs", "gr"], writes=["xng_tm"])
        for dc in range(8):
            p.op("pe", lambda e, dc=dc: e.transpose(
                psTrB[:, dc * 128:(dc + 1) * 128],
                xng_tm[:, dc * 128:(dc + 1) * 128], ident[:, :]),
                reads=["xng_tm", "ident"], writes=["psTrB"], tag=f"tr{s}")
        p.op("act", lambda e: e.activation(out=xsl[:, :], in_=psTrB[:, :],
                                           func=AF.Copy),
             reads=["psTrB"], writes=[xk], tag=f"tre{s}")

        def mm_pass(wos, tagp):
            for dc in range(8):
                for r_, wo_ in enumerate(wos):
                    p.op("pe", lambda e, dc=dc, r_=r_, wo_=wo_:
                         e.matmul(psMain[:, r_ * 512:(r_ + 1) * 512],
                                  xsl[:, dc * 128:(dc + 1) * 128],
                                  wqkv16[:, dc * 6144 + wo_: dc * 6144 + wo_ + 512],
                                  start=(dc == 0), stop=(dc == 7)),
                         reads=[xk, ("wlA" if wo_ <= 1536 else
                                     "wlB" if wo_ <= 2560 else
                                     "wlC" if wo_ <= 3584 else "wlD")],
                         writes=[f"psM:{r_}"], tag=tagp)

        def evac(out_ap, lo, hi, wr):
            rk = [f"psM:{r_}" for r_ in range(lo // 512, (hi + 511) // 512)]
            p.op("act", lambda e: e.activation(out=out_ap, in_=psMain[:, lo:hi],
                                               func=AF.Identity, scale=col(scl_a)),
                 reads=rk + [f"scl:{s}"], writes=wr)

        mm_pass([0, 512, 1024], f"p1m{s}")
        evac(q2[i][:, :], 0, 1024, [f"q2:{i}"])
        evac(u2[i][:, 0:512], 1024, 1536, [f"u2:{i}"])
        mm_pass([1536, 2048, 2560], f"p2m{s}")
        evac(u2[i][:, 512:1024], 0, 512, [f"u2:{i}"])
        evac(xng_tm[:, :], 512, 1536, ["xng_tm"])
        p.op("pool", lambda e: e.tensor_mul(out=u2[i][:, :], in0=u2[i][:, :],
                                            in1=xng_tm[:, :]),
             reads=[f"u2:{i}", "xng_tm"], writes=[f"u2:{i}"])
        mm_pass([3072, 3584, 4096], f"p3m{s}")
        evac(g2[i][:, :], 0, 1024, [f"g2:{i}"])
        evac(arf2[i][:, 0:512], 1024, 1536, [f"arf:{i}"])
        mm_pass([4608, 5120, 5632], f"p4m{s}")
        evac(arf2[i][:, 512:1024], 0, 512, [f"arf:{i}"])
        evac(aif2[i][:, :], 512, 1536, [f"aif:{i}"])

    def be_a(s):
        i = s % 2
        p.op("act", lambda e: e.activation(out=fA[:, :], in_=arf2[i][:, :],
                                           func=AF.Square),
             reads=[f"arf:{i}"], writes=["fA"])
        p.op("act", lambda e: e.activation(out=fB[:, :], in_=aif2[i][:, :],
                                           func=AF.Square),
             reads=[f"aif:{i}"], writes=["fB"])
        p.op("pool", lambda e: e.tensor_add(out=fC[:, :], in0=fA[:, :],
                                            in1=fB[:, :]),
             reads=["fA", "fB"], writes=["fC"])
        p.op("act", lambda e: e.activation(out=fC[:, :], in_=fC[:, :],
                                           func=AF.Ln, bias=c38[:, :]),
             reads=["fC", "c38"], writes=["fC"])
        p.op("act", lambda e: e.activation(out=fC[:, :], in_=fC[:, :],
                                           func=AF.Exp, scale=0.5),
             reads=["fC"], writes=["fC"])
        p.op("dve", lambda e: e.tensor_add(out=fD[:, :], in0=fC[:, :],
                                           in1=arf2[i][:, :]),
             reads=["fC", f"arf:{i}"], writes=["fD"])
        p.op("act", lambda e: e.activation(out=fA[:, :], in_=fD[:, :],
                                           func=AF.Square),
             reads=["fD"], writes=["fA"])
        p.op("pool", lambda e: e.tensor_add(out=fA[:, :], in0=fA[:, :],
                                            in1=fB[:, :]),
             reads=["fA", "fB"], writes=["fA"])
        p.op("act", lambda e: e.activation(out=fA[:, :], in_=fA[:, :],
                                           func=AF.Ln, bias=c38[:, :]),
             reads=["fA", "c38"], writes=["fA"])
        p.op("act", lambda e: e.activation(out=fA[:, :], in_=fA[:, :],
                                           func=AF.Exp, scale=0.5),
             reads=["fA"], writes=["fA"])
        p.op("dve", lambda e: e.scalar_tensor_tensor(out=fD[:, :], in0=fA[:, :],
                                                     scalar=1e-12, in1=fD[:, :],
                                                     op0=OP.add, op1=OP.add),
             reads=["fA", "fD"], writes=["fD"])
        p.op("act", lambda e: e.activation(out=fE[:, :], in_=fC[:, :],
                                           func=AF.Exp, scale=-1.0),
             reads=["fC"], writes=["fE"])
        p.op("act", lambda e: e.activation(out=bb[4][:, :], in_=fE[:, :],
                                           func=AF.Ln, bias=1.0),
             reads=["fE"], writes=["b4"])
        p.op("act", lambda e: e.activation(out=fE[:, :], in_=fD[:, :], func=AF.Ln),
             reads=["fD"], writes=["fE"])
        p.op("act", lambda e: e.activation(out=fE[:, :], in_=fE[:, :],
                                           func=AF.Exp, scale=-1.0),
             reads=["fE"], writes=["fE"])
        p.op("dve", lambda e: e.tensor_mul(out=fC[:, :], in0=aif2[i][:, :],
                                           in1=fE[:, :]),
             reads=[f"aif:{i}", "fE"], writes=["fC"])
        p.op("dve", lambda e: e.tensor_copy(out=fD[:, :], in_=g2[i][:, :]),
             reads=[f"g2:{i}"], writes=["fD"])
        p.op("pool", lambda e: e.tensor_mul(out=bb[0][:, :], in0=q2[i][:, :],
                                            in1=g2[i][:, :]),
             reads=[f"q2:{i}", f"g2:{i}"], writes=["b0"])
        for h in range(2):
            p.op("pe", lambda e, h=h: e.matmul(psCum[:, h * 512:(h + 1) * 512],
                                               tri_m1[:, :],
                                               bb[4][:, h * 512:(h + 1) * 512],
                                               start=True, stop=True),
                 reads=["tri_m1", "b4"], writes=[f"psC:{h}"], tag=f"cs1{s}")
        for cc in range(8):
            p.op("pe", lambda e, cc=cc: e.matmul(
                psAux[:, cc * 2:cc * 2 + 2], bb[4][:, cc * 128:(cc + 1) * 128],
                cind[:, :], start=True, stop=True),
                reads=["b4", "cind"], writes=["psA:0"], tag=f"ag{s}")
        p.op("act", lambda e: e.activation(out=bb[7][:, :], in_=psCum[:, :],
                                           func=AF.Exp),
             reads=["psC:0", "psC:1"], writes=["b7"])
        p.op("act", lambda e: e.activation(out=fB[:, :], in_=psCum[:, :],
                                           func=AF.Exp, scale=-1.0),
             reads=["psC:0", "psC:1"], writes=["fB"])
        p.op("act", lambda e: e.activation(
            out=magt.ap().rearrange("p (k cc) -> p k cc", k=2),
            in_=psAux[:, 0:16].rearrange("p (cc k) -> p k cc", cc=8),
            func=AF.Exp, scale=-1.0),
            reads=["psA:0"], writes=["magt"])
        p.op("dve", lambda e: e.tensor_mul(out=fB[:, :], in0=fB[:, :],
                                           in1=u2[i][:, :]),
             reads=[f"u2:{i}", "fB"], writes=["fB"])

    def be_b(s):
        p.op("act", lambda e: e.activation(out=fC[:, :], in_=fC[:, :],
                                           func=AF.Arctan),
             reads=["fC"], writes=["fC"])
        p.op("act", lambda e: e.activation(out=fD[:, :], in_=fD[:, :],
                                           func=AF.Sigmoid),
             reads=["fD"], writes=["fD"])
        p.op("dve", lambda e: e.tensor_mul(out=bb[0][:, :], in0=bb[0][:, :],
                                           in1=fD[:, :]),
             reads=["b0", "fD"], writes=["b0"])
        p.op("dve", lambda e: e.tensor_copy(out=bb[5][:, :], in_=fC[:, :]),
             reads=["fC"], writes=["b5"])
        p.op("dve", lambda e: e.scalar_tensor_tensor(out=bb[6][:, :],
                                                     in0=bb[5][:, :], scalar=-1.0,
                                                     in1=fC[:, :], op0=OP.mult,
                                                     op1=OP.add),
             reads=["b5", "fC"], writes=["b6"])
        for cc in range(8):
            p.op("pe", lambda e, cc=cc: e.matmul(
                psAux[:, 16 + cc * 2:16 + cc * 2 + 2],
                bb[5][:, cc * 128:(cc + 1) * 128],
                cind[:, :], start=True, stop=False),
                reads=["b5", "cind"], writes=["psA:0"], tag=f"agt{s}")
            p.op("pe", lambda e, cc=cc: e.matmul(
                psAux[:, 16 + cc * 2:16 + cc * 2 + 2],
                bb[6][:, cc * 128:(cc + 1) * 128],
                cind[:, :], start=False, stop=True),
                reads=["b6", "cind"], writes=["psA:0"], tag=f"agt{s}")
        p.op("act", lambda e: e.activation(
            out=aggS[:, 16:32].rearrange("p (k cc) -> p k cc", k=2),
            in_=psAux[:, 16:32].rearrange("p (cc k) -> p k cc", cc=8),
            func=AF.Copy),
            reads=["psA:0"], writes=["aggS"])
        for h in range(2):
            p.op("pe", lambda e, h=h: e.matmul(psCum[:, h * 512:(h + 1) * 512],
                                               tri_p4[:, :],
                                               bb[5][:, h * 512:(h + 1) * 512],
                                               start=True, stop=False),
                 reads=["tri_p4", "b5"], writes=[f"psC:{h}"], tag=f"cs2{s}")
            p.op("pe", lambda e, h=h: e.matmul(psCum[:, h * 512:(h + 1) * 512],
                                               tri_p4[:, :],
                                               bb[6][:, h * 512:(h + 1) * 512],
                                               start=False, stop=True),
                 reads=["tri_p4", "b6"], writes=[f"psC:{h}"], tag=f"cs2{s}")
        p.op("dve", lambda e: e.tensor_scalar(out=fA.ap().bitcast(I32),
                                              in0=psCum[:, :],
                                              scalar1=1.0 / TWO_PI, scalar2=None,
                                              op0=OP.mult),
             reads=["psC:0", "psC:1"], writes=["fA"])
        p.op("dve", lambda e: e.scalar_tensor_tensor(out=fD[:, :],
                                                     in0=fA.ap().bitcast(I32),
                                                     scalar=-TWO_PI,
                                                     in1=psCum[:, :],
                                                     op0=OP.mult, op1=OP.add),
             reads=["fA", "psC:0", "psC:1"], writes=["fD"])
        p.op("dve", lambda e: e.tensor_scalar(out=fD[:, :], in0=fD[:, :],
                                              scalar1=6.2821, scalar2=-6.2821,
                                              op0=OP.min, op1=OP.max),
             reads=["fD"], writes=["fD"])
        p.op("dve", lambda e: e.tensor_scalar(out=tki[:, :], in0=aggS[:, 16:32],
                                              scalar1=4.0 / TWO_PI, scalar2=None,
                                              op0=OP.mult),
             reads=["aggS"], writes=["tki"])
        p.op("dve", lambda e: e.tensor_scalar(out=tt1[:, :], in0=aggS[:, 16:32],
                                              scalar1=4.0, scalar2=None,
                                              op0=OP.mult),
             reads=["aggS"], writes=["tt1"])
        p.op("dve", lambda e: e.scalar_tensor_tensor(out=tt1[:, :], in0=tki[:, :],
                                                     scalar=-TWO_PI, in1=tt1[:, :],
                                                     op0=OP.mult, op1=OP.add),
             reads=["tki", "tt1"], writes=["tt1"])
        p.op("dve", lambda e: e.tensor_scalar(out=tt1[:, :], in0=tt1[:, :],
                                              scalar1=6.2821, scalar2=-6.2821,
                                              op0=OP.min, op1=OP.max),
             reads=["tt1"], writes=["tt1"])
        p.op("act", lambda e: e.activation(out=fC[:, :], in_=fD[:, :],
                                           func=AF.Sin, scale=0.25),
             reads=["fD"], writes=["fC"])
        p.op("act", lambda e: e.activation(out=fE[:, :], in_=fD[:, :],
                                           func=AF.Sin, scale=0.25, bias=cPIH[:, :]),
             reads=["fD", "cPIH"], writes=["fE"])
        p.op("act", lambda e: e.activation(out=tt2[:, :], in_=tt1[:, :],
                                           func=AF.Sin, scale=0.25),
             reads=["tt1"], writes=["tt2"])
        p.op("act", lambda e: e.activation(out=tt3[:, :], in_=tt1[:, :],
                                           func=AF.Sin, scale=0.25, bias=cPIH[:, :]),
             reads=["tt1", "cPIH"], writes=["tt3"])
        p.op("act", lambda e: e.activation(out=fA[:, :], in_=fC[:, :],
                                           func=AF.Square),
             reads=["fC"], writes=["fA"])
        p.op("dve", lambda e: e.tensor_mul(out=fC[:, :], in0=fC[:, :],
                                           in1=fE[:, :]),
             reads=["fC", "fE"], writes=["fC"])
        p.op("dve", lambda e: e.tensor_scalar(out=fA[:, :], in0=fA[:, :],
                                              scalar1=-2.0, scalar2=1.0,
                                              op0=OP.mult, op1=OP.add),
             reads=["fA"], writes=["fA"])
        p.op("pool", lambda e: e.tensor_mul(out=fE[:, :], in0=fC[:, :],
                                            in1=fA[:, :]),
             reads=["fC", "fA"], writes=["fE"])
        p.op("dve", lambda e: e.tensor_scalar(out=bb[5][:, :], in0=fE[:, :],
                                              scalar1=4.0, scalar2=None,
                                              op0=OP.mult),
             reads=["fE"], writes=["b5"])
        p.op("act", lambda e: e.activation(out=fA[:, :], in_=fC[:, :],
                                           func=AF.Square),
             reads=["fC"], writes=["fA"])
        p.op("dve", lambda e: e.tensor_scalar(out=bb[6][:, :], in0=fA[:, :],
                                              scalar1=-8.0, scalar2=1.0,
                                              op0=OP.mult, op1=OP.add),
             reads=["fA"], writes=["b6"])
        p.op("act", lambda e: e.activation(out=tt4[:, :], in_=tt2[:, :],
                                           func=AF.Square),
             reads=["tt2"], writes=["tt4"])
        p.op("dve", lambda e: e.tensor_mul(out=tt2[:, :], in0=tt2[:, :],
                                           in1=tt3[:, :]),
             reads=["tt2", "tt3"], writes=["tt2"])
        p.op("dve", lambda e: e.tensor_scalar(out=tt3[:, :], in0=tt4[:, :],
                                              scalar1=-2.0, scalar2=1.0,
                                              op0=OP.mult, op1=OP.add),
             reads=["tt4"], writes=["tt3"])
        p.op("dve", lambda e: e.tensor_mul(out=tt1[:, :], in0=tt2[:, :],
                                           in1=tt3[:, :]),
             reads=["tt2", "tt3"], writes=["tt1"])
        p.op("dve", lambda e: e.tensor_scalar(out=tt1[:, :], in0=tt1[:, :],
                                              scalar1=4.0, scalar2=None,
                                              op0=OP.mult),
             reads=["tt1"], writes=["tt1"])
        p.op("act", lambda e: e.activation(out=tt4[:, :], in_=tt2[:, :],
                                           func=AF.Square),
             reads=["tt2"], writes=["tt4"])
        p.op("dve", lambda e: e.tensor_scalar(out=tt4[:, :], in0=tt4[:, :],
                                              scalar1=-8.0, scalar2=1.0,
                                              op0=OP.mult, op1=OP.add),
             reads=["tt4"], writes=["tt4"])
        p.op("dve", lambda e: e.tensor_mul(out=bb[4][:, :], in0=fB[:, :],
                                           in1=bb[6][:, :]),
             reads=["fB", "b6"], writes=["b4"])
        p.op("dve", lambda e: e.tensor_mul(out=bb[9][:, :], in0=fB[:, :],
                                           in1=bb[5][:, :]),
             reads=["fB", "b5"], writes=["b9"])
        p.op("pool", lambda e: e.tensor_mul(out=bb[1][:, :], in0=bb[7][:, :],
                                            in1=bb[6][:, :]),
             reads=["b7", "b6"], writes=["b1"])
        p.op("pool", lambda e: e.tensor_mul(out=bb[2][:, :], in0=bb[7][:, :],
                                            in1=bb[5][:, :]),
             reads=["b7", "b5"], writes=["b2"])
        for cc in range(8):
            p.op("pe", lambda e, cc=cc: e.matmul(
                psAux[:, 32 + cc * 2:32 + cc * 2 + 2],
                bb[4][:, cc * 128:(cc + 1) * 128], cind[:, :],
                start=True, stop=True),
                reads=["b4", "cind"], writes=["psA:0"], tag=f"ag2{s}")
            p.op("pe", lambda e, cc=cc: e.matmul(
                psAux[:, 48 + cc * 2:48 + cc * 2 + 2],
                bb[9][:, cc * 128:(cc + 1) * 128], cind[:, :],
                start=True, stop=True),
                reads=["b9", "cind"], writes=["psA:0"], tag=f"ag2{s}")
        p.op("act", lambda e: e.activation(
            out=aggS[:, 32:64].rearrange("p (g k cc) -> p g k cc", g=2, k=2),
            in_=psAux[:, 32:64].rearrange("p (g cc k) -> p g k cc", g=2, cc=8),
            func=AF.Copy),
            reads=["psA:0"], writes=["aggS"])
        for h in range(2):
            p.op("pe", lambda e, h=h: e.matmul(psCum[:, h * 512:(h + 1) * 512],
                                               tri_p1[:, :],
                                               bb[4][:, h * 512:(h + 1) * 512],
                                               start=True, stop=True),
                 reads=["tri_p1", "b4"], writes=[f"psC:{h}"], tag=f"cs3{s}")
            p.op("pe", lambda e, h=h: e.matmul(psAux[:, h * 512:(h + 1) * 512],
                                               tri_m1[:, :],
                                               bb[9][:, h * 512:(h + 1) * 512],
                                               start=True, stop=True),
                 reads=["tri_m1", "b9"],
                 writes=["psA:0"] if h == 0 else ["psA:1"], tag=f"cs3{s}")
        p.op("dve", lambda e: e.tensor_mul(out=fA[:, :], in0=bb[1][:, :],
                                           in1=psCum[:, :]),
             reads=["b1", "psC:0", "psC:1"], writes=["fA"])
        p.op("dve", lambda e: e.tensor_mul(out=fB[:, :], in0=bb[2][:, :],
                                           in1=psAux[:, 0:1024]),
             reads=["b2", "psA:0", "psA:1"], writes=["fB"])
        p.op("dve", lambda e: e.tensor_sub(out=fA[:, :], in0=fA[:, :],
                                           in1=fB[:, :]),
             reads=["fA", "fB"], writes=["fA"])
        p.op("pool", lambda e: e.tensor_mul(out=bb[5][:, :], in0=bb[0][:, :],
                                            in1=bb[1][:, :]),
             reads=["b0", "b1"], writes=["b5"])
        p.op("pool", lambda e: e.tensor_mul(out=bb[7][:, :], in0=bb[0][:, :],
                                            in1=bb[2][:, :]),
             reads=["b0", "b2"], writes=["b7"])
        p.op("dve", lambda e: e.tensor_mul(out=bb[0][:, :], in0=bb[0][:, :],
                                           in1=fA[:, :]),
             reads=["b0", "fA"], writes=["b0"])
        rsp = ["qsp", "qact"][s % 2]
        p.dma(rsp, lambda e: e.dma_start(out=ylocD[s], in_=bb[0][:, :]),
              reads=["b0"], writes=[f"ylD:{s}"], tag=f"sp{s}")
        p.dma(rsp, lambda e: e.dma_start(out=a1D[s], in_=bb[5][:, :]),
              reads=["b5"], writes=[f"a1D:{s}"], tag=f"sp{s}")
        p.dma(rsp, lambda e: e.dma_start(out=a2D[s], in_=bb[7][:, :]),
              reads=["b7"], writes=[f"a2D:{s}"], tag=f"sp{s}")
        p.op("dve", lambda e: e.tensor_mul(out=acr_t[:, :], in0=magt[:, :],
                                           in1=tt4[:, :]),
             reads=["magt", "tt4"], writes=["acr_t"])
        p.op("dve", lambda e: e.tensor_mul(out=aci_t[:, :], in0=magt[:, :],
                                           in1=tt1[:, :]),
             reads=["magt", "tt1"], writes=["aci_t"])
        p.op("dve", lambda e: e.tensor_mul(out=slr_t[:, :], in0=acr_t[:, :],
                                           in1=aggS[:, 32:48]),
             reads=["acr_t", "aggS"], writes=["slr_t"])
        p.op("dve", lambda e: e.tensor_mul(out=tt1[:, :], in0=aci_t[:, :],
                                           in1=aggS[:, 48:64]),
             reads=["aci_t", "aggS"], writes=["tt1"])
        p.op("dve", lambda e: e.tensor_add(out=slr_t[:, :], in0=slr_t[:, :],
                                           in1=tt1[:, :]),
             reads=["slr_t", "tt1"], writes=["slr_t"])
        p.op("dve", lambda e: e.tensor_mul(out=sli_t[:, :], in0=aci_t[:, :],
                                           in1=aggS[:, 32:48]),
             reads=["aci_t", "aggS"], writes=["sli_t"])
        p.op("dve", lambda e: e.tensor_mul(out=tt1[:, :], in0=acr_t[:, :],
                                           in1=aggS[:, 48:64]),
             reads=["acr_t", "aggS"], writes=["tt1"])
        p.op("dve", lambda e: e.tensor_sub(out=sli_t[:, :], in0=sli_t[:, :],
                                           in1=tt1[:, :]),
             reads=["sli_t", "tt1"], writes=["sli_t"])
        for k in range(2):
            c = 2 * s + k
            lo, hi = k * 8, (k + 1) * 8
            co, co1 = c * 8, (c + 1) * 8

            def cop(fn, out_ap, a_ap, b_ap, rd, wr):
                p.op("dve",
                     lambda e, fn=fn, out_ap=out_ap, a_ap=a_ap, b_ap=b_ap:
                     getattr(e, fn)(out=out_ap, in0=a_ap, in1=b_ap),
                     reads=rd, writes=wr)

            cop("tensor_mul", ch1[:, :], acr_t[:, lo:hi], hP_r[:, co:co + 8],
                ["acr_t", f"hP:{c}"], ["ch1"])
            cop("tensor_mul", ch2[:, :], aci_t[:, lo:hi], hP_i[:, co:co + 8],
                ["aci_t", f"hPi:{c}"], ["ch2"])
            cop("tensor_sub", ch1[:, :], ch1[:, :], ch2[:, :],
                ["ch1", "ch2"], ["ch1"])
            cop("tensor_add", hP_r[:, co1:co1 + 8], ch1[:, :], slr_t[:, lo:hi],
                ["ch1", "slr_t"], [f"hP:{c + 1}"])
            cop("tensor_mul", ch1[:, :], acr_t[:, lo:hi], hP_i[:, co:co + 8],
                ["acr_t", f"hPi:{c}"], ["ch1"])
            cop("tensor_mul", ch2[:, :], aci_t[:, lo:hi], hP_r[:, co:co + 8],
                ["aci_t", f"hP:{c}"], ["ch2"])
            cop("tensor_add", ch1[:, :], ch1[:, :], ch2[:, :],
                ["ch1", "ch2"], ["ch1"])
            cop("tensor_add", hP_i[:, co1:co1 + 8], ch1[:, :], sli_t[:, lo:hi],
                ["ch1", "sli_t"], [f"hPi:{c + 1}"])
            if c + 1 < NCH:
                cop("tensor_mul", ch1[:, :], acr_t[:, lo:hi], PP_r[:, co:co + 8],
                    ["acr_t", f"PP:{c}"], ["ch1"])
                cop("tensor_mul", ch2[:, :], aci_t[:, lo:hi], PP_i[:, co:co + 8],
                    ["aci_t", f"PPi:{c}"], ["ch2"])
                cop("tensor_sub", PP_r[:, co1:co1 + 8], ch1[:, :], ch2[:, :],
                    ["ch1", "ch2"], [f"PP:{c + 1}"])
                cop("tensor_mul", ch1[:, :], acr_t[:, lo:hi], PP_i[:, co:co + 8],
                    ["acr_t", f"PPi:{c}"], ["ch1"])
                cop("tensor_mul", ch2[:, :], aci_t[:, lo:hi], PP_r[:, co:co + 8],
                    ["aci_t", f"PP:{c}"], ["ch2"])
                cop("tensor_add", PP_i[:, co1:co1 + 8], ch1[:, :], ch2[:, :],
                    ["ch1", "ch2"], [f"PPi:{c + 1}"])

    fe(0)
    fe(1)
    be_a(0)
    for s in range(NSL):
        if s + 2 < NSL:
            fe(s + 2)
        be_b(s)
        if s + 1 < NSL:
            be_a(s + 1)
        if s < 8:
            plan_sums(8)
            if s == 7:
                plan_sums(100)
                plan_wscale()
                if _KDQ:
                    for i_, (tn, kk) in enumerate([(mc_w1, "mc_w1"), (mc_w2, "mc_w2"),
                                                   (acc_w1, "acc_w1"), (acc_w2, "acc_w2"),
                                                   (sw_w1, "sw_w1"), (sw_w2, "sw_w2")]):
                        p.dma("qsp", lambda e, tn=tn, i_=i_: e.dma_start(
                            out=dbgq_d[0, :, i_:i_ + 1], in_=tn[:, :],
                            allow_slow_non_contiguous=True),
                            reads=[kk], writes=["dbgq:0"], tag="dbq")
        else:
            plan_slabs(9)
    plan_slabs(100)

    p.op("dve", lambda e: e.tensor_copy(out=Hpack[:, 0:8],
                                        in_=hP_r[:, NCH * 8:(NCH + 1) * 8]),
         reads=[f"hP:{NCH}"], writes=["Hpack"])
    p.op("dve", lambda e: e.tensor_copy(out=Hpack[:, 8:16],
                                        in_=hP_i[:, NCH * 8:(NCH + 1) * 8]),
         reads=[f"hPi:{NCH}"], writes=["Hpack"])
    p.dma("qpool", lambda e: e.dma_start(out=ccin_d, in_=Hpack[:, :]),
          reads=["Hpack"], writes=["ccin"], tag="cc")
    p.op("pool", lambda e: e.collective_compute(
        "AllGather", OP.bypass, ins=[ccin_d], outs=[ccout_d],
        replica_groups=[[0, 1, 2, 3, 4, 5, 6, 7]]),
        reads=["ccin"], writes=["ccout"], tag="cc")
    p.dma("qpool", lambda e: e.dma_start(
        out=gatc.ap().rearrange("p (r j) -> p r j", r=8),
        in_=ccout_d.rearrange("r (p j) -> p r j", p=128)),
        reads=["ccout"], writes=["gatc"], tag="cc")
    p.op("dve", lambda e: e.tensor_mul(out=gatc[:, :], in0=gatc[:, :],
                                       in1=selrep[:, :]),
         reads=["gatc", "selrep"], writes=["gatc"])
    p.op("dve", lambda e: e.tensor_copy(out=Hprev[:, :], in_=gatc[:, 0:16]),
         reads=["gatc"], writes=["Hprev"])
    for r_ in range(1, 8):
        p.op("dve", lambda e, r_=r_: e.tensor_add(
            out=Hprev[:, :], in0=Hprev[:, :], in1=gatc[:, r_ * 16:(r_ + 1) * 16]),
            reads=["gatc", "Hprev"], writes=["Hprev"])
    for c in range(NCH):
        p.op("dve", lambda e, c=c: e.tensor_copy(out=HrR[:, c * 8:(c + 1) * 8],
                                                 in_=Hprev[:, 0:8]),
             reads=["Hprev"], writes=["HrR"])
        p.op("dve", lambda e, c=c: e.tensor_copy(out=HiR[:, c * 8:(c + 1) * 8],
                                                 in_=Hprev[:, 8:16]),
             reads=["Hprev"], writes=["HiR"])
    hp_reads = ([f"hP:{c}" for c in range(NCH)] +
                [f"hPi:{c}" for c in range(NCH)] +
                [f"PP:{c}" for c in range(NCH)] +
                [f"PPi:{c}" for c in range(NCH)])
    httR_o = httR.ap().rearrange("p (cc c) -> p cc c", cc=8).transpose([0, 2, 1])
    httI_o = httI.ap().rearrange("p (cc c) -> p cc c", cc=8).transpose([0, 2, 1])
    fA3 = fA.ap().rearrange("p (c cc) -> p c cc", cc=8)
    hPr3 = hP_r.ap().rearrange("p (c cc) -> p c cc", cc=8)
    hPi3 = hP_i.ap().rearrange("p (c cc) -> p c cc", cc=8)
    p.op("dve", lambda e: e.tensor_mul(out=fA[:, 0:256], in0=PP_r[:, 0:256],
                                       in1=HrR[:, :]),
         reads=hp_reads + ["HrR"], writes=["fA"])
    p.op("dve", lambda e: e.tensor_mul(out=fB[:, 0:256], in0=PP_i[:, 0:256],
                                       in1=HiR[:, :]),
         reads=["HiR"] + hp_reads, writes=["fB"])
    p.op("dve", lambda e: e.tensor_sub(out=fA[:, 0:256], in0=fA[:, 0:256],
                                       in1=fB[:, 0:256]),
         reads=["fA", "fB"], writes=["fA"])
    p.op("dve", lambda e: e.tensor_tensor(out=httR_o, in0=fA3[:, 0:32, :],
                                          in1=hPr3[:, 0:32, :], op=OP.add),
         reads=["fA"] + hp_reads, writes=["httR"])
    p.op("dve", lambda e: e.tensor_mul(out=fA[:, 0:256], in0=PP_r[:, 0:256],
                                       in1=HiR[:, :]),
         reads=hp_reads + ["HiR", "httR"], writes=["fA"])
    p.op("dve", lambda e: e.tensor_mul(out=fB[:, 0:256], in0=PP_i[:, 0:256],
                                       in1=HrR[:, :]),
         reads=["HrR"] + hp_reads, writes=["fB"])
    p.op("dve", lambda e: e.tensor_add(out=fA[:, 0:256], in0=fA[:, 0:256],
                                       in1=fB[:, 0:256]),
         reads=["fA", "fB"], writes=["fA"])
    p.op("dve", lambda e: e.tensor_tensor(out=httI_o, in0=fA3[:, 0:32, :],
                                          in1=hPi3[:, 0:32, :], op=OP.add),
         reads=["fA"] + hp_reads, writes=["httI"])
    for cc in range(8):
        p.op("pe", lambda e, cc=cc: e.transpose(
            psTrB[0:32, 0:128], httR[:, cc * 32:(cc + 1) * 32], ident[:, :]),
            reads=["httR", "ident", "psTrB"], writes=["psTrB"], tag="htr")
        p.op("act", lambda e, cc=cc: e.activation(
            out=hrt2R[0:32, cc * 128:(cc + 1) * 128], in_=psTrB[0:32, 0:128],
            func=AF.Copy),
            reads=["psTrB"], writes=["hrt2R"], tag="htr")
        p.op("pe", lambda e, cc=cc: e.transpose(
            psTrB[0:32, 0:128], httI[:, cc * 32:(cc + 1) * 32], ident[:, :]),
            reads=["httI", "ident", "psTrB"], writes=["psTrB"], tag="htr")
        p.op("act", lambda e, cc=cc: e.activation(
            out=hrt2I[0:32, cc * 128:(cc + 1) * 128], in_=psTrB[0:32, 0:128],
            func=AF.Copy),
            reads=["psTrB"], writes=["hrt2I"], tag="htr")

    s1.close()
    p.fence()
    s2 = contextlib.ExitStack()
    w2res = sb(s2, "w2res", [128, 32 * D], BF16)
    b2r16 = sb(s2, "b2r16", [1, D], BF16)
    b2st = sb(s2, "b2st", [1, D], F32)
    p.dma("qsp", lambda e: e.dma_start(out=b2st[:, :], in_=b2r_d),
          writes=["b2st"], tag="b2l")
    p.op("act", lambda e: e.activation(out=b2r16[:, :], in_=b2st[:, :],
                                       func=AF.Identity, scale=sw_w2[0:1, :]),
         reads=["b2st", "sw_w2"], writes=["b2r16"])
    for fc in range(32):
        p.dma(["qsp", "qact"][fc % 2], lambda e, fc=fc: e.dma_start(
            out=w2res[:, fc * D:(fc + 1) * D],
            in_=w2t16_d[fc * 128:(fc + 1) * 128, :]),
            reads=[f"slab:w2:{fc}"], writes=["w2res"], tag="w2r")
    yattn = sb(s2, "yattn", [128, NSL * D], BF16)
    s2a = contextlib.ExitStack()
    wo16s = sb(s2a, "wo16s", [128, 8 * D], BF16)
    ylb = sb(s2a, "ylb", [128, D], BF16)
    a1b = sb(s2a, "a1b", [128, D], BF16)
    a2b = sb(s2a, "a2b", [128, D], BF16)
    ypb = sb(s2a, "ypb", [128, D], BF16)
    ypfm = sb(s2a, "ypfm", [128, D], BF16)

    for fc in range(8):
        p.dma("qpool", lambda e, fc=fc: e.dma_start(
            out=wo16s[:, fc * D:(fc + 1) * D],
            in_=wo_d[fc * 128:(fc + 1) * 128, :]),
            writes=["wo16s"], tag="wold")

    _KSKIP = set(_os.environ.get("KSKIP", "").split(","))
    _KP4 = False

    def p4_slice(s):
        rr = ["qsp", "qact"][s % 2]
        p.dma(rr, lambda e: e.dma_start(out=ylb[:, :], in_=ylocD[s]),
              reads=[f"ylD:{s}", "ylb"], writes=["ylb"], tag=f"rl{s}")
        p.dma(rr, lambda e: e.dma_start(out=a1b[:, :], in_=a1D[s]),
              reads=[f"a1D:{s}", "a1b"], writes=["a1b"], tag=f"rl{s}")
        p.dma(rr, lambda e: e.dma_start(out=a2b[:, :], in_=a2D[s]),
              reads=[f"a2D:{s}", "a2b"], writes=["a2b"], tag=f"rl{s}")
        for h in range(2):
            p.op("pe", lambda e, h=h: e.matmul(
                psCum[:, h * 512:(h + 1) * 512], chselA[:, s * 128:(s + 1) * 128],
                hrt2R[:, h * 512:(h + 1) * 512], start=True, stop=True),
                reads=["chselA", "hrt2R"], writes=[f"psC:{h}"], tag=f"hr{s}")
            p.op("pe", lambda e, h=h: e.matmul(
                psMain[:, h * 512:(h + 1) * 512], chselA[:, s * 128:(s + 1) * 128],
                hrt2I[:, h * 512:(h + 1) * 512], start=True, stop=True),
                reads=["chselA", "hrt2I"], writes=[f"psM:{h}"], tag=f"hr{s}")
        p.op("dve", lambda e: e.tensor_mul(out=fA[:, :], in0=a1b[:, :],
                                           in1=psCum[:, :]),
             reads=["a1b", "psC:0", "psC:1"], writes=["fA"])
        p.op("dve", lambda e: e.tensor_mul(out=fB[:, :], in0=a2b[:, :],
                                           in1=psMain[:, 0:1024]),
             reads=["a2b", "psM:0", "psM:1"], writes=["fB"])
        p.op("dve", lambda e: e.tensor_sub(out=fA[:, :], in0=fA[:, :],
                                           in1=fB[:, :]),
             reads=["fA", "fB"], writes=["fA"])
        if "corr" in _KSKIP:
            p.op("dve", lambda e: e.tensor_copy(out=ypb[:, :], in_=ylb[:, :]),
                 reads=["ylb"], writes=["ypb"])
        else:
            p.op("dve", lambda e: e.tensor_add(out=ypb[:, :], in0=ylb[:, :],
                                               in1=fA[:, :]),
                 reads=["ylb", "fA"], writes=["ypb"])
        for dc in range(8):
            p.op("pe", lambda e, dc=dc: e.transpose(
                psTrB[:, dc * 128:(dc + 1) * 128],
                ypb[:, dc * 128:(dc + 1) * 128], ident[:, :]),
                reads=["ypb", "ident", "psTrB"], writes=["psTrB"], tag=f"ytr{s}")
        p.op("act", lambda e: e.activation(out=ypfm[:, :], in_=psTrB[:, :],
                                           func=AF.Copy),
             reads=["psTrB"], writes=["ypfm"], tag=f"ytre{s}")
        for fc in range(8):
            for h in range(2):
                p.op("pe", lambda e, fc=fc, h=h: e.matmul(
                    psAux[:, h * 512:(h + 1) * 512],
                    ypfm[:, fc * 128:(fc + 1) * 128],
                    wo16s[:, fc * D + h * 512: fc * D + (h + 1) * 512],
                    start=(fc == 0), stop=(fc == 7)),
                    reads=["ypfm", "wo16s"], writes=[f"psA:{h}"], tag=f"wo{s}")
        p.op("act", lambda e: e.activation(out=yattn[:, s * D:(s + 1) * D],
                                           in_=psAux[:, :], func=AF.Copy),
             reads=["psA:0", "psA:1"], writes=[f"yattn:{s}"], tag=f"woe{s}")

    for s in range(NSL):
        p4_slice(s)

    s2a.close()
    p.fence()
    xqfm = sb(s2, "xqfm", [128, 8 * 512], BF16)
    hg_g = sb(s2, "hg_g", [128, 32 * 512], BF16)
    hsqb = sb(s2, "hsqb", [128, 512], BF16)
    mxb = sb(s2, "mxb", [128, 512], BF16)
    mnb = sb(s2, "mnb", [128, 512], BF16)
    w1sl = sb(s2, "w1sl", [128, 2 * 1024], BF16)
    xq_tm = sb(s2, "xq_tm", [128, D], BF16)
    xs2 = sb(s2, "xs2", [128, D], F32)
    mrow = sb(s2, "mrow", [1, 512], F32)
    mrowb = sb(s2, "mrowb", [1, 512], BF16)
    mrowb2 = sb(s2, "mrowb2", [1, 512], BF16)
    r1 = sb(s2, "r1", [1, 512], F32)
    r2 = sb(s2, "r2", [1, 512], F32)
    c2rep = sb(s2, "c2rep", [128, 512], BF16)
    i2rep = sb(s2, "i2rep", [128, 512], BF16)
    q16i = sb(s2, "q16i", [128, 512], I32)
    mxcol = sb(s2, "mxcol", [128, 4], F32)

    _KSKIP = set(_os.environ.get("KSKIP", "").split(","))

    def p5_group(g):
        for j in range(4):
            s = g * 4 + j
            p.dma("qsp", lambda e, s=s: e.dma_start(
                out=xs[:, :], in_=x_d[s * 128:(s + 1) * 128, :]),
                reads=["xs"], writes=["xs"], tag=f"x2{s}")
            p.op("act", lambda e, s=s: e.activation(
                out=xs[:, :], in_=xs[:, :], func=AF.Identity,
                scale=istd_a[:, s:s + 1], bias=nmu_a[:, s:s + 1]),
                reads=["xs", f"istd:{s}", f"nmu:{s}"], writes=["xs"])
            p.op("act", lambda e, s=s: e.activation(
                out=fA[:, :], in_=xs[:, :], func=AF.Identity,
                scale=ct_a[:, s:s + 1]),
                reads=["xs", f"ct:{s}"], writes=["fA"])
            p.op("dve", lambda e: e.tensor_copy(out=fB.ap().bitcast(I32),
                                                in_=fA[:, :]),
                 reads=["fA", "fB"], writes=["fB"])
            p.op("dve", lambda e: e.tensor_scalar(out=fA[:, :],
                                                  in0=fB.ap().bitcast(I32),
                                                  scalar1=127.0, scalar2=None,
                                                  op0=OP.min),
                 reads=["fB"], writes=["fA"])
            p.op("act", lambda e, s=s: e.activation(
                out=xq_tm[:, :], in_=fA[:, :], func=AF.Identity,
                scale=invs_a[:, s:s + 1]),
                reads=["fA", f"invs:{s}"], writes=["xq_tm"])
            for dc in range(8):
                p.op("pe", lambda e, dc=dc: e.transpose(
                    psTrB[:, dc * 128:(dc + 1) * 128],
                    xq_tm[:, dc * 128:(dc + 1) * 128], ident[:, :]),
                    reads=["xq_tm", "ident", "psTrB"], writes=["psTrB"],
                    tag=f"xqt{s}")
            p.op("act", lambda e, j=j: e.activation(
                out=xqfm.ap().rearrange("p (c t) -> p c t", c=8)[:, :,
                                                                j * 128:(j + 1) * 128],
                in_=psTrB.ap().rearrange("p (c t) -> p c t", c=8),
                func=AF.Copy),
                reads=["psTrB"], writes=[f"xqfm:{j}"], tag=f"xqe{s}")
        xq_reads = [f"xqfm:{j}" for j in range(4)]
        for fc in range(32 if "ffn1" not in _KSKIP else 0):
            sl = fc % 2
            p.dma(["qsp", "qact"][sl], lambda e, fc=fc, sl=sl: e.dma_start(
                out=w1sl[:, sl * 1024:(sl + 1) * 1024], in_=w1t16_d[fc]),
                reads=[f"slab:w1:{fc}", f"w1sl:{sl}"], writes=[f"w1sl:{sl}"],
                tag=f"w1l{g}")
            for dc in range(8):
                p.op("pe", lambda e, fc=fc, sl=sl, dc=dc: e.matmul(
                    psMain[:, 0:512],
                    w1sl[:, sl * 1024 + dc * 128: sl * 1024 + (dc + 1) * 128],
                    xqfm[:, dc * 512:(dc + 1) * 512],
                    start=(dc == 0), stop=(dc == 7)),
                    reads=[f"w1sl:{sl}"] + xq_reads, writes=["psM:0"],
                    tag=f"f1m{g}")
            p.op("act", lambda e, fc=fc: e.activation(
                out=hg_g[:, fc * 512:(fc + 1) * 512], in_=psMain[:, 0:512],
                func=AF.Gelu, scale=mc_w1[:, :], bias=b1c[:, fc:fc + 1]),
                reads=["psM:0", "b1c", "mc_w1"], writes=[f"hg:{fc}"],
                tag=f"f1e{g}")
            p.op("act", lambda e, fc=fc: e.activation(
                out=hsqb[:, :], in_=hg_g[:, fc * 512:(fc + 1) * 512],
                func=AF.Square),
                reads=[f"hg:{fc}", "hsqb"], writes=["hsqb"])
            p.op("pe", lambda e, fc=fc: e.matmul(
                psMain[:, 512:1024], onesr[:, :], hsqb[:, :],
                start=(fc == 0), stop=(fc == 31)),
                reads=["hsqb", "onesr"], writes=["psM:1"], tag=f"sq{g}")
            if fc == 0:
                p.op("dve", lambda e: e.tensor_copy(out=mxb[:, :],
                                                    in_=hg_g[:, 0:512]),
                     reads=["hg:0", "mxb"], writes=["mxb"])
                p.op("dve", lambda e: e.tensor_copy(out=mnb[:, :],
                                                    in_=hg_g[:, 0:512]),
                     reads=["hg:0", "mnb"], writes=["mnb"])
            else:
                p.op("dve", lambda e, fc=fc: e.tensor_tensor(
                    out=mxb[:, :], in0=mxb[:, :],
                    in1=hg_g[:, fc * 512:(fc + 1) * 512], op=OP.max),
                    reads=["mxb", f"hg:{fc}"], writes=["mxb"])
                p.op("dve", lambda e, fc=fc: e.tensor_tensor(
                    out=mnb[:, :], in0=mnb[:, :],
                    in1=hg_g[:, fc * 512:(fc + 1) * 512], op=OP.min),
                    reads=["mnb", f"hg:{fc}"], writes=["mnb"])
        if "stats" in _KSKIP:
            return
        p.op("dve", lambda e: e.tensor_scalar(out=mnb[:, :], in0=mnb[:, :],
                                              scalar1=-1.0, scalar2=None,
                                              op0=OP.mult),
             reads=["mnb"], writes=["mnb"])
        p.op("dve", lambda e: e.tensor_max(out=mxb[:, :], in0=mxb[:, :],
                                           in1=mnb[:, :]),
             reads=["mxb", "mnb"], writes=["mxb"])
        for t_ in range(4):
            p.op("pe", lambda e, t_=t_: e.transpose(
                psTrB[:, 0:128], mxb[:, t_ * 128:(t_ + 1) * 128], ident[:, :]),
                reads=["mxb", "ident", "psTrB"], writes=["psTrB"], tag=f"mxt{g}")
            p.op("dve", lambda e, t_=t_: e.tensor_reduce(
                out=mxcol[:, t_:t_ + 1], in_=psTrB[:, 0:128], axis=AX,
                op=OP.max),
                reads=["psTrB"], writes=["mxcol"])
        for t_ in range(4):
            p.dma("qpool", lambda e, t_=t_: e.dma_start(
                out=mrow[0:1, t_ * 128:(t_ + 1) * 128],
                in_=mxcol[:, t_:t_ + 1]),
                reads=["mxcol"], writes=["mrow"], tag=f"mx{g}")
        p.op("act", lambda e: e.activation(out=r1[:, :], in_=psMain[0:1, 512:1024],
                                           func=AF.Sqrt, scale=1.0 / FF, bias=cEPS[0:1, :]),
             reads=["psM:1", "cEPS"], writes=["r1"])
        p.op("dve", lambda e: e.reciprocal(out=r1[:, :], in_=r1[:, :]),
             reads=["r1"], writes=["r1"])
        p.op("dve", lambda e: e.tensor_mul(out=r2[:, :], in0=mrow[:, :],
                                           in1=r1[:, :]),
             reads=["mrow", "r1"], writes=["r2"])
        p.op("dve", lambda e: e.tensor_scalar(out=r2[:, :], in0=r2[:, :],
                                              scalar1=1e-5, scalar2=None,
                                              op0=OP.max),
             reads=["r2"], writes=["r2"])
        p.op("dve", lambda e: e.tensor_scalar(out=mrowb2[:, :], in0=r2[:, :],
                                              scalar1=1.0 / 127.0, scalar2=None,
                                              op0=OP.mult),
             reads=["r2"], writes=["mrowb2"])
        p.op("dve", lambda e: e.reciprocal(out=r2[:, :], in_=r2[:, :]),
             reads=["r2"], writes=["r2"])
        p.op("dve", lambda e: e.tensor_mul(out=r2[:, :], in0=r2[:, :],
                                           in1=r1[:, :]),
             reads=["r2", "r1"], writes=["r2"])
        p.op("dve", lambda e: e.tensor_scalar(out=mrowb[:, :], in0=r2[:, :],
                                              scalar1=127.0, scalar2=None,
                                              op0=OP.mult),
             reads=["r2"], writes=["mrowb"])
        p.op("pe", lambda e: e.matmul(psAux[:, 0:512], onesr[0:1, :],
                                      mrowb[:, :], start=True, stop=True),
             reads=["onesr", "mrowb"], writes=["psA:0"], tag=f"rep{g}")
        p.op("pe", lambda e: e.matmul(psAux[:, 512:1024], onesr[0:1, :],
                                      mrowb2[:, :], start=True, stop=True),
             reads=["onesr", "mrowb2"], writes=["psA:1"], tag=f"rep{g}")
        p.op("act", lambda e: e.activation(out=c2rep[:, :], in_=psAux[:, 0:512],
                                           func=AF.Copy),
             reads=["psA:0"], writes=["c2rep"])
        p.op("act", lambda e: e.activation(out=i2rep[:, :], in_=psAux[:, 512:1024],
                                           func=AF.Copy),
             reads=["psA:1"], writes=["i2rep"])
        for fc in range(32 if "quant" not in _KSKIP else 0):
            p.op("dve", lambda e, fc=fc: e.tensor_tensor(
                out=q16i[:, :], in0=hg_g[:, fc * 512:(fc + 1) * 512],
                in1=c2rep[:, :], op=OP.mult),
                reads=[f"hg:{fc}", "c2rep"], writes=["q16i"])
            p.op("dve", lambda e, fc=fc: e.scalar_tensor_tensor(
                out=hg_g[:, fc * 512:(fc + 1) * 512], in0=q16i[:, :],
                scalar=127.0, in1=i2rep[:, :], op0=OP.min, op1=OP.mult),
                reads=["q16i", "i2rep"], writes=[f"hg:{fc}"])
        if _KDQ and g == 0:
            p.dma("qpool", lambda e: e.dma_start(out=dbgq_d[1], in_=ylocD[0]),
                  reads=["ylD:0"], writes=["dbgq:1"], tag="dbq")
            p.dma("qpool", lambda e: e.dma_start(out=dbgq_d[2], in_=yattn[:, 0:1024]),
                  reads=["yattn:0"], writes=["dbgq:2"], tag="dbq")
            p.dma("qpool", lambda e: e.dma_start(out=dbgq_d[3, 0:32, :],
                                                 in_=hrt2R[:, :]),
                  reads=["hrt2R"], writes=["dbgq:3"], tag="dbq")
            p.dma("qpool", lambda e: e.dma_start(out=dbgq_d[3, 32:64, :],
                                                 in_=a1D[0, 0:32, :]),
                  reads=["a1D:0"], writes=["dbgq:3"], tag="dbq")
            p.dma("qpool", lambda e: e.dma_start(out=dbgq_d[3, 64:96, :],
                                                 in_=a2D[0, 0:32, :]),
                  reads=["a2D:0"], writes=["dbgq:3"], tag="dbq")
            p.dma("qpool", lambda e: e.dma_start(out=dbgq_d[3, 96:128, :],
                                                 in_=ylocD[15, 0:32, :]),
                  reads=["ylD:15"], writes=["dbgq:3"], tag="dbq")
        for half in range(2 if "ffn2" not in _KSKIP else 0):
            psO = [psCum, psAux]
            pks = ["psC", "psA"]
            for fc in range(32):
                for j2 in range(2):
                    j = half * 2 + j2
                    for h in range(2):
                        p.op("pe", lambda e, fc=fc, j=j, j2=j2, h=h:
                             e.matmul(
                                 psO[j2][:, h * 512:(h + 1) * 512],
                                 hg_g[:, fc * 512 + j * 128:
                                      fc * 512 + (j + 1) * 128],
                                 w2res[:, fc * D + h * 512:
                                       fc * D + (h + 1) * 512],
                                 start=(fc == 0), stop=False),
                             reads=["w2res", f"hg:{fc}"],
                             writes=[pks[j2] + f":{h}"], tag=f"f2m{g}")
            for j2 in range(2):
                j = half * 2 + j2
                s = g * 4 + j
                pk = pks[j2]
                for h in range(2):
                    p.op("pe", lambda e, j2=j2, h=h: e.matmul(
                        psO[j2][:, h * 512:(h + 1) * 512], onesr[0:1, :],
                        b2r16[:, h * 512:(h + 1) * 512], start=False, stop=True),
                        reads=["onesr", "b2r16"], writes=[pk + f":{h}"],
                        tag=f"b2{g}")
                p.dma("qsp", lambda e, s=s: e.dma_start(
                    out=xs2[:, :], in_=x_d[s * 128:(s + 1) * 128, :]),
                    reads=["xs2"], writes=["xs2"], tag=f"x3{s}")
                p.op("act", lambda e, j2=j2: e.activation(
                    out=fA[:, :], in_=psO[j2][:, :], func=AF.Identity,
                    scale=mc_w2[:, :]),
                    reads=[pk + ":0", pk + ":1", "mc_w2"], writes=["fA"])
                p.op("dve", lambda e: e.tensor_add(
                    out=fA[:, :], in0=fA[:, :], in1=xs2[:, :]),
                    reads=["fA", "xs2"], writes=["fA"])
                p.op("dve", lambda e, s=s: e.tensor_add(
                    out=fA[:, :], in0=fA[:, :], in1=yattn[:, s * D:(s + 1) * D]),
                    reads=["fA", f"yattn:{s}"], writes=["fA"])
                p.dma("qpool", lambda e, s=s: e.dma_start(
                    out=out_d[s * 128:(s + 1) * 128, :], in_=fA[:, :]),
                    reads=["fA"], writes=[f"out:{s}"], tag=f"o{s}")

    for g in range(4):
        p5_group(g)

    if _KDQ:
        p.barrier_read("sp", [f"dbgq:{i_}" for i_ in range(4)])
    p.barrier_read("sp", [f"out:{s}" for s in range(NSL)])
    rem = nc.sbuf_bytes_remaining
    import sys
    print(f"[kernel-build] sbuf remaining {rem}B/partition, "
          f"{len(p.ops)} ops", file=sys.stderr)
    p.emit()
    s2.close()
    gstack.close()
    return nc, rem


_NC_CACHE = [None]
kernel_last_res = [None]


def kernel(L0_attn_x, L0_attn_gamma, L0_attn_wq, L0_attn_wk, L0_attn_wv,
           L0_attn_wa, L0_attn_wg, L0_attn_wo, L0_ffn_w1, L0_ffn_b1,
           L0_ffn_w2, L0_ffn_b2):
    x = np.ascontiguousarray(np.asarray(L0_attn_x, np.float32))
    gamma = np.asarray(L0_attn_gamma, np.float32)
    wq = np.ascontiguousarray(np.asarray(L0_attn_wq, np.float32))
    wk = np.ascontiguousarray(np.asarray(L0_attn_wk, np.float32))
    wv = np.ascontiguousarray(np.asarray(L0_attn_wv, np.float32))
    wa = np.ascontiguousarray(np.asarray(L0_attn_wa, np.float32))
    wg = np.ascontiguousarray(np.asarray(L0_attn_wg, np.float32))
    wo = np.ascontiguousarray(np.asarray(L0_attn_wo, np.float32))
    w1 = np.asarray(L0_ffn_w1, np.float32)
    b1 = np.asarray(L0_ffn_b1, np.float32)
    w2 = np.asarray(L0_ffn_w2, np.float32)
    b2 = np.asarray(L0_ffn_b2, np.float32)

    w1t = np.ascontiguousarray(w1.T.reshape(D, 32, 128).transpose(1, 0, 2))
    w2t = np.ascontiguousarray(w2.T)
    b1c = np.ascontiguousarray(b1.reshape(32, 128).T)
    b2r = np.ascontiguousarray(b2.reshape(1, D))
    gr = np.ascontiguousarray(np.broadcast_to(gamma, (128, D)))

    BF = ml_dtypes.bfloat16
    tri = np.zeros((128, 128), np.float32)
    for b_ in range(2):
        for s_ in range(64):
            tri[b_ * 64 + s_, b_ * 64 + s_:(b_ + 1) * 64] = 1.0
    consts = {
        "tri_m1": (-tri).astype(BF),
        "tri_p4": (4.0 * tri).astype(BF),
        "tri_p1": tri.astype(BF),
        "ident": np.eye(128, dtype=np.float32).astype(BF),
        "onesr": np.ones((128, 128), np.float32).astype(BF),
    }
    cind = np.zeros((128, 2), np.float32)
    cind[0:64, 0] = 1.0
    cind[64:128, 1] = 1.0
    consts["cind"] = cind.astype(BF)
    chsel = np.zeros((32, NSL * 128), np.float32)
    for s_ in range(NSL):
        chsel[2 * s_, s_ * 128: s_ * 128 + 64] = 1.0
        chsel[2 * s_ + 1, s_ * 128 + 64: (s_ + 1) * 128] = 1.0
    consts["chsel"] = chsel.astype(BF)

    if _NC_CACHE[0] is None:
        _NC_CACHE[0] = build()[0]
    nc = _NC_CACHE[0]

    in_maps = []
    for c in range(8):
        b_, h_ = c // 2, c % 2
        sel = np.zeros(8, np.float32)
        if h_ == 1:
            sel[c - 1] = 1.0
        selrep = np.ascontiguousarray(
            np.broadcast_to(np.repeat(sel, 16)[None, :], (128, 128)))
        m = {"x": np.ascontiguousarray(x[b_, h_ * T:(h_ + 1) * T]),
             "wq": wq, "wk": wk, "wv": wv, "wg": wg, "wa": wa, "wo": wo,
             "w1t": w1t, "w2t": w2t, "b1c": b1c, "b2r": b2r, "gr": gr,
             "selrep": selrep}
        m.update(consts)
        in_maps.append(m)
    res = run_bass_kernel_spmd(nc, in_maps, core_ids=list(range(8)))
    kernel_last_res[0] = res
    out = np.empty((4, 4096, D), np.float32)
    for c in range(8):
        b_, h_ = c // 2, c % 2
        out[b_, h_ * T:(h_ + 1) * T] = res.results[c]["out"]
    return out


# version 08 (final)
# speedup vs baseline: 1.0975x; 1.0059x over previous
import math
import os as _os
import contextlib
from collections import defaultdict

import numpy as np
import ml_dtypes
import concourse.bass as bass
import concourse.mybir as mybir
from concourse.bass_utils import run_bass_kernel_spmd

F32 = mybir.dt.float32
BF16 = mybir.dt.bfloat16
I32 = mybir.dt.int32
I16 = mybir.dt.int16
AF = mybir.ActivationFunctionType
OP = mybir.AluOpType
PI = math.pi
TWO_PI = 2.0 * math.pi

T = 2048
D = 1024
FF = 4096
NSL = 16
C = 64
NCH = T // C
EPS = 1e-5
AX = mybir.AxisListType.X


class _Op:
    __slots__ = ("eng", "emit", "reads", "writes", "dma", "idx", "need_inc",
                 "inc_val", "waits", "tag")

    def __init__(self, eng, emit, reads, writes, dma, tag):
        self.eng, self.emit, self.reads, self.writes = eng, emit, reads, writes
        self.dma = dma
        self.idx = None
        self.need_inc = False
        self.inc_val = None
        self.waits = []
        self.tag = tag


_Q_OF = {"qsp": "sp", "qpool": "pool", "qact": "act", "qdve": "dve"}


class Prog:
    def __init__(self, nc):
        self.nc = nc
        self.ops = []
        self.last_w = {}
        self.readers = defaultdict(dict)
        self.pending_fence = None
        self._fenced = set()

    def op(self, eng, emit, reads=(), writes=(), tag=""):
        return self._plan(eng, emit, reads, writes, None, tag)

    def dma(self, queue, emit, reads=(), writes=(), tag=""):
        return self._plan(_Q_OF[queue], emit, reads, writes, queue, tag)

    def barrier_read(self, eng, keys, tag="fin"):
        return self._plan(eng, None, tuple(keys), (), None, tag)

    def fence(self):
        snap = {}
        for o in self.ops:
            snap[o.dma if o.dma else o.eng] = o
        for o in snap.values():
            o.need_inc = True
        self.pending_fence = dict(snap)
        self._fenced = set()

    def _plan(self, eng, emit, reads, writes, dma, tag):
        o = _Op(eng, emit, tuple(reads), tuple(writes), dma, tag)
        sem_name = dma if dma else eng
        deps = {}

        def add_dep(p_):
            if p_ is None:
                return
            psem = p_.dma if p_.dma else p_.eng
            cur = deps.get(psem)
            if cur is None or p_.idx > cur.idx:
                deps[psem] = p_

        if self.pending_fence is not None and sem_name not in self._fenced:
            self._fenced.add(sem_name)
            for p_ in self.pending_fence.values():
                add_dep(p_)
        for r in o.reads:
            add_dep(self.last_w.get(r))
        for w in o.writes:
            add_dep(self.last_w.get(w))
            for p_ in self.readers.get(w, {}).values():
                add_dep(p_)

        o.idx = len(self.ops)
        if dma:
            o.need_inc = True
        self.ops.append(o)
        for p_ in deps.values():
            p_.need_inc = True
            o.waits.append(p_)
        if emit is not None:
            for w in o.writes:
                self.last_w[w] = o
                self.readers[w] = {}
            for r in o.reads:
                self.readers[r][sem_name] = o
        return o

    def emit(self):
        nc = self.nc
        counts = defaultdict(int)
        for o in self.ops:
            if o.need_inc:
                sem = o.dma if o.dma else o.eng
                counts[sem] += 16 if o.dma else 1
                o.inc_val = counts[sem]
        per_eng = defaultdict(list)
        for o in self.ops:
            per_eng[o.eng].append(o)
        stack = contextlib.ExitStack()
        sems = {s: stack.enter_context(nc.semaphore("sem_" + s))
                for s in sorted(counts)}

        def run_engine(eng_obj, eng_name):
            waited = {}
            for o in per_eng[eng_name]:
                for p_ in o.waits:
                    psem = p_.dma if p_.dma else p_.eng
                    v = p_.inc_val
                    if waited.get(psem, -1) < v:
                        eng_obj.wait_ge(sems[psem], v)
                        waited[psem] = v
                if o.emit is None:
                    continue
                inst = o.emit(eng_obj)
                if o.need_inc:
                    sem = o.dma if o.dma else o.eng
                    inst.then_inc(sems[sem], 16 if o.dma else 1)

        with stack:
            with nc.Block() as block:
                for name, attr in [("pe", "tensor"), ("act", "scalar"),
                                   ("dve", "vector"), ("pool", "gpsimd"),
                                   ("sp", "sync")]:
                    if per_eng[name]:
                        getattr(block, attr)(
                            lambda e, n=name: run_engine(e, n))


def build():
    nc = bass.Bass("TRN2", target_bir_lowering=False)
    p = Prog(nc)

    dt = nc.dram_tensor
    x_d = dt("x", [T, D], F32, kind="ExternalInput").ap()
    wq_d = dt("wq", [D, D], F32, kind="ExternalInput").ap()
    wk_d = dt("wk", [D, D], F32, kind="ExternalInput").ap()
    wv_d = dt("wv", [D, D], F32, kind="ExternalInput").ap()
    wg_d = dt("wg", [D, D], F32, kind="ExternalInput").ap()
    wa_d = dt("wa", [D, 2 * D], F32, kind="ExternalInput").ap()
    wo_d = dt("wo", [D, D], F32, kind="ExternalInput").ap()
    w1t_d = dt("w1t", [32, D, 128], F32, kind="ExternalInput").ap()
    w2t_d = dt("w2t", [FF, D], F32, kind="ExternalInput").ap()
    b1c_d = dt("b1c", [128, 32], F32, kind="ExternalInput").ap()
    b2r_d = dt("b2r", [1, D], F32, kind="ExternalInput").ap()
    gr_d = dt("gr", [128, D], F32, kind="ExternalInput").ap()
    tri_m1_d = dt("tri_m1", [128, 128], BF16, kind="ExternalInput").ap()
    tri_p4_d = dt("tri_p4", [128, 128], BF16, kind="ExternalInput").ap()
    tri_p1_d = dt("tri_p1", [128, 128], BF16, kind="ExternalInput").ap()
    ident_d = dt("ident", [128, 128], BF16, kind="ExternalInput").ap()
    cind_d = dt("cind", [128, 2], BF16, kind="ExternalInput").ap()
    ones_d = dt("onesr", [128, 128], BF16, kind="ExternalInput").ap()
    chsel_d = dt("chsel", [32, NSL * 128], BF16, kind="ExternalInput").ap()
    selrep_d = dt("selrep", [128, 128], F32, kind="ExternalInput").ap()
    out_d = dt("out", [T, D], F32, kind="ExternalOutput").ap()
    w1t16_d = dt("w1t16", [32, 128, 1024], BF16, kind="Internal").ap()
    w2t16_d = dt("w2t16", [FF, D], BF16, kind="Internal").ap()
    wo16_d = dt("wo16", [D, D], BF16, kind="Internal").ap()
    ylocD = dt("ylocD", [NSL, 128, D], BF16, kind="Internal").ap()
    a1D = dt("a1D", [NSL, 128, D], BF16, kind="Internal").ap()
    a2D = dt("a2D", [NSL, 128, D], BF16, kind="Internal").ap()
    ccin_d = dt("ccin", [1, 2048], F32, kind="Internal").ap()
    ccout_d = dt("ccout", [8, 2048], F32, kind="Internal").ap()

    gstack = contextlib.ExitStack()

    def sb(stk, n, shape, dty):
        return stk.enter_context(nc.sbuf_tensor(n, shape, dty))

    def psum(stk, n, shape, dty):
        return stk.enter_context(nc.psum_tensor(n, shape, dty))

    tri_m1 = sb(gstack, "s_tri_m1", [128, 128], BF16)
    tri_p4 = sb(gstack, "s_tri_p4", [128, 128], BF16)
    tri_p1 = sb(gstack, "s_tri_p1", [128, 128], BF16)
    ident = sb(gstack, "s_ident", [128, 128], BF16)
    cind = sb(gstack, "s_cind", [128, 2], BF16)
    onesr = sb(gstack, "s_onesr", [128, 128], BF16)
    chselA = sb(gstack, "s_chsel", [32, NSL * 128], BF16)
    selrep = sb(gstack, "s_selrep", [128, 128], F32)
    gr = sb(gstack, "s_gr", [128, D], F32)
    b1c = sb(gstack, "s_b1c", [128, 32], F32)
    bn6 = sb(gstack, "bn6", [128, 12], F32)
    bn2 = sb(gstack, "bn2", [128, 2], F32)
    istd_a = sb(gstack, "istd_a", [128, NSL], F32)
    nmu_a = sb(gstack, "nmu_a", [128, NSL], F32)
    scl_a = sb(gstack, "scl_a", [128, NSL], F32)
    ct_a = sb(gstack, "ct_a", [128, NSL], F32)
    invs_a = sb(gstack, "invs_a", [128, NSL], F32)
    cEPS = sb(gstack, "cEPS", [128, 1], F32)
    cPIH = sb(gstack, "cPIH", [128, 1], F32)
    c38 = sb(gstack, "c38", [128, 1], F32)
    tA = sb(gstack, "tA", [128, 1], F32)
    tB = sb(gstack, "tB", [128, 1], F32)
    tC = sb(gstack, "tC", [128, 1], F32)
    tD_ = sb(gstack, "tD", [128, 1], F32)
    aggS = sb(gstack, "aggS", [128, 80], F32)
    hP_r = sb(gstack, "hP_r", [128, 8 * (NCH + 1)], F32)
    hP_i = sb(gstack, "hP_i", [128, 8 * (NCH + 1)], F32)
    PP_r = sb(gstack, "PP_r", [128, 8 * NCH], F32)
    PP_i = sb(gstack, "PP_i", [128, 8 * NCH], F32)
    magt = sb(gstack, "magt", [128, 16], F32)
    tt1 = sb(gstack, "tt1", [128, 16], F32)
    tt2 = sb(gstack, "tt2", [128, 16], F32)
    tt3 = sb(gstack, "tt3", [128, 16], F32)
    tt4 = sb(gstack, "tt4", [128, 16], F32)
    tki = sb(gstack, "tki", [128, 16], I32)
    acr_t = sb(gstack, "acr_t", [128, 16], F32)
    aci_t = sb(gstack, "aci_t", [128, 16], F32)
    slr_t = sb(gstack, "slr_t", [128, 16], F32)
    sli_t = sb(gstack, "sli_t", [128, 16], F32)
    ch1 = sb(gstack, "ch1", [128, 8], F32)
    ch2 = sb(gstack, "ch2", [128, 8], F32)
    Hpack = sb(gstack, "Hpack", [128, 16], F32)
    gatc = sb(gstack, "gatc", [128, 128], F32)
    Hprev = sb(gstack, "Hprev", [128, 16], F32)
    HrR = sb(gstack, "HrR", [128, 8 * NCH], F32)
    HiR = sb(gstack, "HiR", [128, 8 * NCH], F32)
    httR = sb(gstack, "httR", [128, 8 * NCH], BF16)
    httI = sb(gstack, "httI", [128, 8 * NCH], BF16)
    hrt2R = sb(gstack, "hrt2R", [32, D], BF16)
    hrt2I = sb(gstack, "hrt2I", [32, D], BF16)
    xs = sb(gstack, "xs", [128, D], F32)
    xng_tm = sb(gstack, "xng_tm", [128, D], BF16)
    fA = sb(gstack, "fA", [128, D], F32)
    fB = sb(gstack, "fB", [128, D], F32)
    wsl_f = sb(gstack, "wsl_f", [128, 1024], F32)
    wsl_b = sb(gstack, "wsl_b", [128, 1024], BF16)
    wslq = sb(gstack, "wslq", [128, 1024], F32)
    wslq2 = sb(gstack, "wslq2", [128, 1024], F32)
    onesf = sb(gstack, "onesf", [128, 128], F32)
    acc_w1 = sb(gstack, "acc_w1", [128, 1], F32)
    acc_w2 = sb(gstack, "acc_w2", [128, 1], F32)
    asum = sb(gstack, "asum", [128, 1], F32)
    sw_w1 = sb(gstack, "sw_w1", [128, 1], F32)
    sw_w2 = sb(gstack, "sw_w2", [128, 1], F32)
    mc_w1 = sb(gstack, "mc_w1", [128, 1], F32)
    mc_w2 = sb(gstack, "mc_w2", [128, 1], F32)

    psMain = psum(gstack, "psMain", [128, 1536], F32)
    psAux = psum(gstack, "psAux", [128, 1024], F32)
    psCum = psum(gstack, "psCum", [128, 1024], F32)
    psTrB = psum(gstack, "psTrB", [128, 1024], BF16)

    for nm, dst, src in [("tri_m1", tri_m1, tri_m1_d), ("tri_p4", tri_p4, tri_p4_d),
                         ("tri_p1", tri_p1, tri_p1_d), ("ident", ident, ident_d),
                         ("cind", cind, cind_d), ("onesr", onesr, ones_d),
                         ("chselA", chselA, chsel_d), ("selrep", selrep, selrep_d),
                         ("gr", gr, gr_d), ("b1c", b1c, b1c_d)]:
        p.dma("qsp", lambda e, dst=dst, src=src: e.dma_start(out=dst[:, :], in_=src),
              writes=[nm], tag="cst")
    p.op("dve", lambda e: e.memset(cEPS[:, :], EPS), writes=["cEPS"])
    p.op("dve", lambda e: e.memset(cPIH[:, :], PI / 2), writes=["cPIH"])
    p.op("dve", lambda e: e.memset(c38[:, :], 1e-38), writes=["c38"])
    p.op("dve", lambda e: e.memset(hP_r[:, 0:8], 0.0), writes=["hP:0"])
    p.op("dve", lambda e: e.memset(hP_i[:, 0:8], 0.0), writes=["hPi:0"])
    p.op("dve", lambda e: e.memset(PP_r[:, 0:8], 1.0), writes=["PP:0"])
    p.op("dve", lambda e: e.memset(PP_i[:, 0:8], 0.0), writes=["PPi:0"])
    p.op("dve", lambda e: e.memset(acc_w1[:, :], 0.0), writes=["acc_w1"])
    p.op("dve", lambda e: e.memset(onesf[:, :], 1.0), writes=["onesf"])
    p.op("dve", lambda e: e.memset(acc_w2[:, :], 0.0), writes=["acc_w2"])

    s0 = contextlib.ExitStack()
    wstage = s0.enter_context(
        nc.sbuf_tensor("wstage", [128, 6144], F32, side="right"))
    s1 = contextlib.ExitStack()
    wqkv16 = sb(s1, "wqkv16", [128, 8 * 6144], BF16)
    for dc in range(8):
        srcs = [(wq_d, 0, D), (wk_d, D, D), (wv_d, 2 * D, D), (wg_d, 3 * D, D),
                (wa_d, 4 * D, 2 * D)]
        rk = [f"wst:{dc}:{off}" for (_, off, _) in srcs]
        for (w, off, width) in srcs:
            p.dma("qsp",
                  lambda e, w=w, off=off, width=width, dc=dc:
                  e.dma_start(out=wstage[:, off:off + width],
                              in_=w[dc * 128:(dc + 1) * 128, :]),
                  reads=[f"wstfree:{j}" for j in range(3)] if dc else [],
                  writes=[f"wst:{dc}:{off}"], tag="wld")
        for j, eng in enumerate(["dve", "act", "pool"]):
            lo, hi = j * 2048, (j + 1) * 2048
            base = dc * 6144
            if eng == "act":
                emit = (lambda e, lo=lo, hi=hi, base=base:
                        e.activation(out=wqkv16[:, base + lo:base + hi],
                                     in_=wstage[:, lo:hi], func=AF.Copy))
            else:
                emit = (lambda e, lo=lo, hi=hi, base=base:
                        e.tensor_copy(out=wqkv16[:, base + lo:base + hi],
                                      in_=wstage[:, lo:hi]))
            p.op(eng, emit, reads=rk, writes=[f"wq16:{dc}", f"wstfree:{j}"],
                 tag="wcast")

    wsl2 = [wsl_f, wslq2]

    def _slab_src(kind, fc, buf):
        if kind == "w1":
            return (w1t_d[fc].rearrange("(c p) j -> p c j", p=128),
                    buf.ap().rearrange("p (c j) -> p c j", c=8))
        if kind == "w2":
            return (w2t_d[fc * 128:(fc + 1) * 128, :], buf[:, :])
        return (wo_d[fc * 128:(fc + 1) * 128, :], buf[:, :])

    sum_jobs = [("w1", fc) for fc in range(32)] + [("w2", fc) for fc in range(32)]
    sum_i = [0]

    def plan_sums(n):
        for _ in range(n):
            if sum_i[0] >= len(sum_jobs):
                return
            j = sum_i[0]
            kind, fc = sum_jobs[j]
            sum_i[0] += 1
            b = j % 2
            srcd, ovw = _slab_src(kind, fc, wsl2[b])
            acc = acc_w1 if kind == "w1" else acc_w2
            ak = "acc_" + kind
            p.dma(["qsp", "qact"][b],
                  lambda e, srcd=srcd, ovw=ovw: e.dma_start(out=ovw, in_=srcd),
                  writes=[f"wslf:{b}"], tag="sma")
            p.op("act", lambda e, b=b: e.activation(out=wslq[:, :],
                                                    in_=wsl2[b][:, :],
                                                    func=AF.Abs,
                                                    accum_out=asum[:, :]),
                 reads=[f"wslf:{b}", "wslq", "asum"],
                 writes=["asum", "wslq"], tag="smb")
            p.op("dve", lambda e, acc=acc: e.tensor_add(out=acc[:, :],
                                                        in0=acc[:, :],
                                                        in1=asum[:, :]),
                 reads=[ak, "asum"], writes=[ak], tag="smc")

    def plan_wscale():
        for kind, acc, sw, mc, ps_off in [("w1", acc_w1, sw_w1, mc_w1, 0),
                                          ("w2", acc_w2, sw_w2, mc_w2, 1)]:
            ak, swk, mck = "acc_" + kind, "sw_" + kind, "mc_" + kind
            p.op("pe", lambda e, acc=acc, ps_off=ps_off: e.matmul(
                psAux[:, ps_off:ps_off + 1], onesf[:, :], acc[:, :],
                start=True, stop=True),
                reads=[ak, "onesf"], writes=["psA:0"], tag="wsc")
            p.op("dve", lambda e, mc=mc, ps_off=ps_off: e.tensor_scalar(
                out=mc[:, :], in0=psAux[:, ps_off:ps_off + 1],
                scalar1=1.0 / (4096.0 * 1024.0), scalar2=1e-5,
                op0=OP.mult, op1=OP.max),
                reads=["psA:0"], writes=[mck], tag="wsc")
            p.op("dve", lambda e, sw=sw, mc=mc: e.reciprocal(out=sw[:, :],
                                                             in_=mc[:, :]),
                 reads=[mck], writes=[swk], tag="wsc")

    slab_jobs = ([("w1", fc) for fc in range(32)] +
                 [("w2", fc) for fc in range(32)] +
                 [("wo", fc) for fc in range(8)])
    slab_i = [0]

    def plan_slabs(n):
        for _ in range(n):
            if slab_i[0] >= len(slab_jobs):
                return
            j = slab_i[0]
            kind, fc = slab_jobs[j]
            slab_i[0] += 1
            b = j % 2
            srcd, ovw = _slab_src(kind, fc, wsl2[b])
            if kind == "w1":
                dst = w1t16_d[fc]
            elif kind == "w2":
                dst = w2t16_d[fc * 128:(fc + 1) * 128, :]
            else:
                dst = wo16_d[fc * 128:(fc + 1) * 128, :]
            p.dma(["qsp", "qact"][b],
                  lambda e, srcd=srcd, ovw=ovw: e.dma_start(out=ovw, in_=srcd),
                  writes=[f"wslf:{b}"], tag="slf")
            if kind == "wo":
                p.op("pool", lambda e, b=b: e.tensor_copy(out=wsl_b[:, :],
                                                          in_=wsl2[b][:, :]),
                     reads=[f"wslf:{b}", "wslc"], writes=["wslc", "wslq"],
                     tag="slc")
            else:
                sw = sw_w1 if kind == "w1" else sw_w2
                mc = mc_w1 if kind == "w1" else mc_w2
                swk, mck = "sw_" + kind, "mc_" + kind
                p.op("act", lambda e, b=b, sw=sw: e.activation(
                    out=wslq[:, :], in_=wsl2[b][:, :], func=AF.Identity,
                    scale=sw[:, :]),
                    reads=[f"wslf:{b}", swk], writes=["wslq"],
                    tag="slq")
                p.op("dve", lambda e: e.tensor_copy(
                    out=wslq.ap().bitcast(I32), in_=wslq[:, :]),
                    reads=["wslq"], writes=["wslq"], tag="slq")
                p.op("dve", lambda e: e.tensor_scalar(
                    out=wslq[:, :], in0=wslq.ap().bitcast(I32),
                    scalar1=1.0, scalar2=-1.0, op0=OP.min, op1=OP.max),
                    reads=["wslq"], writes=["wslq"], tag="slq")
                p.op("act", lambda e, mc=mc: e.activation(
                    out=wsl_b[:, :], in_=wslq[:, :], func=AF.Identity,
                    scale=mc[:, :]),
                    reads=["wslq", mck], writes=["wslc"], tag="slq")
            p.dma("qpool", lambda e, dst=dst: e.dma_start(out=dst, in_=wsl_b[:, :]),
                  reads=["wslc"], writes=[f"slab:{kind}:{fc}"], tag="slw")

    s0.close()
    p.fence()
    xngfm = sb(s1, "xngfm0", [128, 1024], BF16)
    arf2 = [sb(s1, f"arf{i}", [128, D], F32) for i in range(2)]
    aif2 = [sb(s1, f"aif{i}", [128, D], F32) for i in range(2)]
    fC = sb(s1, "fC", [128, D], F32)
    fD = sb(s1, "fD", [128, D], F32)
    fE = sb(s1, "fE", [128, D], F32)
    q2 = [sb(s1, f"q2{i}", [128, D], BF16) for i in range(2)]
    g2 = [sb(s1, f"g2{i}", [128, D], BF16) for i in range(2)]
    u2 = [sb(s1, f"u2{i}", [128, D], BF16) for i in range(2)]
    bb = {j: sb(s1, f"bb{j}", [128, D], BF16) for j in (0, 1, 2, 4, 5, 6, 7, 9)}

    def fe(s):
        i = s % 2
        col = lambda a: a[:, s:s + 1]
        xsl = xngfm
        xk = "xngfm:0"
        p.dma("qsp", lambda e: e.dma_start(out=xs[:, :],
                                           in_=x_d[s * 128:(s + 1) * 128, :]),
              reads=["xs"], writes=["xs"], tag=f"x{s}")
        p.op("dve", lambda e: e.bn_stats(out=bn6[:, 0:6], in_=xs[:, 0:512]),
             reads=["xs"], writes=["bn6"])
        p.op("dve", lambda e: e.bn_stats(out=bn6[:, 6:12], in_=xs[:, 512:1024]),
             reads=["xs"], writes=["bn6"])
        p.op("dve", lambda e: e.bn_aggr(out=bn2[:, :], in_=bn6[:, :]),
             reads=["bn6"], writes=["bn2"])
        p.op("act", lambda e: e.activation(out=tA[:, :], in_=bn2[:, 1:2],
                                           func=AF.Ln, bias=cEPS[:, :]),
             reads=["bn2", "cEPS"], writes=["tA"])
        p.op("act", lambda e: e.activation(out=col(istd_a), in_=tA[:, :],
                                           func=AF.Exp, scale=-0.5),
             reads=["tA"], writes=[f"istd:{s}"])
        p.op("dve", lambda e: e.tensor_mul(out=tB[:, :], in0=bn2[:, 0:1],
                                           in1=col(istd_a)),
             reads=["bn2", f"istd:{s}"], writes=["tB"])
        p.op("dve", lambda e: e.tensor_scalar(out=col(nmu_a), in0=tB[:, :],
                                              scalar1=-1.0, scalar2=None,
                                              op0=OP.mult),
             reads=["tB"], writes=[f"nmu:{s}"])
        p.op("dve", lambda e: e.tensor_mul(out=tB[:, :], in0=bn2[:, 1:2],
                                           in1=col(istd_a)),
             reads=["bn2", f"istd:{s}"], writes=["tB"])
        p.op("dve", lambda e: e.tensor_mul(out=tB[:, :], in0=tB[:, :],
                                           in1=col(istd_a)),
             reads=["tB", f"istd:{s}"], writes=["tB"])
        p.op("act", lambda e: e.activation(out=tA[:, :], in_=tB[:, :],
                                           func=AF.Ln, scale=float(D),
                                           bias=cEPS[:, :]),
             reads=["tB", "cEPS"], writes=["tA"])
        p.op("act", lambda e: e.activation(out=tA[:, :], in_=tA[:, :],
                                           func=AF.Exp, scale=-0.5),
             reads=["tA"], writes=["tA"])
        p.op("dve", lambda e: e.tensor_scalar(out=col(scl_a), in0=tA[:, :],
                                              scalar1=32.0, scalar2=None,
                                              op0=OP.mult),
             reads=["tA"], writes=[f"scl:{s}"])
        p.op("act", lambda e: e.activation(out=tC[:, :], in_=tB[:, :],
                                           func=AF.Ln, bias=cEPS[:, :]),
             reads=["tB", "cEPS"], writes=["tC"])
        p.op("act", lambda e: e.activation(out=tC[:, :], in_=tC[:, :],
                                           func=AF.Exp, scale=-0.5),
             reads=["tC"], writes=["tC"])
        p.op("act", lambda e: e.activation(out=xs[:, :], in_=xs[:, :],
                                           func=AF.Identity, scale=col(istd_a),
                                           bias=col(nmu_a)),
             reads=["xs", f"istd:{s}", f"nmu:{s}"], writes=["xs"])
        p.op("dve", lambda e: e.tensor_reduce(out=tD_[:, :], in_=xs[:, :],
                                              axis=AX, op=OP.max,
                                              apply_absolute_value=True),
             reads=["xs"], writes=["tD"])
        p.op("dve", lambda e: e.tensor_mul(out=tB[:, :], in0=tD_[:, :], in1=tC[:, :]),
             reads=["tD", "tC"], writes=["tB"])
        p.op("dve", lambda e: e.tensor_scalar(out=tB[:, :], in0=tB[:, :],
                                              scalar1=1e-5, scalar2=None, op0=OP.max),
             reads=["tB"], writes=["tB"])
        p.op("dve", lambda e: e.tensor_scalar(out=col(invs_a), in0=tB[:, :],
                                              scalar1=1.0 / 127.0, scalar2=None,
                                              op0=OP.mult),
             reads=["tB"], writes=[f"invs:{s}"])
        p.op("dve", lambda e: e.reciprocal(out=tB[:, :], in_=tB[:, :]),
             reads=["tB"], writes=["tB"])
        p.op("dve", lambda e: e.tensor_mul(out=tB[:, :], in0=tB[:, :], in1=tC[:, :]),
             reads=["tB", "tC"], writes=["tB"])
        p.op("dve", lambda e: e.tensor_scalar(out=col(ct_a), in0=tB[:, :],
                                              scalar1=127.0, scalar2=None,
                                              op0=OP.mult),
             reads=["tB"], writes=[f"ct:{s}"])
        p.op("dve", lambda e: e.tensor_mul(out=xng_tm[:, :], in0=xs[:, :],
                                           in1=gr[:, :]),
             reads=["xs", "gr"], writes=["xng_tm"])
        for dc in range(8):
            p.op("pe", lambda e, dc=dc: e.transpose(
                psTrB[:, dc * 128:(dc + 1) * 128],
                xng_tm[:, dc * 128:(dc + 1) * 128], ident[:, :]),
                reads=["xng_tm", "ident"], writes=["psTrB"], tag=f"tr{s}")
        p.op("act", lambda e: e.activation(out=xsl[:, :], in_=psTrB[:, :],
                                           func=AF.Copy),
             reads=["psTrB"], writes=[xk], tag=f"tre{s}")

        def mm_pass(wos, tagp):
            for dc in range(8):
                for r_, wo_ in enumerate(wos):
                    p.op("pe", lambda e, dc=dc, r_=r_, wo_=wo_:
                         e.matmul(psMain[:, r_ * 512:(r_ + 1) * 512],
                                  xsl[:, dc * 128:(dc + 1) * 128],
                                  wqkv16[:, dc * 6144 + wo_: dc * 6144 + wo_ + 512],
                                  start=(dc == 0), stop=(dc == 7)),
                         reads=[xk, f"wq16:{dc}"], writes=[f"psM:{r_}"], tag=tagp)

        def evac(out_ap, lo, hi, wr):
            rk = [f"psM:{r_}" for r_ in range(lo // 512, (hi + 511) // 512)]
            p.op("act", lambda e: e.activation(out=out_ap, in_=psMain[:, lo:hi],
                                               func=AF.Identity, scale=col(scl_a)),
                 reads=rk + [f"scl:{s}"], writes=wr)

        mm_pass([0, 512, 1024], f"p1m{s}")
        evac(q2[i][:, :], 0, 1024, [f"q2:{i}"])
        evac(u2[i][:, 0:512], 1024, 1536, [f"u2:{i}"])
        mm_pass([1536, 2048, 2560], f"p2m{s}")
        evac(u2[i][:, 512:1024], 0, 512, [f"u2:{i}"])
        evac(xng_tm[:, :], 512, 1536, ["xng_tm"])
        p.op("pool", lambda e: e.tensor_mul(out=u2[i][:, :], in0=u2[i][:, :],
                                            in1=xng_tm[:, :]),
             reads=[f"u2:{i}", "xng_tm"], writes=[f"u2:{i}"])
        mm_pass([3072, 3584, 4096], f"p3m{s}")
        evac(g2[i][:, :], 0, 1024, [f"g2:{i}"])
        evac(arf2[i][:, 0:512], 1024, 1536, [f"arf:{i}"])
        mm_pass([4608, 5120, 5632], f"p4m{s}")
        evac(arf2[i][:, 512:1024], 0, 512, [f"arf:{i}"])
        evac(aif2[i][:, :], 512, 1536, [f"aif:{i}"])

    def be_a(s):
        i = s % 2
        p.op("act", lambda e: e.activation(out=fA[:, :], in_=arf2[i][:, :],
                                           func=AF.Square),
             reads=[f"arf:{i}"], writes=["fA"])
        p.op("act", lambda e: e.activation(out=fB[:, :], in_=aif2[i][:, :],
                                           func=AF.Square),
             reads=[f"aif:{i}"], writes=["fB"])
        p.op("pool", lambda e: e.tensor_add(out=fC[:, :], in0=fA[:, :],
                                            in1=fB[:, :]),
             reads=["fA", "fB"], writes=["fC"])
        p.op("act", lambda e: e.activation(out=fC[:, :], in_=fC[:, :],
                                           func=AF.Ln, bias=c38[:, :]),
             reads=["fC", "c38"], writes=["fC"])
        p.op("act", lambda e: e.activation(out=fC[:, :], in_=fC[:, :],
                                           func=AF.Exp, scale=0.5),
             reads=["fC"], writes=["fC"])
        p.op("dve", lambda e: e.tensor_add(out=fD[:, :], in0=fC[:, :],
                                           in1=arf2[i][:, :]),
             reads=["fC", f"arf:{i}"], writes=["fD"])
        p.op("act", lambda e: e.activation(out=fA[:, :], in_=fD[:, :],
                                           func=AF.Square),
             reads=["fD"], writes=["fA"])
        p.op("pool", lambda e: e.tensor_add(out=fA[:, :], in0=fA[:, :],
                                            in1=fB[:, :]),
             reads=["fA", "fB"], writes=["fA"])
        p.op("act", lambda e: e.activation(out=fA[:, :], in_=fA[:, :],
                                           func=AF.Ln, bias=c38[:, :]),
             reads=["fA", "c38"], writes=["fA"])
        p.op("act", lambda e: e.activation(out=fA[:, :], in_=fA[:, :],
                                           func=AF.Exp, scale=0.5),
             reads=["fA"], writes=["fA"])
        p.op("dve", lambda e: e.scalar_tensor_tensor(out=fD[:, :], in0=fA[:, :],
                                                     scalar=1e-12, in1=fD[:, :],
                                                     op0=OP.add, op1=OP.add),
             reads=["fA", "fD"], writes=["fD"])
        p.op("act", lambda e: e.activation(out=fE[:, :], in_=fC[:, :],
                                           func=AF.Exp, scale=-1.0),
             reads=["fC"], writes=["fE"])
        p.op("act", lambda e: e.activation(out=bb[4][:, :], in_=fE[:, :],
                                           func=AF.Ln, bias=1.0),
             reads=["fE"], writes=["b4"])
        p.op("act", lambda e: e.activation(out=fE[:, :], in_=fD[:, :], func=AF.Ln),
             reads=["fD"], writes=["fE"])
        p.op("act", lambda e: e.activation(out=fE[:, :], in_=fE[:, :],
                                           func=AF.Exp, scale=-1.0),
             reads=["fE"], writes=["fE"])
        p.op("dve", lambda e: e.tensor_mul(out=fC[:, :], in0=aif2[i][:, :],
                                           in1=fE[:, :]),
             reads=[f"aif:{i}", "fE"], writes=["fC"])
        p.op("dve", lambda e: e.tensor_copy(out=fD[:, :], in_=g2[i][:, :]),
             reads=[f"g2:{i}"], writes=["fD"])
        p.op("pool", lambda e: e.tensor_mul(out=bb[0][:, :], in0=q2[i][:, :],
                                            in1=g2[i][:, :]),
             reads=[f"q2:{i}", f"g2:{i}"], writes=["b0"])
        for h in range(2):
            p.op("pe", lambda e, h=h: e.matmul(psCum[:, h * 512:(h + 1) * 512],
                                               tri_m1[:, :],
                                               bb[4][:, h * 512:(h + 1) * 512],
                                               start=True, stop=True),
                 reads=["tri_m1", "b4"], writes=[f"psC:{h}"], tag=f"cs1{s}")
        for cc in range(8):
            p.op("pe", lambda e, cc=cc: e.matmul(
                psAux[:, cc * 2:cc * 2 + 2], bb[4][:, cc * 128:(cc + 1) * 128],
                cind[:, :], start=True, stop=True),
                reads=["b4", "cind"], writes=["psA:0"], tag=f"ag{s}")
        p.op("act", lambda e: e.activation(out=bb[7][:, :], in_=psCum[:, :],
                                           func=AF.Exp),
             reads=["psC:0", "psC:1"], writes=["b7"])
        p.op("act", lambda e: e.activation(out=fB[:, :], in_=psCum[:, :],
                                           func=AF.Exp, scale=-1.0),
             reads=["psC:0", "psC:1"], writes=["fB"])
        p.op("act", lambda e: e.activation(
            out=magt.ap().rearrange("p (k cc) -> p k cc", k=2),
            in_=psAux[:, 0:16].rearrange("p (cc k) -> p k cc", cc=8),
            func=AF.Exp, scale=-1.0),
            reads=["psA:0"], writes=["magt"])
        p.op("dve", lambda e: e.tensor_mul(out=fB[:, :], in0=fB[:, :],
                                           in1=u2[i][:, :]),
             reads=[f"u2:{i}", "fB"], writes=["fB"])

    def be_b(s):
        p.op("act", lambda e: e.activation(out=fC[:, :], in_=fC[:, :],
                                           func=AF.Arctan),
             reads=["fC"], writes=["fC"])
        p.op("act", lambda e: e.activation(out=fD[:, :], in_=fD[:, :],
                                           func=AF.Sigmoid),
             reads=["fD"], writes=["fD"])
        p.op("dve", lambda e: e.tensor_mul(out=bb[0][:, :], in0=bb[0][:, :],
                                           in1=fD[:, :]),
             reads=["b0", "fD"], writes=["b0"])
        p.op("dve", lambda e: e.tensor_copy(out=bb[5][:, :], in_=fC[:, :]),
             reads=["fC"], writes=["b5"])
        p.op("dve", lambda e: e.scalar_tensor_tensor(out=bb[6][:, :],
                                                     in0=bb[5][:, :], scalar=-1.0,
                                                     in1=fC[:, :], op0=OP.mult,
                                                     op1=OP.add),
             reads=["b5", "fC"], writes=["b6"])
        for cc in range(8):
            p.op("pe", lambda e, cc=cc: e.matmul(
                psAux[:, 16 + cc * 2:16 + cc * 2 + 2],
                bb[5][:, cc * 128:(cc + 1) * 128],
                cind[:, :], start=True, stop=False),
                reads=["b5", "cind"], writes=["psA:0"], tag=f"agt{s}")
            p.op("pe", lambda e, cc=cc: e.matmul(
                psAux[:, 16 + cc * 2:16 + cc * 2 + 2],
                bb[6][:, cc * 128:(cc + 1) * 128],
                cind[:, :], start=False, stop=True),
                reads=["b6", "cind"], writes=["psA:0"], tag=f"agt{s}")
        p.op("act", lambda e: e.activation(
            out=aggS[:, 16:32].rearrange("p (k cc) -> p k cc", k=2),
            in_=psAux[:, 16:32].rearrange("p (cc k) -> p k cc", cc=8),
            func=AF.Copy),
            reads=["psA:0"], writes=["aggS"])
        for h in range(2):
            p.op("pe", lambda e, h=h: e.matmul(psCum[:, h * 512:(h + 1) * 512],
                                               tri_p4[:, :],
                                               bb[5][:, h * 512:(h + 1) * 512],
                                               start=True, stop=False),
                 reads=["tri_p4", "b5"], writes=[f"psC:{h}"], tag=f"cs2{s}")
            p.op("pe", lambda e, h=h: e.matmul(psCum[:, h * 512:(h + 1) * 512],
                                               tri_p4[:, :],
                                               bb[6][:, h * 512:(h + 1) * 512],
                                               start=False, stop=True),
                 reads=["tri_p4", "b6"], writes=[f"psC:{h}"], tag=f"cs2{s}")
        p.op("dve", lambda e: e.tensor_scalar(out=fA.ap().bitcast(I32),
                                              in0=psCum[:, :],
                                              scalar1=1.0 / TWO_PI, scalar2=None,
                                              op0=OP.mult),
             reads=["psC:0", "psC:1"], writes=["fA"])
        p.op("dve", lambda e: e.scalar_tensor_tensor(out=fD[:, :],
                                                     in0=fA.ap().bitcast(I32),
                                                     scalar=-TWO_PI,
                                                     in1=psCum[:, :],
                                                     op0=OP.mult, op1=OP.add),
             reads=["fA", "psC:0", "psC:1"], writes=["fD"])
        p.op("dve", lambda e: e.tensor_scalar(out=fD[:, :], in0=fD[:, :],
                                              scalar1=6.2821, scalar2=-6.2821,
                                              op0=OP.min, op1=OP.max),
             reads=["fD"], writes=["fD"])
        p.op("dve", lambda e: e.tensor_scalar(out=tki[:, :], in0=aggS[:, 16:32],
                                              scalar1=4.0 / TWO_PI, scalar2=None,
                                              op0=OP.mult),
             reads=["aggS"], writes=["tki"])
        p.op("dve", lambda e: e.tensor_scalar(out=tt1[:, :], in0=aggS[:, 16:32],
                                              scalar1=4.0, scalar2=None,
                                              op0=OP.mult),
             reads=["aggS"], writes=["tt1"])
        p.op("dve", lambda e: e.scalar_tensor_tensor(out=tt1[:, :], in0=tki[:, :],
                                                     scalar=-TWO_PI, in1=tt1[:, :],
                                                     op0=OP.mult, op1=OP.add),
             reads=["tki", "tt1"], writes=["tt1"])
        p.op("dve", lambda e: e.tensor_scalar(out=tt1[:, :], in0=tt1[:, :],
                                              scalar1=6.2821, scalar2=-6.2821,
                                              op0=OP.min, op1=OP.max),
             reads=["tt1"], writes=["tt1"])
        p.op("act", lambda e: e.activation(out=fC[:, :], in_=fD[:, :],
                                           func=AF.Sin, scale=0.25),
             reads=["fD"], writes=["fC"])
        p.op("act", lambda e: e.activation(out=fE[:, :], in_=fD[:, :],
                                           func=AF.Sin, scale=0.25, bias=cPIH[:, :]),
             reads=["fD", "cPIH"], writes=["fE"])
        p.op("act", lambda e: e.activation(out=tt2[:, :], in_=tt1[:, :],
                                           func=AF.Sin, scale=0.25),
             reads=["tt1"], writes=["tt2"])
        p.op("act", lambda e: e.activation(out=tt3[:, :], in_=tt1[:, :],
                                           func=AF.Sin, scale=0.25, bias=cPIH[:, :]),
             reads=["tt1", "cPIH"], writes=["tt3"])
        p.op("act", lambda e: e.activation(out=fA[:, :], in_=fC[:, :],
                                           func=AF.Square),
             reads=["fC"], writes=["fA"])
        p.op("dve", lambda e: e.tensor_mul(out=fC[:, :], in0=fC[:, :],
                                           in1=fE[:, :]),
             reads=["fC", "fE"], writes=["fC"])
        p.op("dve", lambda e: e.tensor_scalar(out=fA[:, :], in0=fA[:, :],
                                              scalar1=-2.0, scalar2=1.0,
                                              op0=OP.mult, op1=OP.add),
             reads=["fA"], writes=["fA"])
        p.op("pool", lambda e: e.tensor_mul(out=fE[:, :], in0=fC[:, :],
                                            in1=fA[:, :]),
             reads=["fC", "fA"], writes=["fE"])
        p.op("dve", lambda e: e.tensor_scalar(out=bb[5][:, :], in0=fE[:, :],
                                              scalar1=4.0, scalar2=None,
                                              op0=OP.mult),
             reads=["fE"], writes=["b5"])
        p.op("act", lambda e: e.activation(out=fA[:, :], in_=fC[:, :],
                                           func=AF.Square),
             reads=["fC"], writes=["fA"])
        p.op("dve", lambda e: e.tensor_scalar(out=bb[6][:, :], in0=fA[:, :],
                                              scalar1=-8.0, scalar2=1.0,
                                              op0=OP.mult, op1=OP.add),
             reads=["fA"], writes=["b6"])
        p.op("act", lambda e: e.activation(out=tt4[:, :], in_=tt2[:, :],
                                           func=AF.Square),
             reads=["tt2"], writes=["tt4"])
        p.op("dve", lambda e: e.tensor_mul(out=tt2[:, :], in0=tt2[:, :],
                                           in1=tt3[:, :]),
             reads=["tt2", "tt3"], writes=["tt2"])
        p.op("dve", lambda e: e.tensor_scalar(out=tt3[:, :], in0=tt4[:, :],
                                              scalar1=-2.0, scalar2=1.0,
                                              op0=OP.mult, op1=OP.add),
             reads=["tt4"], writes=["tt3"])
        p.op("dve", lambda e: e.tensor_mul(out=tt1[:, :], in0=tt2[:, :],
                                           in1=tt3[:, :]),
             reads=["tt2", "tt3"], writes=["tt1"])
        p.op("dve", lambda e: e.tensor_scalar(out=tt1[:, :], in0=tt1[:, :],
                                              scalar1=4.0, scalar2=None,
                                              op0=OP.mult),
             reads=["tt1"], writes=["tt1"])
        p.op("act", lambda e: e.activation(out=tt4[:, :], in_=tt2[:, :],
                                           func=AF.Square),
             reads=["tt2"], writes=["tt4"])
        p.op("dve", lambda e: e.tensor_scalar(out=tt4[:, :], in0=tt4[:, :],
                                              scalar1=-8.0, scalar2=1.0,
                                              op0=OP.mult, op1=OP.add),
             reads=["tt4"], writes=["tt4"])
        p.op("dve", lambda e: e.tensor_mul(out=bb[4][:, :], in0=fB[:, :],
                                           in1=bb[6][:, :]),
             reads=["fB", "b6"], writes=["b4"])
        p.op("dve", lambda e: e.tensor_mul(out=bb[9][:, :], in0=fB[:, :],
                                           in1=bb[5][:, :]),
             reads=["fB", "b5"], writes=["b9"])
        p.op("pool", lambda e: e.tensor_mul(out=bb[1][:, :], in0=bb[7][:, :],
                                            in1=bb[6][:, :]),
             reads=["b7", "b6"], writes=["b1"])
        p.op("pool", lambda e: e.tensor_mul(out=bb[2][:, :], in0=bb[7][:, :],
                                            in1=bb[5][:, :]),
             reads=["b7", "b5"], writes=["b2"])
        for cc in range(8):
            p.op("pe", lambda e, cc=cc: e.matmul(
                psAux[:, 32 + cc * 2:32 + cc * 2 + 2],
                bb[4][:, cc * 128:(cc + 1) * 128], cind[:, :],
                start=True, stop=True),
                reads=["b4", "cind"], writes=["psA:0"], tag=f"ag2{s}")
            p.op("pe", lambda e, cc=cc: e.matmul(
                psAux[:, 48 + cc * 2:48 + cc * 2 + 2],
                bb[9][:, cc * 128:(cc + 1) * 128], cind[:, :],
                start=True, stop=True),
                reads=["b9", "cind"], writes=["psA:0"], tag=f"ag2{s}")
        p.op("act", lambda e: e.activation(
            out=aggS[:, 32:64].rearrange("p (g k cc) -> p g k cc", g=2, k=2),
            in_=psAux[:, 32:64].rearrange("p (g cc k) -> p g k cc", g=2, cc=8),
            func=AF.Copy),
            reads=["psA:0"], writes=["aggS"])
        for h in range(2):
            p.op("pe", lambda e, h=h: e.matmul(psCum[:, h * 512:(h + 1) * 512],
                                               tri_p1[:, :],
                                               bb[4][:, h * 512:(h + 1) * 512],
                                               start=True, stop=True),
                 reads=["tri_p1", "b4"], writes=[f"psC:{h}"], tag=f"cs3{s}")
            p.op("pe", lambda e, h=h: e.matmul(psAux[:, h * 512:(h + 1) * 512],
                                               tri_m1[:, :],
                                               bb[9][:, h * 512:(h + 1) * 512],
                                               start=True, stop=True),
                 reads=["tri_m1", "b9"],
                 writes=["psA:0"] if h == 0 else ["psA:1"], tag=f"cs3{s}")
        p.op("dve", lambda e: e.tensor_mul(out=fA[:, :], in0=bb[1][:, :],
                                           in1=psCum[:, :]),
             reads=["b1", "psC:0", "psC:1"], writes=["fA"])
        p.op("dve", lambda e: e.tensor_mul(out=fB[:, :], in0=bb[2][:, :],
                                           in1=psAux[:, 0:1024]),
             reads=["b2", "psA:0", "psA:1"], writes=["fB"])
        p.op("dve", lambda e: e.tensor_sub(out=fA[:, :], in0=fA[:, :],
                                           in1=fB[:, :]),
             reads=["fA", "fB"], writes=["fA"])
        p.op("pool", lambda e: e.tensor_mul(out=bb[5][:, :], in0=bb[0][:, :],
                                            in1=bb[1][:, :]),
             reads=["b0", "b1"], writes=["b5"])
        p.op("pool", lambda e: e.tensor_mul(out=bb[7][:, :], in0=bb[0][:, :],
                                            in1=bb[2][:, :]),
             reads=["b0", "b2"], writes=["b7"])
        p.op("dve", lambda e: e.tensor_mul(out=bb[0][:, :], in0=bb[0][:, :],
                                           in1=fA[:, :]),
             reads=["b0", "fA"], writes=["b0"])
        p.dma("qpool", lambda e: e.dma_start(out=ylocD[s], in_=bb[0][:, :]),
              reads=["b0"], writes=[f"ylD:{s}"], tag=f"sp{s}")
        p.dma("qpool", lambda e: e.dma_start(out=a1D[s], in_=bb[5][:, :]),
              reads=["b5"], writes=[f"a1D:{s}"], tag=f"sp{s}")
        p.dma("qpool", lambda e: e.dma_start(out=a2D[s], in_=bb[7][:, :]),
              reads=["b7"], writes=[f"a2D:{s}"], tag=f"sp{s}")
        p.op("dve", lambda e: e.tensor_mul(out=acr_t[:, :], in0=magt[:, :],
                                           in1=tt4[:, :]),
             reads=["magt", "tt4"], writes=["acr_t"])
        p.op("dve", lambda e: e.tensor_mul(out=aci_t[:, :], in0=magt[:, :],
                                           in1=tt1[:, :]),
             reads=["magt", "tt1"], writes=["aci_t"])
        p.op("dve", lambda e: e.tensor_mul(out=slr_t[:, :], in0=acr_t[:, :],
                                           in1=aggS[:, 32:48]),
             reads=["acr_t", "aggS"], writes=["slr_t"])
        p.op("dve", lambda e: e.tensor_mul(out=tt1[:, :], in0=aci_t[:, :],
                                           in1=aggS[:, 48:64]),
             reads=["aci_t", "aggS"], writes=["tt1"])
        p.op("dve", lambda e: e.tensor_add(out=slr_t[:, :], in0=slr_t[:, :],
                                           in1=tt1[:, :]),
             reads=["slr_t", "tt1"], writes=["slr_t"])
        p.op("dve", lambda e: e.tensor_mul(out=sli_t[:, :], in0=aci_t[:, :],
                                           in1=aggS[:, 32:48]),
             reads=["aci_t", "aggS"], writes=["sli_t"])
        p.op("dve", lambda e: e.tensor_mul(out=tt1[:, :], in0=acr_t[:, :],
                                           in1=aggS[:, 48:64]),
             reads=["acr_t", "aggS"], writes=["tt1"])
        p.op("dve", lambda e: e.tensor_sub(out=sli_t[:, :], in0=sli_t[:, :],
                                           in1=tt1[:, :]),
             reads=["sli_t", "tt1"], writes=["sli_t"])
        for k in range(2):
            c = 2 * s + k
            lo, hi = k * 8, (k + 1) * 8
            co, co1 = c * 8, (c + 1) * 8

            def cop(fn, out_ap, a_ap, b_ap, rd, wr):
                p.op("dve",
                     lambda e, fn=fn, out_ap=out_ap, a_ap=a_ap, b_ap=b_ap:
                     getattr(e, fn)(out=out_ap, in0=a_ap, in1=b_ap),
                     reads=rd, writes=wr)

            cop("tensor_mul", ch1[:, :], acr_t[:, lo:hi], hP_r[:, co:co + 8],
                ["acr_t", f"hP:{c}"], ["ch1"])
            cop("tensor_mul", ch2[:, :], aci_t[:, lo:hi], hP_i[:, co:co + 8],
                ["aci_t", f"hPi:{c}"], ["ch2"])
            cop("tensor_sub", ch1[:, :], ch1[:, :], ch2[:, :],
                ["ch1", "ch2"], ["ch1"])
            cop("tensor_add", hP_r[:, co1:co1 + 8], ch1[:, :], slr_t[:, lo:hi],
                ["ch1", "slr_t"], [f"hP:{c + 1}"])
            cop("tensor_mul", ch1[:, :], acr_t[:, lo:hi], hP_i[:, co:co + 8],
                ["acr_t", f"hPi:{c}"], ["ch1"])
            cop("tensor_mul", ch2[:, :], aci_t[:, lo:hi], hP_r[:, co:co + 8],
                ["aci_t", f"hP:{c}"], ["ch2"])
            cop("tensor_add", ch1[:, :], ch1[:, :], ch2[:, :],
                ["ch1", "ch2"], ["ch1"])
            cop("tensor_add", hP_i[:, co1:co1 + 8], ch1[:, :], sli_t[:, lo:hi],
                ["ch1", "sli_t"], [f"hPi:{c + 1}"])
            if c + 1 < NCH:
                cop("tensor_mul", ch1[:, :], acr_t[:, lo:hi], PP_r[:, co:co + 8],
                    ["acr_t", f"PP:{c}"], ["ch1"])
                cop("tensor_mul", ch2[:, :], aci_t[:, lo:hi], PP_i[:, co:co + 8],
                    ["aci_t", f"PPi:{c}"], ["ch2"])
                cop("tensor_sub", PP_r[:, co1:co1 + 8], ch1[:, :], ch2[:, :],
                    ["ch1", "ch2"], [f"PP:{c + 1}"])
                cop("tensor_mul", ch1[:, :], acr_t[:, lo:hi], PP_i[:, co:co + 8],
                    ["acr_t", f"PPi:{c}"], ["ch1"])
                cop("tensor_mul", ch2[:, :], aci_t[:, lo:hi], PP_r[:, co:co + 8],
                    ["aci_t", f"PP:{c}"], ["ch2"])
                cop("tensor_add", PP_i[:, co1:co1 + 8], ch1[:, :], ch2[:, :],
                    ["ch1", "ch2"], [f"PPi:{c + 1}"])

    fe(0)
    fe(1)
    be_a(0)
    for s in range(NSL):
        if s + 2 < NSL:
            fe(s + 2)
        be_b(s)
        if s + 1 < NSL:
            be_a(s + 1)
        if s < 8:
            plan_sums(8)
            if s == 7:
                plan_sums(100)
                plan_wscale()
        else:
            plan_slabs(9)
    plan_slabs(100)

    p.op("dve", lambda e: e.tensor_copy(out=Hpack[:, 0:8],
                                        in_=hP_r[:, NCH * 8:(NCH + 1) * 8]),
         reads=[f"hP:{NCH}"], writes=["Hpack"])
    p.op("dve", lambda e: e.tensor_copy(out=Hpack[:, 8:16],
                                        in_=hP_i[:, NCH * 8:(NCH + 1) * 8]),
         reads=[f"hPi:{NCH}"], writes=["Hpack"])
    p.dma("qpool", lambda e: e.dma_start(out=ccin_d, in_=Hpack[:, :]),
          reads=["Hpack"], writes=["ccin"], tag="cc")
    p.op("pool", lambda e: e.collective_compute(
        "AllGather", OP.bypass, ins=[ccin_d], outs=[ccout_d],
        replica_groups=[[0, 1, 2, 3, 4, 5, 6, 7]]),
        reads=["ccin"], writes=["ccout"], tag="cc")
    p.dma("qpool", lambda e: e.dma_start(
        out=gatc.ap().rearrange("p (r j) -> p r j", r=8),
        in_=ccout_d.rearrange("r (p j) -> p r j", p=128)),
        reads=["ccout"], writes=["gatc"], tag="cc")
    p.op("dve", lambda e: e.tensor_mul(out=gatc[:, :], in0=gatc[:, :],
                                       in1=selrep[:, :]),
         reads=["gatc", "selrep"], writes=["gatc"])
    p.op("dve", lambda e: e.tensor_copy(out=Hprev[:, :], in_=gatc[:, 0:16]),
         reads=["gatc"], writes=["Hprev"])
    for r_ in range(1, 8):
        p.op("dve", lambda e, r_=r_: e.tensor_add(
            out=Hprev[:, :], in0=Hprev[:, :], in1=gatc[:, r_ * 16:(r_ + 1) * 16]),
            reads=["gatc", "Hprev"], writes=["Hprev"])
    for c in range(NCH):
        p.op("dve", lambda e, c=c: e.tensor_copy(out=HrR[:, c * 8:(c + 1) * 8],
                                                 in_=Hprev[:, 0:8]),
             reads=["Hprev"], writes=["HrR"])
        p.op("dve", lambda e, c=c: e.tensor_copy(out=HiR[:, c * 8:(c + 1) * 8],
                                                 in_=Hprev[:, 8:16]),
             reads=["Hprev"], writes=["HiR"])
    hp_reads = ([f"hP:{c}" for c in range(NCH)] +
                [f"hPi:{c}" for c in range(NCH)] +
                [f"PP:{c}" for c in range(NCH)] +
                [f"PPi:{c}" for c in range(NCH)])
    httR_o = httR.ap().rearrange("p (cc c) -> p cc c", cc=8).transpose([0, 2, 1])
    httI_o = httI.ap().rearrange("p (cc c) -> p cc c", cc=8).transpose([0, 2, 1])
    fA3 = fA.ap().rearrange("p (c cc) -> p c cc", cc=8)
    hPr3 = hP_r.ap().rearrange("p (c cc) -> p c cc", cc=8)
    hPi3 = hP_i.ap().rearrange("p (c cc) -> p c cc", cc=8)
    p.op("dve", lambda e: e.tensor_mul(out=fA[:, 0:256], in0=PP_r[:, 0:256],
                                       in1=HrR[:, :]),
         reads=hp_reads + ["HrR"], writes=["fA"])
    p.op("dve", lambda e: e.tensor_mul(out=fB[:, 0:256], in0=PP_i[:, 0:256],
                                       in1=HiR[:, :]),
         reads=["HiR"] + hp_reads, writes=["fB"])
    p.op("dve", lambda e: e.tensor_sub(out=fA[:, 0:256], in0=fA[:, 0:256],
                                       in1=fB[:, 0:256]),
         reads=["fA", "fB"], writes=["fA"])
    p.op("dve", lambda e: e.tensor_tensor(out=httR_o, in0=fA3[:, 0:32, :],
                                          in1=hPr3[:, 0:32, :], op=OP.add),
         reads=["fA"] + hp_reads, writes=["httR"])
    p.op("dve", lambda e: e.tensor_mul(out=fA[:, 0:256], in0=PP_r[:, 0:256],
                                       in1=HiR[:, :]),
         reads=hp_reads + ["HiR", "httR"], writes=["fA"])
    p.op("dve", lambda e: e.tensor_mul(out=fB[:, 0:256], in0=PP_i[:, 0:256],
                                       in1=HrR[:, :]),
         reads=["HrR"] + hp_reads, writes=["fB"])
    p.op("dve", lambda e: e.tensor_add(out=fA[:, 0:256], in0=fA[:, 0:256],
                                       in1=fB[:, 0:256]),
         reads=["fA", "fB"], writes=["fA"])
    p.op("dve", lambda e: e.tensor_tensor(out=httI_o, in0=fA3[:, 0:32, :],
                                          in1=hPi3[:, 0:32, :], op=OP.add),
         reads=["fA"] + hp_reads, writes=["httI"])
    for cc in range(8):
        p.op("pe", lambda e, cc=cc: e.transpose(
            psTrB[0:32, 0:128], httR[:, cc * 32:(cc + 1) * 32], ident[:, :]),
            reads=["httR", "ident", "psTrB"], writes=["psTrB"], tag="htr")
        p.op("act", lambda e, cc=cc: e.activation(
            out=hrt2R[0:32, cc * 128:(cc + 1) * 128], in_=psTrB[0:32, 0:128],
            func=AF.Copy),
            reads=["psTrB"], writes=["hrt2R"], tag="htr")
        p.op("pe", lambda e, cc=cc: e.transpose(
            psTrB[0:32, 0:128], httI[:, cc * 32:(cc + 1) * 32], ident[:, :]),
            reads=["httI", "ident", "psTrB"], writes=["psTrB"], tag="htr")
        p.op("act", lambda e, cc=cc: e.activation(
            out=hrt2I[0:32, cc * 128:(cc + 1) * 128], in_=psTrB[0:32, 0:128],
            func=AF.Copy),
            reads=["psTrB"], writes=["hrt2I"], tag="htr")

    s1.close()
    p.fence()
    s2 = contextlib.ExitStack()
    wo16s = sb(s2, "wo16s", [128, 8 * D], BF16)
    b2r16 = sb(s2, "b2r16", [1, D], BF16)
    p.dma("qsp", lambda e: e.dma_start(out=wsl_f[0:1, :], in_=b2r_d),
          reads=["wslf"], writes=["wslf"], tag="b2l")
    p.op("dve", lambda e: e.tensor_copy(out=b2r16[:, :], in_=wsl_f[0:1, :]),
         reads=["wslf"], writes=["b2r16"])
    yattn = sb(s2, "yattn", [128, NSL * D], BF16)
    ylb = sb(s2, "ylb", [128, D], BF16)
    a1b = sb(s2, "a1b", [128, D], BF16)
    a2b = sb(s2, "a2b", [128, D], BF16)
    ypb = sb(s2, "ypb", [128, D], BF16)
    ypfm = sb(s2, "ypfm", [128, D], BF16)
    fCC = sb(s2, "fCC", [128, D], F32)

    for fc in range(8):
        p.dma("qsp", lambda e, fc=fc: e.dma_start(
            out=wo16s[:, fc * D:(fc + 1) * D],
            in_=wo16_d[fc * 128:(fc + 1) * 128, :]),
            reads=[f"slab:wo:{fc}"], writes=["wo16s"], tag="wold")

    _KSKIP = set(_os.environ.get("KSKIP", "").split(","))
    _KP4 = False

    def p4_slice(s):
        p.dma("qsp", lambda e: e.dma_start(out=ylb[:, :], in_=ylocD[s]),
              reads=[f"ylD:{s}", "ylb"], writes=["ylb"], tag=f"rl{s}")
        p.dma("qsp", lambda e: e.dma_start(out=a1b[:, :], in_=a1D[s]),
              reads=[f"a1D:{s}", "a1b"], writes=["a1b"], tag=f"rl{s}")
        p.dma("qsp", lambda e: e.dma_start(out=a2b[:, :], in_=a2D[s]),
              reads=[f"a2D:{s}", "a2b"], writes=["a2b"], tag=f"rl{s}")
        for h in range(2):
            p.op("pe", lambda e, h=h: e.matmul(
                psCum[:, h * 512:(h + 1) * 512], chselA[:, s * 128:(s + 1) * 128],
                hrt2R[:, h * 512:(h + 1) * 512], start=True, stop=True),
                reads=["chselA", "hrt2R"], writes=[f"psC:{h}"], tag=f"hr{s}")
            p.op("pe", lambda e, h=h: e.matmul(
                psMain[:, h * 512:(h + 1) * 512], chselA[:, s * 128:(s + 1) * 128],
                hrt2I[:, h * 512:(h + 1) * 512], start=True, stop=True),
                reads=["chselA", "hrt2I"], writes=[f"psM:{h}"], tag=f"hr{s}")
        p.op("dve", lambda e: e.tensor_mul(out=fA[:, :], in0=a1b[:, :],
                                           in1=psCum[:, :]),
             reads=["a1b", "psC:0", "psC:1"], writes=["fA"])
        p.op("dve", lambda e: e.tensor_mul(out=fB[:, :], in0=a2b[:, :],
                                           in1=psMain[:, 0:1024]),
             reads=["a2b", "psM:0", "psM:1"], writes=["fB"])
        p.op("dve", lambda e: e.tensor_sub(out=fA[:, :], in0=fA[:, :],
                                           in1=fB[:, :]),
             reads=["fA", "fB"], writes=["fA"])
        if "corr" in _KSKIP:
            p.op("dve", lambda e: e.tensor_copy(out=ypb[:, :], in_=ylb[:, :]),
                 reads=["ylb"], writes=["ypb"])
        else:
            p.op("dve", lambda e: e.tensor_add(out=ypb[:, :], in0=ylb[:, :],
                                               in1=fA[:, :]),
                 reads=["ylb", "fA"], writes=["ypb"])
        for dc in range(8):
            p.op("pe", lambda e, dc=dc: e.transpose(
                psTrB[:, dc * 128:(dc + 1) * 128],
                ypb[:, dc * 128:(dc + 1) * 128], ident[:, :]),
                reads=["ypb", "ident", "psTrB"], writes=["psTrB"], tag=f"ytr{s}")
        p.op("act", lambda e: e.activation(out=ypfm[:, :], in_=psTrB[:, :],
                                           func=AF.Copy),
             reads=["psTrB"], writes=["ypfm"], tag=f"ytre{s}")
        for fc in range(8):
            for h in range(2):
                p.op("pe", lambda e, fc=fc, h=h: e.matmul(
                    psAux[:, h * 512:(h + 1) * 512],
                    ypfm[:, fc * 128:(fc + 1) * 128],
                    wo16s[:, fc * D + h * 512: fc * D + (h + 1) * 512],
                    start=(fc == 0), stop=(fc == 7)),
                    reads=["ypfm", "wo16s"], writes=[f"psA:{h}"], tag=f"wo{s}")
        p.op("act", lambda e: e.activation(out=yattn[:, s * D:(s + 1) * D],
                                           in_=psAux[:, :], func=AF.Copy),
             reads=["psA:0", "psA:1"], writes=[f"yattn:{s}"], tag=f"woe{s}")

    for s in range(NSL):
        p4_slice(s)

    xqfm = sb(s2, "xqfm", [128, 8 * 512], BF16)
    hg_g = sb(s2, "hg_g", [128, 32 * 512], BF16)
    hsqb = sb(s2, "hsqb", [128, 512], BF16)
    mxb = sb(s2, "mxb", [128, 512], BF16)
    mnb = sb(s2, "mnb", [128, 512], BF16)
    w1sl = sb(s2, "w1sl", [128, 2 * 1024], BF16)
    w2sl = sb(s2, "w2sl", [128, 2 * 1024], BF16)
    xq_tm = sb(s2, "xq_tm", [128, D], BF16)
    xs2 = sb(s2, "xs2", [128, D], F32)
    mrow = sb(s2, "mrow", [1, 512], F32)
    mrowb = sb(s2, "mrowb", [1, 512], BF16)
    mrowb2 = sb(s2, "mrowb2", [1, 512], BF16)
    r1 = sb(s2, "r1", [1, 512], F32)
    r2 = sb(s2, "r2", [1, 512], F32)
    c2rep = sb(s2, "c2rep", [128, 512], BF16)
    i2rep = sb(s2, "i2rep", [128, 512], BF16)
    q16i = sb(s2, "q16i", [128, 512], I32)
    of1 = sb(s2, "of1", [128, D], F32)
    mxcol = sb(s2, "mxcol", [128, 4], F32)

    _KSKIP = set(_os.environ.get("KSKIP", "").split(","))

    def p5_group(g):
        for j in range(4):
            s = g * 4 + j
            p.dma("qsp", lambda e, s=s: e.dma_start(
                out=xs[:, :], in_=x_d[s * 128:(s + 1) * 128, :]),
                reads=["xs"], writes=["xs"], tag=f"x2{s}")
            p.op("act", lambda e, s=s: e.activation(
                out=xs[:, :], in_=xs[:, :], func=AF.Identity,
                scale=istd_a[:, s:s + 1], bias=nmu_a[:, s:s + 1]),
                reads=["xs", f"istd:{s}", f"nmu:{s}"], writes=["xs"])
            p.op("act", lambda e, s=s: e.activation(
                out=fA[:, :], in_=xs[:, :], func=AF.Identity,
                scale=ct_a[:, s:s + 1]),
                reads=["xs", f"ct:{s}"], writes=["fA"])
            p.op("dve", lambda e: e.tensor_copy(out=fB.ap().bitcast(I32),
                                                in_=fA[:, :]),
                 reads=["fA", "fB"], writes=["fB"])
            p.op("dve", lambda e: e.tensor_scalar(out=fA[:, :],
                                                  in0=fB.ap().bitcast(I32),
                                                  scalar1=127.0, scalar2=None,
                                                  op0=OP.min),
                 reads=["fB"], writes=["fA"])
            p.op("act", lambda e, s=s: e.activation(
                out=xq_tm[:, :], in_=fA[:, :], func=AF.Identity,
                scale=invs_a[:, s:s + 1]),
                reads=["fA", f"invs:{s}"], writes=["xq_tm"])
            for dc in range(8):
                p.op("pe", lambda e, dc=dc: e.transpose(
                    psTrB[:, dc * 128:(dc + 1) * 128],
                    xq_tm[:, dc * 128:(dc + 1) * 128], ident[:, :]),
                    reads=["xq_tm", "ident", "psTrB"], writes=["psTrB"],
                    tag=f"xqt{s}")
            p.op("act", lambda e, j=j: e.activation(
                out=xqfm.ap().rearrange("p (c t) -> p c t", c=8)[:, :,
                                                                j * 128:(j + 1) * 128],
                in_=psTrB.ap().rearrange("p (c t) -> p c t", c=8),
                func=AF.Copy),
                reads=["psTrB"], writes=[f"xqfm:{j}"], tag=f"xqe{s}")
        xq_reads = [f"xqfm:{j}" for j in range(4)]
        for fc in range(32 if "ffn1" not in _KSKIP else 0):
            sl = fc % 2
            p.dma("qsp", lambda e, fc=fc, sl=sl: e.dma_start(
                out=w1sl[:, sl * 1024:(sl + 1) * 1024], in_=w1t16_d[fc]),
                reads=[f"slab:w1:{fc}", f"w1sl:{sl}"], writes=[f"w1sl:{sl}"],
                tag=f"w1l{g}")
            for dc in range(8):
                p.op("pe", lambda e, fc=fc, sl=sl, dc=dc: e.matmul(
                    psMain[:, 0:512],
                    w1sl[:, sl * 1024 + dc * 128: sl * 1024 + (dc + 1) * 128],
                    xqfm[:, dc * 512:(dc + 1) * 512],
                    start=(dc == 0), stop=(dc == 7)),
                    reads=[f"w1sl:{sl}"] + xq_reads, writes=["psM:0"],
                    tag=f"f1m{g}")
            p.op("act", lambda e, fc=fc: e.activation(
                out=hg_g[:, fc * 512:(fc + 1) * 512], in_=psMain[:, 0:512],
                func=AF.Gelu, bias=b1c[:, fc:fc + 1]),
                reads=["psM:0", "b1c"], writes=[f"hg:{fc}"], tag=f"f1e{g}")
            p.op("act", lambda e, fc=fc: e.activation(
                out=hsqb[:, :], in_=hg_g[:, fc * 512:(fc + 1) * 512],
                func=AF.Square),
                reads=[f"hg:{fc}", "hsqb"], writes=["hsqb"])
            p.op("pe", lambda e, fc=fc: e.matmul(
                psMain[:, 512:1024], onesr[:, :], hsqb[:, :],
                start=(fc == 0), stop=(fc == 31)),
                reads=["hsqb", "onesr"], writes=["psM:1"], tag=f"sq{g}")
            if fc == 0:
                p.op("dve", lambda e: e.tensor_copy(out=mxb[:, :],
                                                    in_=hg_g[:, 0:512]),
                     reads=["hg:0", "mxb"], writes=["mxb"])
                p.op("dve", lambda e: e.tensor_copy(out=mnb[:, :],
                                                    in_=hg_g[:, 0:512]),
                     reads=["hg:0", "mnb"], writes=["mnb"])
            else:
                p.op("dve", lambda e, fc=fc: e.tensor_tensor(
                    out=mxb[:, :], in0=mxb[:, :],
                    in1=hg_g[:, fc * 512:(fc + 1) * 512], op=OP.max),
                    reads=["mxb", f"hg:{fc}"], writes=["mxb"])
                p.op("dve", lambda e, fc=fc: e.tensor_tensor(
                    out=mnb[:, :], in0=mnb[:, :],
                    in1=hg_g[:, fc * 512:(fc + 1) * 512], op=OP.min),
                    reads=["mnb", f"hg:{fc}"], writes=["mnb"])
        if "stats" in _KSKIP:
            return
        p.op("dve", lambda e: e.tensor_scalar(out=mnb[:, :], in0=mnb[:, :],
                                              scalar1=-1.0, scalar2=None,
                                              op0=OP.mult),
             reads=["mnb"], writes=["mnb"])
        p.op("dve", lambda e: e.tensor_max(out=mxb[:, :], in0=mxb[:, :],
                                           in1=mnb[:, :]),
             reads=["mxb", "mnb"], writes=["mxb"])
        for t_ in range(4):
            p.op("pe", lambda e, t_=t_: e.transpose(
                psTrB[:, 0:128], mxb[:, t_ * 128:(t_ + 1) * 128], ident[:, :]),
                reads=["mxb", "ident", "psTrB"], writes=["psTrB"], tag=f"mxt{g}")
            p.op("dve", lambda e, t_=t_: e.tensor_reduce(
                out=mxcol[:, t_:t_ + 1], in_=psTrB[:, 0:128], axis=AX,
                op=OP.max),
                reads=["psTrB"], writes=["mxcol"])
        for t_ in range(4):
            p.dma("qpool", lambda e, t_=t_: e.dma_start(
                out=mrow[0:1, t_ * 128:(t_ + 1) * 128],
                in_=mxcol[:, t_:t_ + 1]),
                reads=["mxcol"], writes=["mrow"], tag=f"mx{g}")
        p.op("act", lambda e: e.activation(out=r1[:, :], in_=psMain[0:1, 512:1024],
                                           func=AF.Sqrt, scale=1.0 / FF, bias=cEPS[0:1, :]),
             reads=["psM:1", "cEPS"], writes=["r1"])
        p.op("dve", lambda e: e.reciprocal(out=r1[:, :], in_=r1[:, :]),
             reads=["r1"], writes=["r1"])
        p.op("dve", lambda e: e.tensor_mul(out=r2[:, :], in0=mrow[:, :],
                                           in1=r1[:, :]),
             reads=["mrow", "r1"], writes=["r2"])
        p.op("dve", lambda e: e.tensor_scalar(out=r2[:, :], in0=r2[:, :],
                                              scalar1=1e-5, scalar2=None,
                                              op0=OP.max),
             reads=["r2"], writes=["r2"])
        p.op("dve", lambda e: e.tensor_scalar(out=mrowb2[:, :], in0=r2[:, :],
                                              scalar1=1.0 / 127.0, scalar2=None,
                                              op0=OP.mult),
             reads=["r2"], writes=["mrowb2"])
        p.op("dve", lambda e: e.reciprocal(out=r2[:, :], in_=r2[:, :]),
             reads=["r2"], writes=["r2"])
        p.op("dve", lambda e: e.tensor_mul(out=r2[:, :], in0=r2[:, :],
                                           in1=r1[:, :]),
             reads=["r2", "r1"], writes=["r2"])
        p.op("dve", lambda e: e.tensor_scalar(out=mrowb[:, :], in0=r2[:, :],
                                              scalar1=127.0, scalar2=None,
                                              op0=OP.mult),
             reads=["r2"], writes=["mrowb"])
        p.op("pe", lambda e: e.matmul(psAux[:, 0:512], onesr[0:1, :],
                                      mrowb[:, :], start=True, stop=True),
             reads=["onesr", "mrowb"], writes=["psA:0"], tag=f"rep{g}")
        p.op("pe", lambda e: e.matmul(psAux[:, 512:1024], onesr[0:1, :],
                                      mrowb2[:, :], start=True, stop=True),
             reads=["onesr", "mrowb2"], writes=["psA:1"], tag=f"rep{g}")
        p.op("act", lambda e: e.activation(out=c2rep[:, :], in_=psAux[:, 0:512],
                                           func=AF.Copy),
             reads=["psA:0"], writes=["c2rep"])
        p.op("act", lambda e: e.activation(out=i2rep[:, :], in_=psAux[:, 512:1024],
                                           func=AF.Copy),
             reads=["psA:1"], writes=["i2rep"])
        for fc in range(32 if "quant" not in _KSKIP else 0):
            p.op("dve", lambda e, fc=fc: e.tensor_tensor(
                out=q16i[:, :], in0=hg_g[:, fc * 512:(fc + 1) * 512],
                in1=c2rep[:, :], op=OP.mult),
                reads=[f"hg:{fc}", "c2rep"], writes=["q16i"])
            p.op("dve", lambda e, fc=fc: e.scalar_tensor_tensor(
                out=hg_g[:, fc * 512:(fc + 1) * 512], in0=q16i[:, :],
                scalar=127.0, in1=i2rep[:, :], op0=OP.min, op1=OP.mult),
                reads=["q16i", "i2rep"], writes=[f"hg:{fc}"])
        for half in range(2 if "ffn2" not in _KSKIP else 0):
            psO = [psCum, psAux]
            pks = ["psC", "psA"]
            for fc in range(32):
                sl = fc % 2
                p.dma("qsp", lambda e, fc=fc, sl=sl: e.dma_start(
                    out=w2sl[:, sl * 1024:(sl + 1) * 1024],
                    in_=w2t16_d[fc * 128:(fc + 1) * 128, :]),
                    reads=[f"slab:w2:{fc}", f"w2sl:{sl}"], writes=[f"w2sl:{sl}"],
                    tag=f"w2l{g}{half}")
                for j2 in range(2):
                    j = half * 2 + j2
                    for h in range(2):
                        p.op("pe", lambda e, fc=fc, sl=sl, j=j, j2=j2, h=h:
                             e.matmul(
                                 psO[j2][:, h * 512:(h + 1) * 512],
                                 hg_g[:, fc * 512 + j * 128:
                                      fc * 512 + (j + 1) * 128],
                                 w2sl[:, sl * 1024 + h * 512:
                                      sl * 1024 + (h + 1) * 512],
                                 start=(fc == 0), stop=False),
                             reads=[f"w2sl:{sl}", f"hg:{fc}"],
                             writes=[pks[j2] + f":{h}"], tag=f"f2m{g}")
            for j2 in range(2):
                j = half * 2 + j2
                s = g * 4 + j
                pk = pks[j2]
                for h in range(2):
                    p.op("pe", lambda e, j2=j2, h=h: e.matmul(
                        psO[j2][:, h * 512:(h + 1) * 512], onesr[0:1, :],
                        b2r16[:, h * 512:(h + 1) * 512], start=False, stop=True),
                        reads=["onesr", "b2r16"], writes=[pk + f":{h}"],
                        tag=f"b2{g}")
                p.dma("qsp", lambda e, s=s: e.dma_start(
                    out=xs2[:, :], in_=x_d[s * 128:(s + 1) * 128, :]),
                    reads=["xs2"], writes=["xs2"], tag=f"x3{s}")
                DBG = _os.environ.get("KDBG", "")
                if DBG == "attn":
                    p.op("dve", lambda e, s=s: e.tensor_copy(
                        out=of1[:, :], in_=yattn[:, s * D:(s + 1) * D]),
                        reads=[f"yattn:{s}", "of1", pk + ":0", pk + ":1", "xs2"],
                        writes=["of1"])
                elif DBG == "ffn":
                    p.op("dve", lambda e, j2=j2: e.tensor_add(
                        out=of1[:, :], in0=psO[j2][:, :], in1=xs2[:, :]),
                        reads=[pk + ":0", pk + ":1", "xs2", "of1"], writes=["of1"])
                else:
                    p.op("dve", lambda e, j2=j2: e.tensor_add(
                        out=of1[:, :], in0=psO[j2][:, :], in1=xs2[:, :]),
                        reads=[pk + ":0", pk + ":1", "xs2", "of1"], writes=["of1"])
                    p.op("dve", lambda e, s=s: e.tensor_add(
                        out=of1[:, :], in0=of1[:, :], in1=yattn[:, s * D:(s + 1) * D]),
                        reads=["of1", f"yattn:{s}"], writes=["of1"])
                p.dma("qpool", lambda e, s=s: e.dma_start(
                    out=out_d[s * 128:(s + 1) * 128, :], in_=of1[:, :]),
                    reads=["of1"], writes=[f"out:{s}"], tag=f"o{s}")

    for g in range(4):
        p5_group(g)

    p.barrier_read("sp", [f"out:{s}" for s in range(NSL)])
    rem = nc.sbuf_bytes_remaining
    import sys
    print(f"[kernel-build] sbuf remaining {rem}B/partition, "
          f"{len(p.ops)} ops", file=sys.stderr)
    p.emit()
    s2.close()
    gstack.close()
    return nc, rem


_NC_CACHE = [None]
kernel_last_res = [None]


def kernel(L0_attn_x, L0_attn_gamma, L0_attn_wq, L0_attn_wk, L0_attn_wv,
           L0_attn_wa, L0_attn_wg, L0_attn_wo, L0_ffn_w1, L0_ffn_b1,
           L0_ffn_w2, L0_ffn_b2):
    x = np.ascontiguousarray(np.asarray(L0_attn_x, np.float32))
    gamma = np.asarray(L0_attn_gamma, np.float32)
    wq = np.ascontiguousarray(np.asarray(L0_attn_wq, np.float32))
    wk = np.ascontiguousarray(np.asarray(L0_attn_wk, np.float32))
    wv = np.ascontiguousarray(np.asarray(L0_attn_wv, np.float32))
    wa = np.ascontiguousarray(np.asarray(L0_attn_wa, np.float32))
    wg = np.ascontiguousarray(np.asarray(L0_attn_wg, np.float32))
    wo = np.ascontiguousarray(np.asarray(L0_attn_wo, np.float32))
    w1 = np.asarray(L0_ffn_w1, np.float32)
    b1 = np.asarray(L0_ffn_b1, np.float32)
    w2 = np.asarray(L0_ffn_w2, np.float32)
    b2 = np.asarray(L0_ffn_b2, np.float32)

    w1t = np.ascontiguousarray(w1.T.reshape(D, 32, 128).transpose(1, 0, 2))
    w2t = np.ascontiguousarray(w2.T)
    b1c = np.ascontiguousarray(b1.reshape(32, 128).T)
    b2r = np.ascontiguousarray(b2.reshape(1, D))
    gr = np.ascontiguousarray(np.broadcast_to(gamma, (128, D)))

    BF = ml_dtypes.bfloat16
    tri = np.zeros((128, 128), np.float32)
    for b_ in range(2):
        for s_ in range(64):
            tri[b_ * 64 + s_, b_ * 64 + s_:(b_ + 1) * 64] = 1.0
    consts = {
        "tri_m1": (-tri).astype(BF),
        "tri_p4": (4.0 * tri).astype(BF),
        "tri_p1": tri.astype(BF),
        "ident": np.eye(128, dtype=np.float32).astype(BF),
        "onesr": np.ones((128, 128), np.float32).astype(BF),
    }
    cind = np.zeros((128, 2), np.float32)
    cind[0:64, 0] = 1.0
    cind[64:128, 1] = 1.0
    consts["cind"] = cind.astype(BF)
    chsel = np.zeros((32, NSL * 128), np.float32)
    for s_ in range(NSL):
        chsel[2 * s_, s_ * 128: s_ * 128 + 64] = 1.0
        chsel[2 * s_ + 1, s_ * 128 + 64: (s_ + 1) * 128] = 1.0
    consts["chsel"] = chsel.astype(BF)

    if _NC_CACHE[0] is None:
        _NC_CACHE[0] = build()[0]
    nc = _NC_CACHE[0]

    in_maps = []
    for c in range(8):
        b_, h_ = c // 2, c % 2
        sel = np.zeros(8, np.float32)
        if h_ == 1:
            sel[c - 1] = 1.0
        selrep = np.ascontiguousarray(
            np.broadcast_to(np.repeat(sel, 16)[None, :], (128, 128)))
        m = {"x": np.ascontiguousarray(x[b_, h_ * T:(h_ + 1) * T]),
             "wq": wq, "wk": wk, "wv": wv, "wg": wg, "wa": wa, "wo": wo,
             "w1t": w1t, "w2t": w2t, "b1c": b1c, "b2r": b2r, "gr": gr,
             "selrep": selrep}
        m.update(consts)
        in_maps.append(m)
    res = run_bass_kernel_spmd(nc, in_maps, core_ids=list(range(8)))
    kernel_last_res[0] = res
    out = np.empty((4, 4096, D), np.float32)
    for c in range(8):
        b_, h_ = c // 2, c % 2
        out[b_, h_ * T:(h_ + 1) * T] = res.results[c]["out"]
    return out
